# version 03
# baseline (speedup 1.0000x reference)
import numpy as np
import contextlib
import concourse.bass as bass
import concourse.mybir as mybir
from concourse.bass_utils import run_bass_kernel_spmd

F32 = mybir.dt.float32
BF16 = mybir.dt.bfloat16
ALU = mybir.AluOpType
AF = mybir.ActivationFunctionType
AX = mybir.AxisListType

NC = 8
D = 1024
DC = 8
I3 = 3072
IC = 24
TC = 2048
TG = 512
NTG = TC // TG
T1 = TC + 1
CM = 12582912.0
QMAX = 32767.0
INV_QMAX = float(np.float32(1.0) / np.float32(32767.0))
INV7 = float(np.float32(1.0) / np.float32(7.0))
INV_D = float(np.float32(1.0) / np.float32(1024.0))
LNEPS = 1e-5
EPS12 = 1e-12
K65536 = 65536.0
INV65536 = float(np.float32(1.0) / np.float32(65536.0))

ENGS = ("sync", "vector", "scalar", "tensor", "gpsimd")
DBG = False
STAGE = 99
SUB = 99
DQN = 99
DQDN = True
DQSLOT2 = True
DQTRUNC = 99


class Tok:
    __slots__ = ("w", "r")
    def __init__(self):
        self.w = None
        self.r = []


class Plan:
    def __init__(self):
        self.cnt = {}
        self.ops = {e: [] for e in ENGS}
        self.marks = {e: {} for e in ENGS}
        self.chans = set()
        for e in ENGS:
            self.cnt[e] = 0

    def emit(self, eng, fn, reads, writes, dma, chan=None, sem_override=None):
        if dma:
            sem = chan
            self.chans.add(chan)
            if sem not in self.cnt:
                self.cnt[sem] = 0
        elif sem_override is not None:
            sem = sem_override
            if sem not in self.cnt:
                self.cnt[sem] = 0
        else:
            sem = eng
        step = 16 if dma else 1
        waits = []
        marks = self.marks[eng]
        def need(sig):
            if sig is None:
                return
            s, v = sig
            if s in self.chans:
                v = self.cnt[s]
            if marks.get(s, 0) < v:
                waits.append((s, v))
                marks[s] = v
        for t in reads:
            need(t.w)
        for t in writes:
            need(t.w)
            for sig in t.r:
                need(sig)
        self.cnt[sem] += step
        sig_new = (sem, self.cnt[sem])
        for t in reads:
            t.r.append(sig_new)
            if len(t.r) > 12:
                t.r = t.r[-12:]
        for t in writes:
            t.w = sig_new
            t.r = []
        self.ops[eng].append((waits, fn, sem, step))

    def barrier_wait(self, eng, reads):
        waits = []
        marks = self.marks[eng]
        for t in reads:
            if t.w is None:
                continue
            s, v = t.w
            if s in self.chans:
                v = self.cnt[s]
            if marks.get(s, 0) < v:
                waits.append((s, v))
                marks[s] = v
        if waits:
            self.ops[eng].append((waits, None, eng, 1))

    def drain_all(self, eng):
        waits = []
        marks = self.marks[eng]
        for s in sorted(self.chans):
            v = self.cnt[s]
            if v > 0 and marks.get(s, 0) < v:
                waits.append((s, v))
                marks[s] = v
        if "ccol" in self.cnt and marks.get("ccol", 0) < self.cnt["ccol"]:
            waits.append(("ccol", self.cnt["ccol"]))
            marks["ccol"] = self.cnt["ccol"]
        if waits:
            self.ops[eng].append((waits, None, eng, 1))

    def realize(self, block, sems):
        def mk(e):
            def run(engh):
                for waits, fn, sem, step in self.ops[e]:
                    for s, v in waits:
                        engh.wait_ge(sems[s], v)
                    if fn is not None:
                        inst = fn(engh)
                        inst.then_inc(sems[sem], step)
            return run
        for e in ENGS:
            getattr(block, e)(mk(e))


def build_program():
    nc = bass.Bass("TRN2", target_bir_lowering=False)
    dtn = nc.dram_tensor
    xt = dtn("xt", [D, TC], F32, kind="ExternalInput").ap()
    wk = dtn("wk", [D, D], F32, kind="ExternalInput").ap()
    wv = dtn("wv", [D, D], F32, kind="ExternalInput").ap()
    wr = dtn("wr", [D, D], F32, kind="ExternalInput").ap()
    wo = dtn("wo", [D, D], F32, kind="ExternalInput").ap()
    wup = dtn("wup", [I3, D], F32, kind="ExternalInput").ap()
    wg = dtn("wg", [I3, D], F32, kind="ExternalInput").ap()
    wdn = dtn("wdn", [D, I3], F32, kind="ExternalInput").ap()
    tdec = dtn("tdec", [1, D], F32, kind="ExternalInput").ap()
    tfirst = dtn("tfirst", [1, D], F32, kind="ExternalInput").ap()
    tmk = dtn("tmk", [1, D], F32, kind="ExternalInput").ap()
    tmv = dtn("tmv", [1, D], F32, kind="ExternalInput").ap()
    tmr = dtn("tmr", [1, D], F32, kind="ExternalInput").ap()
    idn = dtn("idn", [128, 128], F32, kind="ExternalInput").ap()
    selw8 = dtn("selw8", [8, 128], F32, kind="ExternalInput").ap()
    selr8 = dtn("selr8", [128, 8], F32, kind="ExternalInput").ap()
    selw16 = dtn("selw16", [16, 128], F32, kind="ExternalInput").ap()
    selr16 = dtn("selr16", [128, 16], F32, kind="ExternalInput").ap()
    out = dtn("out", [D, TC], F32, kind="ExternalOutput").ap()
    if DBG:
        dbg1 = dtn("dbg1", [D, TC], F32, kind="ExternalOutput").ap()
        dbg2 = dtn("dbg2", [D, TC], F32, kind="ExternalOutput").ap()
        dbg3 = dtn("dbg3", [D, TC], F32, kind="ExternalOutput").ap()
        dbg4 = dtn("dbg4", [128, 512], F32, kind="ExternalOutput").ap()
        dbg5 = dtn("dbg5", [D, TC], F32, kind="ExternalOutput").ap()
    upbuf = dtn("upbuf", [I3, TC], F32).ap()
    gbuf = dtn("gbuf", [I3, TC], F32).ap()
    t2buf = dtn("t2buf", [I3, TC], F32).ap()
    wupT = dtn("wupT", [IC, 8, 128, 128], BF16).ap()
    wgT = dtn("wgT", [IC, 8, 128, 128], BF16).ap()
    wdnT = dtn("wdnT", [DC, IC, 128, 128], BF16).ap()
    wkb = dtn("wkb", [D, D], BF16).ap()
    wvb = dtn("wvb", [D, D], BF16).ap()
    wrb = dtn("wrb", [D, D], BF16).ap()
    wob = dtn("wob", [D, D], BF16).ap()
    xspill = dtn("xspill", [D, TC], F32).ap()
    mixbuf = dtn("mixbuf", [3, D, TC], BF16).ap()
    scanbuf = dtn("scanbuf", [2 * D, T1], F32).ap()
    bounce = dtn("bounce", [128, 1], F32).ap()
    cc_in = dtn("cc_in", [128, 128], F32).ap()
    cc_out = dtn("cc_out", [128, 128], F32).ap()

    p = Plan()
    stack = contextlib.ExitStack()
    sems = {}
    for e in ENGS:
        sems[e] = stack.enter_context(nc.semaphore("s_" + e))
    sb = lambda name, shp, dtp: stack.enter_context(nc.sbuf_tensor(name, shp, dtp))
    psal = lambda name, shp, dtp: stack.enter_context(nc.psum_tensor(name, shp, dtp))

    XTB = sb("XTB", [128, DC * T1], F32)
    XT = XTB[:, 0:DC * TC]
    B2 = XTB[:, :]
    A2 = sb("A2", [128, DC * T1], F32)
    CB = sb("CB", [128, DC * TC], BF16)
    SC = CB.ap().bitcast(F32)[:, 0:3 * T1]
    MT = sb("MT", [128, 6 * TG], F32)
    US = sb("US", [128, 4 * TG], F32)
    SQT = sb("SQT", [128, TG], F32)
    WB = sb("WB", [128, 2 * D], F32)
    LW = sb("LW", [128, 2 * 1024], BF16)
    TS2 = sb("TS2", [128, 2 * 128], BF16)
    DQS = sb("DQS", [128, 256], F32)
    MAXC = sb("MAXC", [128, 240], F32)
    COLS = sb("COLS", [128, 16], F32)
    DCV = sb("DCV", [128, 64], F32)
    ONEC = sb("ONEC", [128, 2], F32)
    ONER = sb("ONER", [1, 128], F32)
    IDF = sb("IDF", [128, 128], F32)
    IDB = sb("IDB", [128, 128], BF16)
    SEL8 = sb("SEL8", [8, 128], F32)
    SELR8 = sb("SELR8", [128, 8], F32)
    SEL16 = sb("SEL16", [16, 128], F32)
    SELR16 = sb("SELR16", [128, 16], F32)
    ROWS = sb("ROWS", [1, 4 * TG], F32)
    SROW = sb("SROW", [1, 64], F32)
    VROW = CB.ap().bitcast(F32)[0:1, 0:5 * D]
    CCT = SQT[:, 0:128]
    XR2 = SQT[0:8, 128:256]
    XRS = SQT[0:8, 256:384]
    H16 = SQT[0:16, 384:512]

    PMM = [psal("PMM%d" % i, [128, TG], F32) for i in range(4)]
    PBC = [psal("PBC%d" % i, [128, TG], F32) for i in range(2)]
    PTR = psal("PTR", [128, 1024], BF16)
    PSG = psal("PSG", [128, 128], F32)

    MX_A = 0
    MX_B = 96
    MX_F = 192
    MX_NT = 193
    MX_NR = 195
    MX_P = 197
    MX_BX = 200
    MX_SF = 208
    MX_B0 = 224

    tks = {}
    def T(name):
        if name not in tks:
            tks[name] = Tok()
        return tks[name]

    def E(eng, fn, r=(), w=()):
        p.emit(eng, fn, [T(x) for x in r], [T(x) for x in w], False)

    def ED(eng, fn, r=(), w=(), chan="c0"):
        p.emit(eng, fn, [T(x) for x in r], [T(x) for x in w], True, chan=chan)

    for dc in range(DC):
        ED("sync", (lambda dc: lambda g: g.dma_start(out=XT[:, dc * TC:(dc + 1) * TC], in_=xt[dc * 128:(dc + 1) * 128, :]))(dc),
           w=("XT%d" % dc,), chan="cin")
    ED("sync", lambda g: g.dma_start(out=IDF[:], in_=idn), w=("IDF",), chan="cin")
    ED("sync", lambda g: g.dma_start(out=SEL8[:], in_=selw8), w=("SEL8",), chan="cin")
    ED("sync", lambda g: g.dma_start(out=SELR8[:], in_=selr8), w=("SELR8",), chan="cin")
    ED("sync", lambda g: g.dma_start(out=SEL16[:], in_=selw16), w=("SEL16",), chan="cin")
    ED("sync", lambda g: g.dma_start(out=SELR16[:], in_=selr16), w=("SELR16",), chan="cin")
    VOFF = {"tdec": 0, "tfirst": 1, "tmk": 2, "tmv": 3, "tmr": 4}
    for nm, src in (("tdec", tdec), ("tfirst", tfirst), ("tmk", tmk), ("tmv", tmv), ("tmr", tmr)):
        o = VOFF[nm] * D
        ED("sync", (lambda src, o: lambda g: g.dma_start(out=VROW[0:1, o:o + D], in_=src))(src, o), w=("CB0", "CB1", "CB2", "CB3", "CB4"), chan="cin")
    E("vector", lambda v: v.memset(ONEC[:], 1.0), w=("ONEC",))
    E("vector", lambda v: v.memset(ONER[:], 1.0), w=("ONER",))
    E("vector", lambda v: v.tensor_copy(out=IDB[:], in_=IDF[:]), r=("IDF",), w=("IDB",))
    E("vector", lambda v: v.memset(CCT[:], 0.0), w=("SQT",))
    ED("gpsimd", lambda g: g.dma_start(out=cc_in, in_=CCT[:]), r=("SQT",), w=("CCIN",), chan="ccg")

    def col_from_vrow(dst, vname, dc):
        o = VOFF[vname] * D + dc * 128
        E("tensor", (lambda dst, o: lambda t: t.matmul(PSG[:, 0:1], VROW[0:1, o:o + 128], ONEC[0:1, 0:1], start=True, stop=True))(dst, o),
          r=("CB0", "CB1", "CB2", "CB3", "CB4", "ONEC"), w=("PSG",))
        E("vector", (lambda dst: lambda v: v.tensor_copy(out=DCV[:, dst:dst + 1], in_=PSG[:, 0:1]))(dst),
          r=("PSG",), w=("DCV",))
    for dc in range(DC):
        col_from_vrow(16 + dc, "tmk", dc)
        col_from_vrow(24 + dc, "tmv", dc)
        col_from_vrow(32 + dc, "tmr", dc)
        col_from_vrow(40 + dc, "tdec", dc)
        col_from_vrow(48 + dc, "tfirst", dc)
    for dc in range(DC):
        E("scalar", (lambda dc: lambda s: s.activation(out=DCV[:, 0 + dc:1 + dc], in_=DCV[:, 48 + dc:49 + dc], func=AF.Exp))(dc),
          r=("DCV",), w=("DCV",))
        E("scalar", (lambda dc: lambda s: s.activation(out=DCV[:, 56 + dc:57 + dc], in_=DCV[:, 40 + dc:41 + dc], func=AF.Exp))(dc),
          r=("DCV",), w=("DCV",))
        E("scalar", (lambda dc: lambda s: s.activation(out=DCV[:, 8 + dc:9 + dc], in_=DCV[:, 56 + dc:57 + dc], func=AF.Exp, scale=-1.0))(dc),
          r=("DCV",), w=("DCV",))

    for wnm, wsrc, wdst in (("wk", wk, wkb), ("wv", wv, wvb), ("wr", wr, wrb), ("wo", wo, wob)):
        for ch in range(8):
            sl = ch % 2
            ED("sync", (lambda wsrc, ch, sl: lambda g: g.dma_start(out=WB[:, sl * D:(sl + 1) * D], in_=wsrc[ch * 128:(ch + 1) * 128, :]))(wsrc, ch, sl),
               r=("WB%d" % sl,), w=("WB%d" % sl,), chan="cwb%d" % sl)
            E("vector", (lambda sl: lambda v: v.tensor_copy(out=LW[:, sl * 1024:(sl + 1) * 1024], in_=WB[:, sl * D:(sl + 1) * D]))(sl),
              r=("WB%d" % sl, "LW%d" % sl), w=("LW%d" % sl,))
            ED("gpsimd", (lambda wdst, ch, sl: lambda g: g.dma_start(out=wdst[ch * 128:(ch + 1) * 128, :], in_=LW[:, sl * 1024:(sl + 1) * 1024]))(wdst, ch, sl),
               r=("LW%d" % sl,), w=("%s_stash" % wnm,), chan="cst")

    def ln_stats_and_apply(tag, maxtrack):
        for tg in range(NTG):
            for dc in range(DC):
                sl = XT[:, dc * TC + tg * TG: dc * TC + tg * TG + TG]
                E("scalar", (lambda sl: lambda s: s.activation(out=SQT[:], in_=sl, func=AF.Square))(sl),
                  r=("XT%d" % dc, "SQT"), w=("SQT",))
                E("tensor", (lambda sl, dc: lambda t: t.matmul(PBC[0][0:1, 0:TG], ONEC[:, 0:1], sl, start=(dc == 0), stop=(dc == DC - 1)))(sl, dc),
                  r=("XT%d" % dc, "ONEC", "PBC0"), w=("PBC0",))
                E("tensor", (lambda dc: lambda t: t.matmul(PBC[1][0:1, 0:TG], ONEC[:, 0:1], SQT[:], start=(dc == 0), stop=(dc == DC - 1)))(dc),
                  r=("SQT", "ONEC", "PBC1"), w=("PBC1",))
            mrow = ROWS[0:1, 0:TG]
            arow = ROWS[0:1, TG:2 * TG]
            vr1 = ROWS[0:1, 2 * TG:3 * TG]
            vr2 = ROWS[0:1, 3 * TG:4 * TG]
            E("vector", (lambda mrow: lambda v: v.tensor_scalar(out=mrow, in0=PBC[0][0:1, 0:TG], scalar1=INV_D, scalar2=None, op0=ALU.mult))(mrow),
              r=("PBC0",), w=("ROWS",))
            E("vector", (lambda vr1: lambda v: v.tensor_scalar(out=vr1, in0=PBC[1][0:1, 0:TG], scalar1=INV_D, scalar2=LNEPS, op0=ALU.mult, op1=ALU.add))(vr1),
              r=("PBC1",), w=("ROWS",))
            E("vector", (lambda mrow, vr2: lambda v: v.tensor_tensor(out=vr2, in0=mrow, in1=mrow, op=ALU.mult))(mrow, vr2),
              r=("ROWS",), w=("ROWS",))
            E("vector", (lambda vr1, vr2: lambda v: v.tensor_tensor(out=vr1, in0=vr1, in1=vr2, op=ALU.subtract))(vr1, vr2),
              r=("ROWS", "ROWS"), w=("ROWS",))
            E("scalar", (lambda vr1, vr2: lambda s: s.activation(out=vr2, in_=vr1, func=AF.Ln))(vr1, vr2),
              r=("ROWS",), w=("ROWS",))
            E("scalar", (lambda arow, vr2: lambda s: s.activation(out=arow, in_=vr2, func=AF.Exp, scale=-0.5))(arow, vr2),
              r=("ROWS",), w=("ROWS",))
            E("vector", (lambda mrow, arow: lambda v: v.scalar_tensor_tensor(out=mrow, in0=mrow, scalar=-1.0, in1=arow, op0=ALU.mult, op1=ALU.mult))(mrow, arow),
              r=("ROWS", "ROWS"), w=("ROWS",))
            E("tensor", (lambda arow: lambda t: t.matmul(PBC[0][:, 0:TG], ONER[0:1, :], arow, start=True, stop=True))(arow),
              r=("ROWS", "ONER", "PBC0"), w=("PBC0",))
            E("tensor", (lambda mrow: lambda t: t.matmul(PBC[1][:, 0:TG], ONER[0:1, :], mrow, start=True, stop=True))(mrow),
              r=("ROWS", "ONER", "PBC1"), w=("PBC1",))
            for dc in range(DC):
                srcsl = XT[:, dc * TC + tg * TG: dc * TC + tg * TG + TG]
                dstsl = A2[:, 1 + dc * T1 + tg * TG: 1 + dc * T1 + tg * TG + TG]
                E("vector", (lambda srcsl, dstsl: lambda v: v.tensor_tensor(out=dstsl, in0=srcsl, in1=PBC[0][:, 0:TG], op=ALU.mult))(srcsl, dstsl),
                  r=("XT%d" % dc, "PBC0", "A2_%d" % dc), w=("A2_%d" % dc,))
                E("vector", (lambda dstsl: lambda v: v.tensor_tensor(out=dstsl, in0=dstsl, in1=PBC[1][:, 0:TG], op=ALU.add))(dstsl),
                  r=("A2_%d" % dc, "PBC1"), w=("A2_%d" % dc,))
                if maxtrack:
                    cix = MX_A + tg * DC + dc
                    E("vector", (lambda dstsl, cix: lambda v: v.tensor_reduce(out=MAXC[:, cix:cix + 1], in_=dstsl, axis=AX.X, op=ALU.max, apply_absolute_value=True))(dstsl, cix),
                      r=("A2_%d" % dc,), w=("MXA",))

    def part_reduce_to_scalar(coltok, srccol, outslot, tagout):
        ED("gpsimd", (lambda srccol: lambda g: g.dma_start(out=bounce, in_=MAXC[:, srccol:srccol + 1]))(srccol),
           r=(coltok,), w=("BOUNCE",), chan="cbo")
        ED("sync", lambda g: g.dma_start(out=ROWS[0:1, 0:128], in_=bounce.rearrange("a b -> b a")),
           r=("BOUNCE", "ROWS"), w=("ROWS",), chan="cbi")
        E("vector", (lambda outslot: lambda v: v.tensor_reduce(out=SROW[0:1, outslot:outslot + 1], in_=ROWS[0:1, 0:128], axis=AX.X, op=ALU.max))(outslot),
          r=("ROWS", "SROW"), w=("SROW", tagout))

    def collective_max(slot0, nsl, rtags, tagout):
        ED("gpsimd", (lambda slot0, nsl: lambda g: g.dma_start(out=cc_in[0:1, 0:nsl], in_=SROW[0:1, slot0:slot0 + nsl]))(slot0, nsl),
           r=rtags + ("SROW",), w=("CCIN",), chan="ccg")
        p.emit("gpsimd", lambda g: g.collective_compute("AllReduce", ALU.max, replica_groups=[list(range(NC))],
                                                        ins=[cc_in.opt()], outs=[cc_out.opt()]),
               [T("CCIN"), T("CCOUT")], [T("CCOUT")], False, sem_override="ccol")
        ED("sync", (lambda slot0, nsl: lambda g: g.dma_start(out=SROW[0:1, slot0:slot0 + nsl], in_=cc_out[0:1, 0:nsl]))(slot0, nsl),
           r=("CCOUT", "SROW"), w=("SROW", tagout), chan="cci")

    def collective_add_tile(rtag, tagout):
        ED("gpsimd", lambda g: g.dma_start(out=cc_in, in_=CCT[:]),
           r=(rtag, "SQT"), w=("CCIN",), chan="ccg")
        p.emit("gpsimd", lambda g: g.collective_compute("AllReduce", ALU.add, replica_groups=[list(range(NC))],
                                                        ins=[cc_in.opt()], outs=[cc_out.opt()]),
               [T("CCIN"), T("CCOUT")], [T("CCOUT")], False, sem_override="ccol")
        ED("sync", lambda g: g.dma_start(out=CCT[:], in_=cc_out),
           r=("CCOUT", "SQT"), w=("SQT", tagout), chan="cci")

    def bcast_slots(slot0, n, colbase, rtags):
        E("tensor", (lambda slot0, n: lambda t: t.matmul(PSG[:, 0:n], ONER[0:1, :], SROW[0:1, slot0:slot0 + n], start=True, stop=True))(slot0, n),
          r=rtags + ("SROW", "ONER", "PSG"), w=("PSG",))
        E("vector", (lambda colbase, n: lambda v: v.tensor_copy(out=COLS[:, colbase:colbase + n], in_=PSG[:, 0:n]))(colbase, n),
          r=("PSG", "COLS"), w=("COLS",))

    def sc_op(fn, r, w):
        E("vector", fn, r=r + ("SROW",), w=w + ("SROW",))

    def dq_chunk(oc, sl, wdst, stash_tag_pfx):
        lim = DQTRUNC if oc >= 1 else 99
        wsl = WB[:, sl * D:(sl + 1) * D]
        bm = DQS[:, sl * 128:sl * 128 + 32]
        sbv = DQS[:, sl * 128 + 64:sl * 128 + 96]
        if lim < 1:
            return
        E("vector", (lambda wsl, bm: lambda g: g.tensor_reduce(out=bm, in_=wsl.rearrange("p (b c) -> p b c", c=32), axis=AX.X, op=ALU.max, apply_absolute_value=True))(wsl, bm),
          r=("WB%d" % sl, "DQS%d" % sl), w=("DQS%d" % sl,))
        E("vector", (lambda bm, sbv: lambda v: v.tensor_scalar(out=sbv, in0=bm, scalar1=INV7, scalar2=EPS12, op0=ALU.mult, op1=ALU.max))(bm, sbv),
          r=("DQS%d" % sl,), w=("DQS%d" % sl,))
        E("vector", (lambda bm, sbv: lambda v: v.reciprocal(out=bm, in_=sbv))(bm, sbv),
          r=("DQS%d" % sl,), w=("DQS%d" % sl,))
        mtw = MT[:, 0:D]
        if lim < 4:
            return
        E("vector", (lambda wsl, bm, mtw: lambda v: v.tensor_tensor(
            out=mtw.rearrange("p (b c) -> p b c", c=32), in0=wsl.rearrange("p (b c) -> p b c", c=32),
            in1=bm.unsqueeze(2).broadcast_to([128, 32, 32]), op=ALU.mult))(wsl, bm, mtw),
          r=("DQS%d" % sl, "WB%d" % sl, "MT0", "MT1"), w=("MT0", "MT1", "WB%d" % sl))
        if lim < 5:
            return
        E("gpsimd", (lambda mtw: lambda g: g.tensor_scalar(out=mtw, in0=mtw, scalar1=CM, scalar2=CM, op0=ALU.add, op1=ALU.subtract))(mtw),
          r=("MT0", "MT1"), w=("MT0", "MT1"))
        if lim < 6:
            return
        E("vector", (lambda mtw, sbv, sl: lambda v: v.tensor_tensor(
            out=LW[:, sl * 1024:(sl + 1) * 1024].rearrange("p (b c) -> p b c", c=32),
            in0=mtw.rearrange("p (b c) -> p b c", c=32),
            in1=sbv.unsqueeze(2).broadcast_to([128, 32, 32]), op=ALU.mult))(mtw, sbv, sl),
          r=("MT0", "MT1", "DQS%d" % sl, "LW%d" % sl), w=("LW%d" % sl,))
        if lim < 7:
            return
        for ic in range(DC):
            tsl = ic % 2
            E("tensor", (lambda sl, ic, tsl: lambda t: t.transpose(PTR[:, tsl * 128:(tsl + 1) * 128], LW[:, sl * 1024 + ic * 128:sl * 1024 + (ic + 1) * 128], IDB[:]))(sl, ic, tsl),
              r=("LW%d" % sl, "IDB", "PTR%d" % tsl), w=("PTR%d" % tsl,))
            E("vector", (lambda tsl: lambda v: v.tensor_copy(out=TS2[:, tsl * 128:(tsl + 1) * 128], in_=PTR[:, tsl * 128:(tsl + 1) * 128]))(tsl),
              r=("PTR%d" % tsl, "TS2%d" % tsl), w=("TS2%d" % tsl,))
            ED("gpsimd", (lambda wdst, oc, ic, tsl: lambda g: g.dma_start(out=wdst[oc, ic], in_=TS2[:, tsl * 128:(tsl + 1) * 128]))(wdst, oc, ic, tsl),
               r=("TS2%d" % tsl,), w=("%sT%d" % (stash_tag_pfx, oc),), chan="cst")

    def dq_chunk_dn(oc, sl):
        segs = [(WB[:, sl * D:(sl + 1) * D], ("WB%d" % sl,), 8, 0),
                (MT[:, 2 * TG:6 * TG], ("MT2", "MT3", "MT4", "MT5"), 16, 8)]
        for seg, (ssl, stoks, nch, chb) in enumerate(segs):
            stok = stoks[0]
            nblk = (nch * 128) // 32
            bm = DQS[:, sl * 128:sl * 128 + nblk]
            sbv = DQS[:, sl * 128 + 64:sl * 128 + 64 + nblk]
            E("vector", (lambda ssl, bm: lambda g: g.tensor_reduce(out=bm, in_=ssl.rearrange("p (b c) -> p b c", c=32), axis=AX.X, op=ALU.max, apply_absolute_value=True))(ssl, bm),
              r=stoks + ("DQS%d" % sl,), w=("DQS%d" % sl,))
            E("vector", (lambda bm, sbv: lambda v: v.tensor_scalar(out=sbv, in0=bm, scalar1=INV7, scalar2=EPS12, op0=ALU.mult, op1=ALU.max))(bm, sbv),
              r=("DQS%d" % sl,), w=("DQS%d" % sl,))
            E("vector", (lambda bm, sbv: lambda v: v.reciprocal(out=bm, in_=sbv))(bm, sbv),
              r=("DQS%d" % sl,), w=("DQS%d" % sl,))
            E("vector", (lambda ssl, bm, nblk: lambda v: v.tensor_tensor(
                out=ssl.rearrange("p (b c) -> p b c", c=32), in0=ssl.rearrange("p (b c) -> p b c", c=32),
                in1=bm.unsqueeze(2).broadcast_to([128, nblk, 32]), op=ALU.mult))(ssl, bm, nblk),
              r=("DQS%d" % sl,) + stoks, w=stoks)
            E("gpsimd", (lambda ssl: lambda g: g.tensor_scalar(out=ssl, in0=ssl, scalar1=CM, scalar2=CM, op0=ALU.add, op1=ALU.subtract))(ssl),
              r=stoks, w=stoks)
            E("vector", (lambda ssl, sbv, nblk: lambda v: v.tensor_tensor(
                out=ssl.rearrange("p (b c) -> p b c", c=32), in0=ssl.rearrange("p (b c) -> p b c", c=32),
                in1=sbv.unsqueeze(2).broadcast_to([128, nblk, 32]), op=ALU.mult))(ssl, sbv, nblk),
              r=stoks + ("DQS%d" % sl,), w=stoks)
            for j in range(nch):
                tsl = j % 2
                E("vector", (lambda ssl, j, tsl: lambda v: v.tensor_copy(out=TS2[:, tsl * 128:(tsl + 1) * 128], in_=ssl[:, j * 128:(j + 1) * 128]))(ssl, j, tsl),
                  r=stoks + ("TS2%d" % tsl,), w=("TS2%d" % tsl,))
                E("tensor", (lambda tsl: lambda t: t.transpose(PTR[:, (2 + tsl) * 128:(3 + tsl) * 128], TS2[:, tsl * 128:(tsl + 1) * 128], IDB[:]))(tsl),
                  r=("TS2%d" % tsl, "IDB", "PTRd%d" % tsl), w=("PTRd%d" % tsl,))
                E("vector", (lambda tsl: lambda v: v.tensor_copy(out=TS2[:, tsl * 128:(tsl + 1) * 128], in_=PTR[:, (2 + tsl) * 128:(3 + tsl) * 128]))(tsl),
                  r=("PTRd%d" % tsl, "TS2%d" % tsl), w=("TS2%d" % tsl,))
                ic = chb + j
                ED("gpsimd", (lambda oc, ic, tsl: lambda g: g.dma_start(out=wdnT[oc, ic], in_=TS2[:, tsl * 128:(tsl + 1) * 128]))(oc, ic, tsl),
                   r=("TS2%d" % tsl,), w=("wdT%d" % oc,), chan="cst")

    def ffn_phase(phase):
        pf = "F%d" % phase
        E("vector", lambda v: v.memset(MAXC[:, MX_NR:MX_NR + 2], 0.0), r=("MXNR",), w=("MXNR",))
        ln_stats_and_apply(pf, True)
        if phase == 1 and SUB <= 0:
            return
        E("vector", lambda v: v.tensor_reduce(out=MAXC[:, MX_F:MX_F + 1], in_=MAXC[:, MX_A:MX_A + 32], axis=AX.X, op=ALU.max),
          r=("MXA",), w=("MXF",))
        part_reduce_to_scalar("MXF", MX_F, 0, pf + "gx")
        collective_max(0, 1, (pf + "gx",), pf + "gxg")
        sc_op(lambda v: v.tensor_scalar(out=SROW[0:1, 1:2], in0=SROW[0:1, 0:1], scalar1=INV_QMAX, scalar2=EPS12, op0=ALU.mult, op1=ALU.max),
              r=(pf + "gxg",), w=(pf + "sx",))
        sc_op(lambda v: v.reciprocal(out=SROW[0:1, 2:3], in_=SROW[0:1, 1:2]),
              r=(pf + "sx",), w=(pf + "rx",))
        bcast_slots(1, 2, 0, (pf + "sx", pf + "rx"))
        if phase == 1 and SUB <= 1:
            return
        for dc in range(DC):
            for tg in range(NTG):
                asl = A2[:, 1 + dc * T1 + tg * TG: 1 + dc * T1 + tg * TG + TG]
                csl = CB[:, dc * TC + tg * TG: dc * TC + tg * TG + TG]
                E("vector", (lambda asl: lambda v: v.tensor_scalar(out=asl, in0=asl, scalar1=COLS[:, 1:2], scalar2=CM, op0=ALU.mult, op1=ALU.add))(asl),
                  r=("A2_%d" % dc, "COLS"), w=("A2_%d" % dc,))
                E("vector", (lambda asl, csl: lambda v: v.tensor_scalar(out=csl, in0=asl, scalar1=CM, scalar2=COLS[:, 0:1], op0=ALU.subtract, op1=ALU.mult))(asl, csl),
                  r=("A2_%d" % dc, "COLS", "CB%d" % dc), w=("CB%d" % dc,))
        if phase == 1 and SUB <= 2:
            return
        if phase == 1:
            for stash_pfx, wsrc, wdst in (("wu", wup, wupT), ("wgx", wg, wgT)):
                for oc in range(IC):
                    if (0 if stash_pfx == "wu" else IC) + oc >= DQN:
                        continue
                    sl = (oc % 2) if DQSLOT2 else 0
                    ED("sync", (lambda wsrc, oc, sl: lambda g: g.dma_start(out=WB[:, sl * D:(sl + 1) * D], in_=wsrc[oc * 128:(oc + 1) * 128, :]))(wsrc, oc, sl),
                       r=("WB%d" % sl,), w=("WB%d" % sl,), chan="cwb%d" % sl)
                    dq_chunk(oc, sl, wdst, stash_pfx)
            for oc in range(DC):
                if not DQDN:
                    continue
                sl = oc % 2
                ED("sync", (lambda oc, sl: lambda g: g.dma_start(out=WB[:, sl * D:(sl + 1) * D], in_=wdn[oc * 128:(oc + 1) * 128, 0:D]))(oc, sl),
                   r=("WB%d" % sl,), w=("WB%d" % sl,), chan="cwb%d" % sl)
                ED("sync", (lambda oc, sl: lambda g: g.dma_start(out=MT[:, 2 * TG:6 * TG], in_=wdn[oc * 128:(oc + 1) * 128, D:I3]))(oc, sl),
                   r=("MT2", "MT3", "MT4", "MT5"), w=("MT2", "MT3", "MT4", "MT5"), chan="cmtw")
                dq_chunk_dn(oc, sl)
        if phase == 1 and SUB <= 3:
            return
        for widx, (stash_pfx, dbuf) in enumerate((("wu", upbuf), ("wgx", gbuf))):
            mxb = MX_A if widx == 0 else MX_B
            mxtok = "MXA" if widx == 0 else "MXB"
            for oc in range(IC):
                sl = oc % 2
                stash = wupT if widx == 0 else wgT
                ED("sync", (lambda stash, oc, sl: lambda g: g.dma_start(out=LW[:, sl * 1024:(sl + 1) * 1024].rearrange("p (i j) -> p i j", j=128), in_=stash[oc].rearrange("i p j -> p i j")))(stash, oc, sl),
                   r=("LW%d" % sl, "%sT%d" % (stash_pfx, oc)), w=("LW%d" % sl,), chan="clw%d" % sl)
                for tg in range(NTG):
                    bi = tg % 2 + (0 if widx == 0 else 2)
                    bank = PMM[bi]
                    def mmfn(oc=oc, tg=tg, sl=sl, bank=bank):
                        def f(t):
                            last = None
                            for ic in range(DC):
                                last = t.matmul(bank[:, 0:TG], LW[:, sl * 1024 + ic * 128: sl * 1024 + (ic + 1) * 128],
                                                CB[:, ic * TC + tg * TG: ic * TC + tg * TG + TG],
                                                start=(ic == 0), stop=(ic == DC - 1))
                            return last
                        return f
                    E("tensor", mmfn(), r=tuple(["LW%d" % sl] + ["CB%d" % ic for ic in range(DC)] + ["PMM%d" % bi]),
                      w=("PMM%d" % bi,))
                    ust = bi
                    E("vector", (lambda bank, ust, oc, tg, mxb: lambda v: v.tensor_scalar(
                        out=US[:, ust * TG:(ust + 1) * TG], in0=bank[:, 0:TG], scalar1=1.0, scalar2=None, op0=ALU.mult,
                        op1=ALU.max, accum_out=MAXC[:, mxb + oc * NTG + tg:mxb + oc * NTG + tg + 1]))(bank, ust, oc, tg, mxb),
                      r=("PMM%d" % ust, "US%d" % ust), w=("US%d" % ust, mxtok))
                    E("vector", (lambda ust, widx: lambda g: g.tensor_reduce(
                        out=MAXC[:, MX_NT + widx:MX_NT + widx + 1], in_=US[:, ust * TG:(ust + 1) * TG], axis=AX.X, op=ALU.max, negate=True))(ust, widx),
                      r=("US%d" % ust, "MXNT%d" % widx), w=("MXNT%d" % widx,))
                    E("vector", (lambda widx: lambda g: g.tensor_tensor(
                        out=MAXC[:, MX_NR + widx:MX_NR + widx + 1], in0=MAXC[:, MX_NR + widx:MX_NR + widx + 1],
                        in1=MAXC[:, MX_NT + widx:MX_NT + widx + 1], op=ALU.max))(widx),
                      r=("MXNT%d" % widx, "MXNR"), w=("MXNR",))
                    ED("gpsimd", (lambda dbuf, oc, tg, ust: lambda g: g.dma_start(
                        out=dbuf[oc * 128:(oc + 1) * 128, tg * TG:(tg + 1) * TG], in_=US[:, ust * TG:(ust + 1) * TG]))(dbuf, oc, tg, ust),
                       r=("US%d" % ust,), w=(pf + "UB%d_%d_%d" % (widx, oc, tg),), chan="cus%d" % ust)
        for widx in range(2):
            mxb = MX_A if widx == 0 else MX_B
            mxtok = "MXA" if widx == 0 else "MXB"
            E("vector", (lambda mxb: lambda v: v.tensor_reduce(out=MAXC[:, MX_P:MX_P + 1], in_=MAXC[:, mxb:mxb + 96], axis=AX.X, op=ALU.max))(mxb),
              r=(mxtok,), w=("MXP",))
            E("vector", (lambda widx: lambda v: v.tensor_tensor(out=MAXC[:, MX_F:MX_F + 1], in0=MAXC[:, MX_P:MX_P + 1], in1=MAXC[:, MX_NR + widx:MX_NR + widx + 1], op=ALU.max))(widx),
              r=("MXP", "MXNR"), w=("MXF",))
            part_reduce_to_scalar("MXF", MX_F, 8 + widx, pf + "gm%d" % widx)
        if phase == 1 and SUB <= 4:
            return
        collective_max(8, 2, (pf + "gm0", pf + "gm1"), pf + "gmg")
        sc_op(lambda v: v.tensor_scalar(out=SROW[0:1, 10:12], in0=SROW[0:1, 8:10], scalar1=INV_QMAX, scalar2=EPS12, op0=ALU.mult, op1=ALU.max),
              r=(pf + "gmg",), w=(pf + "sug",))
        sc_op(lambda v: v.reciprocal(out=SROW[0:1, 12:14], in_=SROW[0:1, 10:12]),
              r=(pf + "sug",), w=(pf + "rug",))
        sc_op(lambda v: v.tensor_scalar(out=SROW[0:1, 14:15], in0=SROW[0:1, 11:12], scalar1=INV65536, scalar2=None, op0=ALU.mult),
              r=(pf + "sug",), w=(pf + "kap",))
        bcast_slots(12, 2, 2, (pf + "rug",))
        for oc in range(IC):
            for tg in range(NTG):
                i = (oc * NTG + tg) % 2
                gt = MT[:, i * TG:(i + 1) * TG]
                sg = MT[:, (2 + i) * TG:(3 + i) * TG]
                t2t = MT[:, (4 + i) * TG:(5 + i) * TG]
                ED("sync", (lambda oc, tg, gt: lambda g: g.dma_start(out=gt, in_=gbuf[oc * 128:(oc + 1) * 128, tg * TG:(tg + 1) * TG]))(oc, tg, gt),
                   r=(pf + "UB1_%d_%d" % (oc, tg), "MT%d" % i), w=("MT%d" % i,), chan="cmg%d" % i)
                E("scalar", (lambda gt, sg: lambda s: s.activation(out=sg, in_=gt, func=AF.Sigmoid))(gt, sg),
                  r=("MT%d" % i, "MT%d" % (2 + i)), w=("MT%d" % (2 + i),))
                E("vector", (lambda sg: lambda v: v.tensor_scalar(out=sg, in0=sg, scalar1=K65536, scalar2=CM, op0=ALU.mult, op1=ALU.add))(sg),
                  r=("MT%d" % (2 + i),), w=("MT%d" % (2 + i),))
                E("vector", (lambda sg: lambda v: v.tensor_scalar(out=sg, in0=sg, scalar1=CM, scalar2=None, op0=ALU.subtract))(sg),
                  r=("MT%d" % (2 + i),), w=("MT%d" % (2 + i),))
                E("vector", (lambda gt: lambda v: v.tensor_scalar(out=gt, in0=gt, scalar1=COLS[:, 3:4], scalar2=CM, op0=ALU.mult, op1=ALU.add))(gt),
                  r=("MT%d" % i, "COLS"), w=("MT%d" % i,))
                E("vector", (lambda gt, sg, t2t: lambda v: v.scalar_tensor_tensor(out=t2t, in0=gt, scalar=-CM, in1=sg, op0=ALU.add, op1=ALU.mult))(gt, sg, t2t),
                  r=("MT%d" % i, "MT%d" % (2 + i), "MT%d" % (4 + i)), w=("MT%d" % (4 + i),))
                cix = MX_A + oc * NTG + tg
                E("vector", (lambda t2t, cix: lambda g: g.tensor_reduce(out=MAXC[:, cix:cix + 1], in_=t2t, axis=AX.X, op=ALU.max, apply_absolute_value=True))(t2t, cix),
                  r=("MT%d" % (4 + i),), w=("MXA",))
                ED("gpsimd", (lambda t2t, oc, tg: lambda g: g.dma_start(out=t2buf[oc * 128:(oc + 1) * 128, tg * TG:(tg + 1) * TG], in_=t2t))(t2t, oc, tg),
                   r=("MT%d" % (4 + i),), w=(pf + "T2%d_%d" % (oc, tg),), chan="cms%d" % i)
        E("vector", lambda v: v.tensor_reduce(out=MAXC[:, MX_F:MX_F + 1], in_=MAXC[:, MX_A:MX_A + 96], axis=AX.X, op=ALU.max),
          r=("MXA",), w=("MXF",))
        part_reduce_to_scalar("MXF", MX_F, 16, pf + "g2")
        collective_max(16, 1, (pf + "g2",), pf + "g2g")
        sc_op(lambda v: v.tensor_tensor(out=SROW[0:1, 17:18], in0=SROW[0:1, 16:17], in1=SROW[0:1, 14:15], op=ALU.mult),
              r=(pf + "g2g", pf + "kap"), w=())
        sc_op(lambda v: v.tensor_scalar(out=SROW[0:1, 18:19], in0=SROW[0:1, 17:18], scalar1=INV_QMAX, scalar2=EPS12, op0=ALU.mult, op1=ALU.max),
              r=(), w=(pf + "s2",))
        sc_op(lambda v: v.reciprocal(out=SROW[0:1, 19:20], in_=SROW[0:1, 18:19]),
              r=(pf + "s2",), w=())
        sc_op(lambda v: v.tensor_tensor(out=SROW[0:1, 20:21], in0=SROW[0:1, 19:20], in1=SROW[0:1, 14:15], op=ALU.mult),
              r=(pf + "kap",), w=(pf + "r2",))
        sc_op(lambda v: v.tensor_tensor(out=SROW[0:1, 21:22], in0=SROW[0:1, 18:19], in1=SROW[0:1, 10:11], op=ALU.mult),
              r=(pf + "s2", pf + "sug"), w=(pf + "lam",))
        bcast_slots(20, 1, 4, (pf + "r2",))
        if phase == 1 and SUB <= 5:
            return
        for oc in range(IC):
            for tg in range(NTG):
                i = (oc * NTG + tg) % 2
                t2t = MT[:, i * TG:(i + 1) * TG]
                ut = MT[:, (2 + i) * TG:(3 + i) * TG]
                t4t = MT[:, (4 + i) * TG:(5 + i) * TG]
                ED("sync", (lambda oc, tg, t2t: lambda g: g.dma_start(out=t2t, in_=t2buf[oc * 128:(oc + 1) * 128, tg * TG:(tg + 1) * TG]))(oc, tg, t2t),
                   r=(pf + "T2%d_%d" % (oc, tg), "MT%d" % i), w=("MT%d" % i,), chan="cmg%d" % i)
                ED("sync", (lambda oc, tg, ut: lambda g: g.dma_start(out=ut, in_=upbuf[oc * 128:(oc + 1) * 128, tg * TG:(tg + 1) * TG]))(oc, tg, ut),
                   r=(pf + "UB0_%d_%d" % (oc, tg), "MT%d" % (2 + i)), w=("MT%d" % (2 + i),), chan="cmu%d" % i)
                E("vector", (lambda t2t: lambda v: v.tensor_scalar(out=t2t, in0=t2t, scalar1=COLS[:, 4:5], scalar2=CM, op0=ALU.mult, op1=ALU.add))(t2t),
                  r=("MT%d" % i, "COLS"), w=("MT%d" % i,))
                E("vector", (lambda ut: lambda v: v.tensor_scalar(out=ut, in0=ut, scalar1=COLS[:, 2:3], scalar2=CM, op0=ALU.mult, op1=ALU.add))(ut),
                  r=("MT%d" % (2 + i), "COLS"), w=("MT%d" % (2 + i),))
                E("vector", (lambda ut: lambda v: v.tensor_scalar(out=ut, in0=ut, scalar1=CM, scalar2=None, op0=ALU.subtract))(ut),
                  r=("MT%d" % (2 + i),), w=("MT%d" % (2 + i),))
                E("vector", (lambda t2t, ut, t4t: lambda v: v.scalar_tensor_tensor(out=t4t, in0=t2t, scalar=-CM, in1=ut, op0=ALU.add, op1=ALU.mult))(t2t, ut, t4t),
                  r=("MT%d" % i, "MT%d" % (2 + i), "MT%d" % (4 + i)), w=("MT%d" % (4 + i),))
                cix = MX_A + oc * NTG + tg
                E("vector", (lambda t4t, cix: lambda g: g.tensor_reduce(out=MAXC[:, cix:cix + 1], in_=t4t, axis=AX.X, op=ALU.max, apply_absolute_value=True))(t4t, cix),
                  r=("MT%d" % (4 + i),), w=("MXA",))
                ED("gpsimd", (lambda t4t, oc, tg: lambda g: g.dma_start(out=gbuf[oc * 128:(oc + 1) * 128, tg * TG:(tg + 1) * TG], in_=t4t))(t4t, oc, tg),
                   r=("MT%d" % (4 + i),), w=(pf + "T4%d_%d" % (oc, tg),), chan="cms%d" % i)
        E("vector", lambda v: v.tensor_reduce(out=MAXC[:, MX_F:MX_F + 1], in_=MAXC[:, MX_A:MX_A + 96], axis=AX.X, op=ALU.max),
          r=("MXA",), w=("MXF",))
        part_reduce_to_scalar("MXF", MX_F, 24, pf + "g4")
        collective_max(24, 1, (pf + "g4",), pf + "g4g")
        sc_op(lambda v: v.tensor_tensor(out=SROW[0:1, 25:26], in0=SROW[0:1, 24:25], in1=SROW[0:1, 21:22], op=ALU.mult),
              r=(pf + "g4g", pf + "lam"), w=())
        sc_op(lambda v: v.tensor_scalar(out=SROW[0:1, 26:27], in0=SROW[0:1, 25:26], scalar1=INV_QMAX, scalar2=EPS12, op0=ALU.mult, op1=ALU.max),
              r=(), w=(pf + "s4",))
        sc_op(lambda v: v.reciprocal(out=SROW[0:1, 27:28], in_=SROW[0:1, 26:27]),
              r=(pf + "s4",), w=())
        sc_op(lambda v: v.tensor_tensor(out=SROW[0:1, 28:29], in0=SROW[0:1, 27:28], in1=SROW[0:1, 21:22], op=ALU.mult),
              r=(pf + "lam",), w=(pf + "r4",))
        sc_op(lambda v: v.tensor_scalar(out=SROW[0:1, 29:30], in0=SROW[0:1, 26:27], scalar1=0.5, scalar2=None, op0=ALU.mult),
              r=(pf + "s4",), w=(pf + "hs4",))
        bcast_slots(28, 2, 5, (pf + "r4", pf + "hs4"))
        if phase == 1 and SUB <= 6:
            return
        for tg in range(NTG):
            for ic in range(IC):
                i = ic % 2
                t4t = MT[:, i * TG:(i + 1) * TG]
                ED("sync", (lambda ic, tg, t4t: lambda g: g.dma_start(out=t4t, in_=gbuf[ic * 128:(ic + 1) * 128, tg * TG:(tg + 1) * TG]))(ic, tg, t4t),
                   r=(pf + "T4%d_%d" % (ic, tg), "MT%d" % i), w=("MT%d" % i,), chan="cmg%d" % i)
                E("vector", (lambda t4t: lambda v: v.tensor_scalar(out=t4t, in0=t4t, scalar1=COLS[:, 5:6], scalar2=CM, op0=ALU.mult, op1=ALU.add))(t4t),
                  r=("MT%d" % i, "COLS"), w=("MT%d" % i,))
                csl = CB[:, ic * TG:(ic + 1) * TG]
                E("vector", (lambda t4t, csl: lambda v: v.tensor_scalar(out=csl, in0=t4t, scalar1=CM, scalar2=None, op0=ALU.subtract))(t4t, csl),
                  r=("MT%d" % i, "CB%d" % (ic // 4)), w=("CB%d" % (ic // 4),))
            for oc in range(DC):
                bank = PMM[oc % 2]
                for grp in range(3):
                    sl = (oc * 3 + grp) % 2
                    ED("sync", (lambda oc, grp, sl: lambda g: g.dma_start(out=LW[:, sl * 1024:(sl + 1) * 1024].rearrange("p (i j) -> p i j", j=128), in_=wdnT[oc, grp * 8:(grp + 1) * 8].rearrange("i p j -> p i j")))(oc, grp, sl),
                       r=("LW%d" % sl, "wdT%d" % oc), w=("LW%d" % sl,), chan="clw%d" % sl)
                    def mmdn(oc=oc, grp=grp, sl=sl, bank=bank, tg=tg):
                        def f(t):
                            last = None
                            for j in range(8):
                                ic = grp * 8 + j
                                last = t.matmul(bank[:, 0:TG], LW[:, sl * 1024 + j * 128: sl * 1024 + (j + 1) * 128],
                                                CB[:, ic * TG:(ic + 1) * TG],
                                                start=(ic == 0), stop=(ic == IC - 1))
                            return last
                        return f
                    E("tensor", mmdn(), r=tuple(["LW%d" % sl] + ["CB%d" % icg for icg in range(grp * 2, grp * 2 + 2)] + ["PMM%d" % (oc % 2)]),
                      w=("PMM%d" % (oc % 2),))
                xsl = XT[:, oc * TC + tg * TG: oc * TC + tg * TG + TG]
                E("vector", (lambda bank, xsl: lambda v: v.scalar_tensor_tensor(out=xsl, in0=bank[:, 0:TG], scalar=COLS[:, 6:7], in1=xsl, op0=ALU.mult, op1=ALU.add))(bank, xsl),
                  r=("PMM%d" % (oc % 2), "COLS", "XT%d" % oc), w=("XT%d" % oc,))
                if phase == 2:
                    ED("gpsimd", (lambda oc, tg, xsl: lambda g: g.dma_start(out=out[oc * 128:(oc + 1) * 128, tg * TG:(tg + 1) * TG], in_=xsl))(oc, tg, xsl),
                       r=("XT%d" % oc,), w=("OUT%d_%d" % (oc, tg),), chan="cout")

    def attn_proj(mi, wstash, wtag, dst_tensor, act, outpfx):
        for dc in range(DC):
            ED("sync", (lambda mi, dc: lambda g: g.dma_start(out=CB[:, dc * TC:(dc + 1) * TC], in_=mixbuf[mi, dc * 128:(dc + 1) * 128, :]))(mi, dc),
               r=("MIXB%d_%d" % (mi, dc), "CB%d" % dc), w=("CB%d" % dc,), chan="cmxr")
        if DBG and mi == 0:
            for dc in range(DC):
                E("vector", (lambda dc: lambda v: v.tensor_copy(out=MT[:, 0:TC], in_=CB[:, dc * TC:(dc + 1) * TC]))(dc),
                  r=("CB%d" % dc, "MT0", "MT1", "MT2", "MT3"), w=("MT0", "MT1", "MT2", "MT3"))
                ED("gpsimd", (lambda dc: lambda g: g.dma_start(out=dbg3[dc * 128:(dc + 1) * 128, :], in_=MT[:, 0:TC]))(dc),
                   r=("MT0", "MT1", "MT2", "MT3"), w=("DBG3_%d" % dc,), chan="cdbg")
        for oc in range(DC):
            sl = oc % 2
            ED("sync", (lambda wstash, oc, sl: lambda g: g.dma_start(out=LW[:, sl * 1024:(sl + 1) * 1024].rearrange("p (i j) -> p i j", j=128), in_=wstash[:, oc * 128:(oc + 1) * 128].rearrange("(i p) j -> p i j", p=128)))(wstash, oc, sl),
               r=("LW%d" % sl, wtag), w=("LW%d" % sl,), chan="clw%d" % sl)
            for tg in range(NTG):
                bi = (oc * NTG + tg) % 2
                bank = PMM[bi]
                def mmp(oc=oc, tg=tg, sl=sl, bank=bank):
                    def f(t):
                        last = None
                        for ic in range(DC):
                            last = t.matmul(bank[:, 0:TG], LW[:, sl * 1024 + ic * 128:sl * 1024 + (ic + 1) * 128],
                                            CB[:, ic * TC + tg * TG:ic * TC + tg * TG + TG],
                                            start=(ic == 0), stop=(ic == DC - 1))
                        return last
                    return f
                E("tensor", mmp(), r=tuple(["LW%d" % sl] + ["CB%d" % ic for ic in range(DC)] + ["PMM%d" % bi]),
                  w=("PMM%d" % bi,))
                dsl = dst_tensor[:, 1 + oc * T1 + tg * TG: 1 + oc * T1 + tg * TG + TG]
                if dst_tensor is A2:
                    extr = ("A2_%d" % oc,)
                else:
                    extr = ("B2_%d" % oc,) + tuple("XSP%d" % q for q in range(DC))
                dtok = extr[0]
                E("scalar", (lambda bank, dsl, act: lambda s: s.activation(out=dsl, in_=bank[:, 0:TG], func=act))(bank, dsl, act),
                  r=("PMM%d" % bi,) + extr, w=(dtok, "%s%d_%d" % (outpfx, oc, tg)))

    def attention():
        for dc in range(DC):
            E("vector", (lambda dc: lambda v: v.tensor_copy(out=MAXC[:, MX_BX + dc:MX_BX + dc + 1], in_=XT[:, dc * TC + TC - 1:dc * TC + TC]))(dc),
              r=("XT%d" % dc,), w=("MXBX",))
        E("tensor", lambda t: t.transpose(PSG[0:8, 0:128], MAXC[:, MX_BX:MX_BX + 8], IDF[:]),
          r=("MXBX", "IDF", "PSG"), w=("PSG",))
        E("scalar", lambda s: s.copy(out=XR2[:, :], in_=PSG[0:8, 0:128]),
          r=("PSG", "SQT"), w=("SQT",))
        E("tensor", lambda t: t.matmul(PSG[:, 0:128], SEL8[:, :], XR2[:, :], start=True, stop=True),
          r=("SQT", "SEL8", "PSG"), w=("PSG",))
        E("vector", lambda v: v.tensor_copy(out=CCT[:], in_=PSG[:, 0:128]),
          r=("PSG", "SQT"), w=("SQT",))
        collective_add_tile("SQT", "bndx")
        E("tensor", lambda t: t.matmul(PSG[0:8, 0:128], SELR8[:, :], CCT[:], start=True, stop=True),
          r=("bndx", "SQT", "SELR8", "PSG"), w=("PSG",))
        E("scalar", lambda s: s.copy(out=XR2[:, :], in_=PSG[0:8, 0:128]),
          r=("PSG", "SQT"), w=("SQT",))
        E("scalar", lambda s: s.activation(out=XRS[:, :], in_=XR2[:, :], func=AF.Square),
          r=("SQT", "SQT"), w=("SQT",))
        E("tensor", lambda t: t.matmul(PBC[0][0:1, 0:128], ONEC[0:8, 0:1], XR2[:, :], start=True, stop=True),
          r=("SQT", "ONEC", "PBC0"), w=("PBC0",))
        E("tensor", lambda t: t.matmul(PBC[1][0:1, 0:128], ONEC[0:8, 0:1], XRS[:, :], start=True, stop=True),
          r=("SQT", "ONEC", "PBC1"), w=("PBC1",))
        sc_op(lambda v: v.tensor_reduce(out=SROW[0:1, 40:41], in_=PBC[0][0:1, 0:128], axis=AX.X, op=ALU.add),
              r=("PBC0",), w=())
        sc_op(lambda v: v.tensor_reduce(out=SROW[0:1, 41:42], in_=PBC[1][0:1, 0:128], axis=AX.X, op=ALU.add),
              r=("PBC1",), w=())
        sc_op(lambda v: v.tensor_scalar(out=SROW[0:1, 40:42], in0=SROW[0:1, 40:42], scalar1=INV_D, scalar2=None, op0=ALU.mult),
              r=(), w=())
        sc_op(lambda v: v.tensor_tensor(out=SROW[0:1, 42:43], in0=SROW[0:1, 40:41], in1=SROW[0:1, 40:41], op=ALU.mult),
              r=(), w=())
        sc_op(lambda v: v.tensor_scalar(out=SROW[0:1, 43:44], in0=SROW[0:1, 42:43], scalar1=-1.0, scalar2=LNEPS, op0=ALU.mult, op1=ALU.add),
              r=(), w=())
        sc_op(lambda v: v.tensor_tensor(out=SROW[0:1, 43:44], in0=SROW[0:1, 41:42], in1=SROW[0:1, 43:44], op=ALU.add),
              r=(), w=())
        E("scalar", lambda s: s.activation(out=SROW[0:1, 44:45], in_=SROW[0:1, 43:44], func=AF.Ln),
          r=("SROW",), w=("SROW",))
        E("scalar", lambda s: s.activation(out=SROW[0:1, 45:46], in_=SROW[0:1, 44:45], func=AF.Exp, scale=-0.5),
          r=("SROW",), w=("SROW",))
        sc_op(lambda v: v.scalar_tensor_tensor(out=SROW[0:1, 46:47], in0=SROW[0:1, 40:41], scalar=-1.0, in1=SROW[0:1, 45:46], op0=ALU.mult, op1=ALU.mult),
              r=(), w=("bln",))
        bcast_slots(45, 2, 7, ("bln",))
        E("tensor", lambda t: t.transpose(PSG[:, 0:8], XR2[:, :], IDF[0:8, 0:8]),
          r=("SQT", "IDF", "PSG"), w=("PSG",))
        E("vector", lambda v: v.tensor_copy(out=MAXC[:, MX_BX:MX_BX + 8], in_=PSG[:, 0:8]),
          r=("PSG", "MXBX"), w=("MXBX",))
        for dc in range(DC):
            E("vector", (lambda dc: lambda v: v.tensor_scalar(out=A2[:, dc * T1:dc * T1 + 1], in0=MAXC[:, MX_BX + dc:MX_BX + dc + 1], scalar1=COLS[:, 7:8], scalar2=COLS[:, 8:9], op0=ALU.mult, op1=ALU.add))(dc),
              r=("MXBX", "COLS", "A2_%d" % dc), w=("A2_%d" % dc,))
        ln_stats_and_apply("A", False)
        if DBG:
            for dc in range(DC):
                ED("gpsimd", (lambda dc: lambda g: g.dma_start(out=dbg2[dc * 128:(dc + 1) * 128, :], in_=A2[:, 1 + dc * T1:1 + dc * T1 + TC]))(dc),
                   r=("A2_%d" % dc,), w=("DBG2_%d" % dc,), chan="cdbg")
            ED("gpsimd", lambda g: g.dma_start(out=dbg4[:, 0:128], in_=SQT[:, 128:256]), r=("SQT",), w=("DBG4",), chan="cdbg")
            ED("gpsimd", lambda g: g.dma_start(out=dbg4[:, 128:136], in_=MAXC[:, MX_BX:MX_BX + 8]), r=("MXBX",), w=("DBG4",), chan="cdbg")
            ED("gpsimd", lambda g: g.dma_start(out=dbg4[:, 136:145], in_=COLS[:, 0:9]), r=("COLS",), w=("DBG4",), chan="cdbg")
            ED("gpsimd", lambda g: g.dma_start(out=dbg4[0:1, 152:159], in_=SROW[0:1, 40:47]), r=("SROW",), w=("DBG4",), chan="cdbg")
            for dc in range(DC):
                ED("gpsimd", (lambda dc: lambda g: g.dma_start(out=dbg4[:, 216 + dc:217 + dc], in_=A2[:, dc * T1:dc * T1 + 1], allow_slow_non_contiguous=True))(dc),
                   r=("A2_%d" % dc,), w=("DBG4",), chan="cdbg")
        for dc in range(DC):
            ED("gpsimd", (lambda dc: lambda g: g.dma_start(out=xspill[dc * 128:(dc + 1) * 128, :], in_=XT[:, dc * TC:(dc + 1) * TC]))(dc),
               r=("XT%d" % dc,), w=("XSP%d" % dc,), chan="csp")
        for mi, colb in ((0, 16), (1, 24), (2, 32)):
            for dc in range(DC):
                cur = A2[:, 1 + dc * T1:1 + dc * T1 + TC]
                prv = A2[:, dc * T1:dc * T1 + TC]
                scr = MT[:, 0:TC]
                E("vector", (lambda cur, prv, scr: lambda v: v.tensor_tensor(out=scr, in0=cur, in1=prv, op=ALU.subtract))(cur, prv, scr),
                  r=("A2_%d" % dc, "MT0", "MT1", "MT2", "MT3"), w=("MT0", "MT1", "MT2", "MT3"))
                E("vector", (lambda prv, scr, colb, dc: lambda v: v.scalar_tensor_tensor(out=CB[:, dc * TC:(dc + 1) * TC], in0=scr, scalar=DCV[:, colb + dc:colb + dc + 1], in1=prv, op0=ALU.mult, op1=ALU.add))(prv, scr, colb, dc),
                  r=("MT0", "MT1", "MT2", "MT3", "A2_%d" % dc, "DCV", "CB%d" % dc), w=("CB%d" % dc,))
                ED("gpsimd", (lambda mi, dc: lambda g: g.dma_start(out=mixbuf[mi, dc * 128:(dc + 1) * 128, :], in_=CB[:, dc * TC:(dc + 1) * TC]))(mi, dc),
                   r=("CB%d" % dc,), w=("MIXB%d_%d" % (mi, dc),), chan="cmx")
        attn_proj(0, wkb, "wk_stash", A2, AF.Exp, "EK")
        attn_proj(1, wvb, "wv_stash", B2, AF.Identity, "VV")
        for dc in range(DC):
            for tg in range(NTG):
                a = A2[:, 1 + dc * T1 + tg * TG:1 + dc * T1 + tg * TG + TG]
                b = B2[:, 1 + dc * T1 + tg * TG:1 + dc * T1 + tg * TG + TG]
                E("gpsimd", (lambda a, b: lambda g: g.tensor_tensor(out=b, in0=b, in1=a, op=ALU.mult))(a, b),
                  r=("A2_%d" % dc, "B2_%d" % dc), w=("B2_%d" % dc,))
        for dc in range(DC):
            E("vector", (lambda dc: lambda v: v.memset(A2[:, dc * T1:dc * T1 + 1], 0.0))(dc),
              r=("A2_%d" % dc,), w=("A2_%d" % dc,))
            E("vector", (lambda dc: lambda v: v.memset(B2[:, dc * T1:dc * T1 + 1], 0.0))(dc),
              r=("B2_%d" % dc,), w=("B2_%d" % dc,))
        for dc in range(DC):
            ek = A2[:, dc * T1:(dc + 1) * T1]
            av = B2[:, dc * T1:(dc + 1) * T1]
            bo = SC[:, 0:T1]
            ao = SC[:, T1:2 * T1]
            emwb = DCV[:, 8 + dc:9 + dc].broadcast_to([128, T1])
            E("vector", (lambda ek, bo, emwb: lambda v: v.tensor_tensor_scan(out=bo, data0=emwb, data1=ek, initial=0.0, op0=ALU.mult, op1=ALU.add))(ek, bo, emwb),
              r=("A2_%d" % dc, "DCV", "SC0", "SC1", "SC2", "CB0", "CB1", "CB2", "CB3", "CB4", "CB5", "CB6"), w=("SC0", "SC1", "SC2", "CB0", "CB1", "CB2", "CB3", "CB4", "CB5", "CB6"))
            E("vector", (lambda av, ao, emwb: lambda v: v.tensor_tensor_scan(out=ao, data0=emwb, data1=av, initial=0.0, op0=ALU.mult, op1=ALU.add))(av, ao, emwb),
              r=("B2_%d" % dc, "DCV", "SC0", "SC1", "SC2", "CB0", "CB1", "CB2", "CB3", "CB4", "CB5", "CB6"), w=("SC0", "SC1", "SC2", "CB0", "CB1", "CB2", "CB3", "CB4", "CB5", "CB6"))
            E("vector", (lambda bo, dc: lambda v: v.tensor_copy(out=MAXC[:, MX_SF + dc:MX_SF + dc + 1], in_=bo[:, TC:TC + 1]))(bo, dc),
              r=("SC0", "SC1", "SC2", "CB0", "CB1", "CB2", "CB3", "CB4", "CB5", "CB6", "MXSF"), w=("MXSF",))
            E("vector", (lambda ao, dc: lambda v: v.tensor_copy(out=MAXC[:, MX_SF + 8 + dc:MX_SF + 8 + dc + 1], in_=ao[:, TC:TC + 1]))(ao, dc),
              r=("SC0", "SC1", "SC2", "CB0", "CB1", "CB2", "CB3", "CB4", "CB5", "CB6", "MXSF"), w=("MXSF",))
            ED("gpsimd", (lambda bo, dc: lambda g: g.dma_start(out=scanbuf[dc * 128:(dc + 1) * 128, :], in_=bo))(bo, dc),
               r=("SC0", "SC1", "SC2", "CB0", "CB1", "CB2", "CB3", "CB4", "CB5", "CB6",), w=("SCB%d" % dc,), chan="csb0")
            ED("gpsimd", (lambda ao, dc: lambda g: g.dma_start(out=scanbuf[D + dc * 128:D + (dc + 1) * 128, :], in_=ao))(ao, dc),
               r=("SC0", "SC1", "SC2", "CB0", "CB1", "CB2", "CB3", "CB4", "CB5", "CB6",), w=("SCA%d" % dc,), chan="csb1")
        E("tensor", lambda t: t.transpose(PSG[0:16, 0:128], MAXC[:, MX_SF:MX_SF + 16], IDF[:]),
          r=("MXSF", "IDF", "PSG"), w=("PSG",))
        E("scalar", lambda s: s.copy(out=H16[:, :], in_=PSG[0:16, 0:128]),
          r=("PSG", "SQT"), w=("SQT",))
        E("tensor", lambda t: t.matmul(PSG[:, 0:128], SEL16[:, :], H16[:, :], start=True, stop=True),
          r=("SQT", "SEL16", "PSG"), w=("PSG",))
        E("vector", lambda v: v.tensor_copy(out=CCT[:], in_=PSG[:, 0:128]),
          r=("PSG", "SQT"), w=("SQT",))
        collective_add_tile("SQT", "hoff")
        E("tensor", lambda t: t.matmul(PSG[0:16, 0:128], SELR16[:, :], CCT[:], start=True, stop=True),
          r=("hoff", "SQT", "SELR16", "PSG"), w=("PSG",))
        E("scalar", lambda s: s.copy(out=H16[:, :], in_=PSG[0:16, 0:128]),
          r=("PSG", "SQT"), w=("SQT",))
        E("tensor", lambda t: t.transpose(PSG[:, 0:16], H16[:, :], IDF[0:16, 0:16]),
          r=("SQT", "IDF", "PSG"), w=("PSG",))
        E("vector", lambda v: v.tensor_copy(out=MAXC[:, MX_B0:MX_B0 + 16], in_=PSG[:, 0:16]),
          r=("PSG", "MXB0"), w=("MXB0",))
        for dc in range(DC):
            ek = A2[:, dc * T1:(dc + 1) * T1]
            av = B2[:, dc * T1:(dc + 1) * T1]
            bo = SC[:, 0:T1]
            ao = SC[:, T1:2 * T1]
            dec = SC[:, 2 * T1:3 * T1]
            emwb = DCV[:, 8 + dc:9 + dc].broadcast_to([128, T1])
            ED("sync", (lambda bo, dc: lambda g: g.dma_start(out=bo, in_=scanbuf[dc * 128:(dc + 1) * 128, :]))(bo, dc),
               r=("SCB%d" % dc, "SC0", "SC1", "SC2", "CB0", "CB1", "CB2", "CB3", "CB4", "CB5", "CB6"), w=("SC0", "SC1", "SC2", "CB0", "CB1", "CB2", "CB3", "CB4", "CB5", "CB6"), chan="csr0")
            ED("sync", (lambda ao, dc: lambda g: g.dma_start(out=ao, in_=scanbuf[D + dc * 128:D + (dc + 1) * 128, :]))(ao, dc),
               r=("SCA%d" % dc, "SC0", "SC1", "SC2", "CB0", "CB1", "CB2", "CB3", "CB4", "CB5", "CB6"), w=("SC0", "SC1", "SC2", "CB0", "CB1", "CB2", "CB3", "CB4", "CB5", "CB6"), chan="csr1")
            E("vector", (lambda dec: lambda v: v.memset(dec[:, 0:T1], 0.0))(dec), r=("SC0", "SC1", "SC2", "CB0", "CB1", "CB2", "CB3", "CB4", "CB5", "CB6",), w=("SC0", "SC1", "SC2", "CB0", "CB1", "CB2", "CB3", "CB4", "CB5", "CB6"))
            E("vector", (lambda dec: lambda v: v.memset(dec[:, 0:1], 1.0))(dec), r=("SC0", "SC1", "SC2", "CB0", "CB1", "CB2", "CB3", "CB4", "CB5", "CB6",), w=("SC0", "SC1", "SC2", "CB0", "CB1", "CB2", "CB3", "CB4", "CB5", "CB6"))
            E("vector", (lambda dec, emwb: lambda v: v.tensor_tensor_scan(out=dec, data0=emwb, data1=dec, initial=0.0, op0=ALU.mult, op1=ALU.add))(dec, emwb),
              r=("SC0", "SC1", "SC2", "CB0", "CB1", "CB2", "CB3", "CB4", "CB5", "CB6", "DCV"), w=("SC0", "SC1", "SC2", "CB0", "CB1", "CB2", "CB3", "CB4", "CB5", "CB6"))
            E("vector", (lambda bo, dec, dc: lambda v: v.scalar_tensor_tensor(out=bo, in0=dec, scalar=MAXC[:, MX_B0 + dc:MX_B0 + dc + 1], in1=bo, op0=ALU.mult, op1=ALU.add))(bo, dec, dc),
              r=("SC0", "SC1", "SC2", "CB0", "CB1", "CB2", "CB3", "CB4", "CB5", "CB6", "MXB0"), w=("SC0", "SC1", "SC2", "CB0", "CB1", "CB2", "CB3", "CB4", "CB5", "CB6"))
            E("vector", (lambda ao, dec, dc: lambda v: v.scalar_tensor_tensor(out=ao, in0=dec, scalar=MAXC[:, MX_B0 + 8 + dc:MX_B0 + 8 + dc + 1], in1=ao, op0=ALU.mult, op1=ALU.add))(ao, dec, dc),
              r=("SC0", "SC1", "SC2", "CB0", "CB1", "CB2", "CB3", "CB4", "CB5", "CB6", "MXB0"), w=("SC0", "SC1", "SC2", "CB0", "CB1", "CB2", "CB3", "CB4", "CB5", "CB6"))
            E("vector", (lambda ek, bo, dc: lambda v: v.scalar_tensor_tensor(out=ek[:, 1:T1], in0=ek[:, 1:T1], scalar=DCV[:, 0 + dc:1 + dc], in1=bo[:, 0:TC], op0=ALU.mult, op1=ALU.add))(ek, bo, dc),
              r=("A2_%d" % dc, "SC0", "SC1", "SC2", "CB0", "CB1", "CB2", "CB3", "CB4", "CB5", "CB6", "DCV"), w=("A2_%d" % dc,))
            E("vector", (lambda av, ao, dc: lambda v: v.scalar_tensor_tensor(out=av[:, 1:T1], in0=av[:, 1:T1], scalar=DCV[:, 0 + dc:1 + dc], in1=ao[:, 0:TC], op0=ALU.mult, op1=ALU.add))(av, ao, dc),
              r=("B2_%d" % dc, "SC0", "SC1", "SC2", "CB0", "CB1", "CB2", "CB3", "CB4", "CB5", "CB6", "DCV"), w=("B2_%d" % dc,))
            E("scalar", (lambda ek: lambda s: s.activation(out=ek[:, 1:T1], in_=ek[:, 1:T1], func=AF.Ln))(ek),
              r=("A2_%d" % dc,), w=("A2_%d" % dc,))
            E("scalar", (lambda ek: lambda s: s.activation(out=ek[:, 1:T1], in_=ek[:, 1:T1], func=AF.Exp, scale=-1.0))(ek),
              r=("A2_%d" % dc,), w=("A2_%d" % dc,))
            E("vector", (lambda ek, av: lambda v: v.tensor_tensor(out=av[:, 1:T1], in0=av[:, 1:T1], in1=ek[:, 1:T1], op=ALU.mult))(ek, av),
              r=("A2_%d" % dc, "B2_%d" % dc), w=("B2_%d" % dc,))
        attn_proj(2, wrb, "wr_stash", A2, AF.Sigmoid, "SR")
        for dc in range(DC):
            for tg in range(NTG):
                a = A2[:, 1 + dc * T1 + tg * TG:1 + dc * T1 + tg * TG + TG]
                b = B2[:, 1 + dc * T1 + tg * TG:1 + dc * T1 + tg * TG + TG]
                c = CB[:, dc * TC + tg * TG:dc * TC + tg * TG + TG]
                E("vector", (lambda a, b, c: lambda v: v.tensor_tensor(out=c, in0=b, in1=a, op=ALU.mult))(a, b, c),
                  r=("A2_%d" % dc, "B2_%d" % dc, "CB%d" % dc), w=("CB%d" % dc,))
        for dc in range(DC):
            ED("sync", (lambda dc: lambda g: g.dma_start(out=XT[:, dc * TC:(dc + 1) * TC], in_=xspill[dc * 128:(dc + 1) * 128, :]))(dc),
               r=("XSP%d" % dc, "XT%d" % dc), w=("XT%d" % dc,) + tuple("B2_%d" % q for q in range(DC)), chan="cxr")
        for oc in range(DC):
            sl = oc % 2
            ED("sync", (lambda oc, sl: lambda g: g.dma_start(out=LW[:, sl * 1024:(sl + 1) * 1024].rearrange("p (i j) -> p i j", j=128), in_=wob[:, oc * 128:(oc + 1) * 128].rearrange("(i p) j -> p i j", p=128)))(oc, sl),
               r=("LW%d" % sl, "wo_stash"), w=("LW%d" % sl,), chan="clw%d" % sl)
            for tg in range(NTG):
                bi = (oc * NTG + tg) % 2
                bank = PMM[bi]
                def mmo(oc=oc, tg=tg, sl=sl, bank=bank):
                    def f(t):
                        last = None
                        for ic in range(DC):
                            last = t.matmul(bank[:, 0:TG], LW[:, sl * 1024 + ic * 128:sl * 1024 + (ic + 1) * 128],
                                            CB[:, ic * TC + tg * TG:ic * TC + tg * TG + TG],
                                            start=(ic == 0), stop=(ic == DC - 1))
                        return last
                    return f
                E("tensor", mmo(), r=tuple(["LW%d" % sl] + ["CB%d" % ic for ic in range(DC)] + ["PMM%d" % bi]),
                  w=("PMM%d" % bi,))
                xsl = XT[:, oc * TC + tg * TG:oc * TC + tg * TG + TG]
                E("vector", (lambda bank, xsl: lambda v: v.tensor_tensor(out=xsl, in0=xsl, in1=bank[:, 0:TG], op=ALU.add))(bank, xsl),
                  r=("PMM%d" % bi, "XT%d" % oc), w=("XT%d" % oc,))

    def finish_early():
        for oc in range(DC):
            for tg in range(NTG):
                xsl = XT[:, oc * TC + tg * TG: oc * TC + tg * TG + TG]
                ED("gpsimd", (lambda oc, tg, xsl: lambda g: g.dma_start(out=out[oc * 128:(oc + 1) * 128, tg * TG:(tg + 1) * TG], in_=xsl))(oc, tg, xsl),
                   r=("XT%d" % oc,), w=("OUT%d_%d" % (oc, tg),), chan="cout")
        p.barrier_wait("gpsimd", [T("OUT%d_%d" % (oc, tg)) for oc in range(DC) for tg in range(NTG)])
        p.drain_all("gpsimd")
        for ch in sorted(p.chans):
            sems[ch] = stack.enter_context(nc.semaphore("sd_" + ch))
        if "ccol" in p.cnt:
            sems["ccol"] = stack.enter_context(nc.semaphore("s_ccol"))
        with nc.Block() as block:
            p.realize(block, sems)
        stack.close()
        return nc

    if STAGE <= 0:
        return finish_early()
    ffn_phase(1)
    if DBG:
        pass
    if DBG:
        for dc in range(DC):
            ED("gpsimd", (lambda dc: lambda g: g.dma_start(out=dbg1[dc * 128:(dc + 1) * 128, :], in_=XT[:, dc * TC:(dc + 1) * TC]))(dc),
               r=("XT%d" % dc,), w=("DBG1_%d" % dc,), chan="cdbg")
    if STAGE <= 1:
        return finish_early()
    attention()
    if DBG:
        for dc in range(DC):
            ED("gpsimd", (lambda dc: lambda g: g.dma_start(out=dbg5[dc * 128:(dc + 1) * 128, :], in_=XT[:, dc * TC:(dc + 1) * TC]))(dc),
               r=("XT%d" % dc,), w=("DBG5_%d" % dc,), chan="cdbg")
    if STAGE <= 2:
        return finish_early()
    ffn_phase(2)
    p.barrier_wait("gpsimd", [T("OUT%d_%d" % (oc, tg)) for oc in range(DC) for tg in range(NTG)])
    p.drain_all("gpsimd")

    for ch in sorted(p.chans):
        sems[ch] = stack.enter_context(nc.semaphore("sd_" + ch))
    if "ccol" in p.cnt:
        sems["ccol"] = stack.enter_context(nc.semaphore("s_ccol"))
    with nc.Block() as block:
        p.realize(block, sems)
    stack.close()
    return nc


def kernel(L0_attn_x, L0_attn_time_decay, L0_attn_time_first, L0_attn_time_mix_k,
           L0_attn_time_mix_v, L0_attn_time_mix_r, L0_attn_W_key, L0_attn_W_value,
           L0_attn_W_receptance, L0_attn_W_output, L0_ffn_w_gate, L0_ffn_w_up, L0_ffn_w_down):
    x = np.asarray(L0_attn_x, np.float32)
    B, Ts, _ = x.shape
    xt_full = np.ascontiguousarray(x.reshape(B * Ts, D).T)
    wkT = np.ascontiguousarray(np.asarray(L0_attn_W_key, np.float32).T)
    wvT = np.ascontiguousarray(np.asarray(L0_attn_W_value, np.float32).T)
    wrT = np.ascontiguousarray(np.asarray(L0_attn_W_receptance, np.float32).T)
    woT = np.ascontiguousarray(np.asarray(L0_attn_W_output, np.float32).T)
    wupN = np.ascontiguousarray(np.asarray(L0_ffn_w_up, np.float32))
    wgN = np.ascontiguousarray(np.asarray(L0_ffn_w_gate, np.float32))
    wdnN = np.ascontiguousarray(np.asarray(L0_ffn_w_down, np.float32))
    tdec = np.ascontiguousarray(np.asarray(L0_attn_time_decay, np.float32).reshape(1, D))
    tfirst = np.ascontiguousarray(np.asarray(L0_attn_time_first, np.float32).reshape(1, D))
    tmk = np.ascontiguousarray(np.asarray(L0_attn_time_mix_k, np.float32).reshape(1, D))
    tmv = np.ascontiguousarray(np.asarray(L0_attn_time_mix_v, np.float32).reshape(1, D))
    tmr = np.ascontiguousarray(np.asarray(L0_attn_time_mix_r, np.float32).reshape(1, D))
    idn = np.eye(128, dtype=np.float32)

    in_maps = []
    for c in range(NC):
        selw8 = np.zeros((8, 128), np.float32)
        for q in range(8):
            selw8[q, 8 * c + q] = 1.0
        selr8 = np.zeros((128, 8), np.float32)
        selw16 = np.zeros((16, 128), np.float32)
        for q in range(16):
            selw16[q, 16 * c + q] = 1.0
        selr16 = np.zeros((128, 16), np.float32)
        if c % 2 == 1:
            for q in range(8):
                selr8[8 * (c - 1) + q, q] = 1.0
            for q in range(16):
                selr16[16 * (c - 1) + q, q] = 1.0
        in_maps.append({
            "xt": np.ascontiguousarray(xt_full[:, c * TC:(c + 1) * TC]),
            "wk": wkT, "wv": wvT, "wr": wrT, "wo": woT,
            "wup": wupN, "wg": wgN, "wdn": wdnN,
            "tdec": tdec, "tfirst": tfirst, "tmk": tmk, "tmv": tmv, "tmr": tmr,
            "idn": idn, "selw8": selw8, "selr8": selr8,
            "selw16": selw16, "selr16": selr16,
        })
    nc = build_program()
    res = run_bass_kernel_spmd(nc, in_maps, core_ids=list(range(NC)))
    outs = [res.results[c]["out"] for c in range(NC)]
    full = np.concatenate(outs, axis=1)
    return np.ascontiguousarray(full.T).reshape(B, Ts, D).astype(np.float32)


# version 05 (final)
# speedup vs baseline: 1.5251x; 1.5251x over previous
import numpy as np
import contextlib
import concourse.bass as bass
import concourse.mybir as mybir
from concourse.bass_utils import run_bass_kernel_spmd

F32 = mybir.dt.float32
BF16 = mybir.dt.bfloat16
ALU = mybir.AluOpType
AF = mybir.ActivationFunctionType
AX = mybir.AxisListType

NC = 8
D = 1024
DC = 8
I3 = 3072
IC = 24
TC = 2048
TG = 512
NTG = TC // TG
T1 = TC + 1
CM = 12582912.0
QMAX = 32767.0
INV_QMAX = float(np.float32(1.0) / np.float32(32767.0))
INV7 = float(np.float32(1.0) / np.float32(7.0))
INV_D = float(np.float32(1.0) / np.float32(1024.0))
LNEPS = 1e-5
EPS12 = 1e-12
K65536 = 65536.0
INV65536 = float(np.float32(1.0) / np.float32(65536.0))

ENGS = ("sync", "vector", "scalar", "tensor", "gpsimd")
DBG = False
STAGE = 99
SUB = 99
NJ = 99
SKIP_EVAC = False
SKIP_STASH = False
P4MODE = 0
DQN = 99
DQDN = True
DQSLOT2 = True
DQTRUNC = 99


class Tok:
    __slots__ = ("w", "r")
    def __init__(self):
        self.w = None
        self.r = []


class Plan:
    def __init__(self):
        self.cnt = {}
        self.ops = {e: [] for e in ENGS}
        self.marks = {e: {} for e in ENGS}
        self.chans = set()
        for e in ENGS:
            self.cnt[e] = 0

    def emit(self, eng, fn, reads, writes, dma, chan=None, sem_override=None):
        if dma:
            sem = chan
            self.chans.add(chan)
            if sem not in self.cnt:
                self.cnt[sem] = 0
        elif sem_override is not None:
            sem = sem_override
            if sem not in self.cnt:
                self.cnt[sem] = 0
        else:
            sem = eng
        step = 16 if dma else 1
        waits = []
        marks = self.marks[eng]
        def need(sig):
            if sig is None:
                return
            s, v = sig
            if s in self.chans:
                v = self.cnt[s]
            if marks.get(s, 0) < v:
                waits.append((s, v))
                marks[s] = v
        for t in reads:
            need(t.w)
        for t in writes:
            need(t.w)
            for sig in t.r:
                need(sig)
        self.cnt[sem] += step
        sig_new = (sem, self.cnt[sem])
        for t in reads:
            t.r.append(sig_new)
            if len(t.r) > 12:
                t.r = t.r[-12:]
        for t in writes:
            t.w = sig_new
            t.r = []
        self.ops[eng].append((waits, fn, sem, step))

    def barrier_wait(self, eng, reads):
        waits = []
        marks = self.marks[eng]
        for t in reads:
            if t.w is None:
                continue
            s, v = t.w
            if s in self.chans:
                v = self.cnt[s]
            if marks.get(s, 0) < v:
                waits.append((s, v))
                marks[s] = v
        if waits:
            self.ops[eng].append((waits, None, eng, 1))

    def drain_all(self, eng):
        waits = []
        marks = self.marks[eng]
        for s in sorted(self.chans):
            v = self.cnt[s]
            if v > 0 and marks.get(s, 0) < v:
                waits.append((s, v))
                marks[s] = v
        if "ccol" in self.cnt and marks.get("ccol", 0) < self.cnt["ccol"]:
            waits.append(("ccol", self.cnt["ccol"]))
            marks["ccol"] = self.cnt["ccol"]
        if waits:
            self.ops[eng].append((waits, None, eng, 1))

    def realize(self, block, sems):
        def mk(e):
            def run(engh):
                for waits, fn, sem, step in self.ops[e]:
                    for s, v in waits:
                        engh.wait_ge(sems[s], v)
                    if fn is not None:
                        inst = fn(engh)
                        inst.then_inc(sems[sem], step)
            return run
        for e in ENGS:
            getattr(block, e)(mk(e))


def build_program():
    nc = bass.Bass("TRN2", target_bir_lowering=False)
    dtn = nc.dram_tensor
    xt = dtn("xt", [D, TC], F32, kind="ExternalInput").ap()
    wk = dtn("wk", [DC, 128, D], F32, kind="ExternalInput").ap()
    wv = dtn("wv", [DC, 128, D], F32, kind="ExternalInput").ap()
    wr = dtn("wr", [DC, 128, D], F32, kind="ExternalInput").ap()
    wo = dtn("wo", [DC, 128, D], F32, kind="ExternalInput").ap()
    wup = dtn("wup", [I3, D], F32, kind="ExternalInput").ap()
    wg = dtn("wg", [I3, D], F32, kind="ExternalInput").ap()
    wdn = dtn("wdn", [D, I3], F32, kind="ExternalInput").ap()
    tdec = dtn("tdec", [128, DC], F32, kind="ExternalInput").ap()
    tfirst = dtn("tfirst", [128, DC], F32, kind="ExternalInput").ap()
    tmk = dtn("tmk", [128, DC], F32, kind="ExternalInput").ap()
    tmv = dtn("tmv", [128, DC], F32, kind="ExternalInput").ap()
    tmr = dtn("tmr", [128, DC], F32, kind="ExternalInput").ap()
    idn = dtn("idn", [128, 128], F32, kind="ExternalInput").ap()
    selw8 = dtn("selw8", [8, 128], F32, kind="ExternalInput").ap()
    selr8 = dtn("selr8", [128, 8], F32, kind="ExternalInput").ap()
    selw16 = dtn("selw16", [16, 128], F32, kind="ExternalInput").ap()
    selr16 = dtn("selr16", [128, 16], F32, kind="ExternalInput").ap()
    out = dtn("out", [D, TC], F32, kind="ExternalOutput").ap()
    if DBG:
        dbg1 = dtn("dbg1", [D, TC], F32, kind="ExternalOutput").ap()
        dbg2 = dtn("dbg2", [D, TC], F32, kind="ExternalOutput").ap()
        dbg3 = dtn("dbg3", [D, TC], F32, kind="ExternalOutput").ap()
        dbg4 = dtn("dbg4", [128, 512], F32, kind="ExternalOutput").ap()
        dbg5 = dtn("dbg5", [D, TC], F32, kind="ExternalOutput").ap()
    upbuf = dtn("upbuf", [I3, TC], F32).ap()
    gbuf = dtn("gbuf", [I3, TC], F32).ap()
    t2buf = dtn("t2buf", [I3, TC], F32).ap()
    wupT = dtn("wupT", [IC, 128, D], BF16).ap()
    wgT = dtn("wgT", [IC, 128, D], BF16).ap()
    wdnT = dtn("wdnT", [DC, 128, I3], BF16).ap()
    xspill = dtn("xspill", [D, TC], F32).ap()
    mixbuf = dtn("mixbuf", [3, D, TC], BF16).ap()
    scanbuf = dtn("scanbuf", [2 * D, T1], F32).ap()
    cc_in = dtn("cc_in", [128, 128], F32).ap()
    cc_out = dtn("cc_out", [128, 128], F32).ap()

    p = Plan()
    stack = contextlib.ExitStack()
    sems = {}
    for e in ENGS:
        sems[e] = stack.enter_context(nc.semaphore("s_" + e))
    sb = lambda name, shp, dtp: stack.enter_context(nc.sbuf_tensor(name, shp, dtp))
    psal = lambda name, shp, dtp: stack.enter_context(nc.psum_tensor(name, shp, dtp))

    XTB = sb("XTB", [128, DC * T1], F32)
    XT = XTB[:, 0:DC * TC]
    B2 = XTB[:, :]
    A2 = sb("A2", [128, DC * T1], F32)
    CB = sb("CB", [128, DC * TC], BF16)
    SC = CB.ap().bitcast(F32)[:, 0:3 * T1]
    MT = sb("MT", [128, 4 * TG], F32)
    US = sb("US", [128, 4 * TG], F32)
    SQT = sb("SQT", [128, TG], F32)
    WB = sb("WB", [128, 2 * D], F32)
    LW = sb("LW", [128, 4 * 1024], BF16)
    LWP = sb("LWP", [128, 2 * 1024], BF16)
    DQS = sb("DQS", [128, 256], F32)
    MAXC = sb("MAXC", [128, 240], F32)
    COLS = sb("COLS", [128, 16], F32)
    DCV = sb("DCV", [128, 64], F32)
    ONEC = sb("ONEC", [128, 2], F32)
    ONER = sb("ONER", [1, 128], F32)
    IDF = sb("IDF", [128, 128], F32)
    IDB = sb("IDB", [128, 128], BF16)
    SEL8 = sb("SEL8", [8, 128], F32)
    SELR8 = sb("SELR8", [128, 8], F32)
    SEL16 = sb("SEL16", [16, 128], F32)
    SELR16 = sb("SELR16", [128, 16], F32)
    ROWS = US[0:1, 0:4 * TG]
    SROW = sb("SROW", [1, 64], F32)
    CCT = SQT[:, 0:128]
    XR2 = SQT[0:8, 128:256]
    XRS = SQT[0:8, 256:384]
    H16 = SQT[0:16, 384:512]

    PMM = [psal("PMM%d" % i, [128, TG], F32) for i in range(4)]
    PBC = [psal("PBC%d" % i, [128, TG], F32) for i in range(2)]
    PTR = psal("PTR", [128, 1024], BF16)
    PSG = psal("PSG", [128, 128], F32)

    MX_A = 0
    MX_B = 96
    MX_F = 192
    MX_NT = 193
    MX_NR = 195
    MX_P = 197
    MX_BX = 200
    MX_SF = 208
    MX_B0 = 224

    tks = {}
    def T(name):
        if name not in tks:
            tks[name] = Tok()
        return tks[name]

    def E(eng, fn, r=(), w=()):
        p.emit(eng, fn, [T(x) for x in r], [T(x) for x in w], False)

    def ED(eng, fn, r=(), w=(), chan="c0"):
        p.emit(eng, fn, [T(x) for x in r], [T(x) for x in w], True, chan=chan)

    for dc in range(DC):
        ED("sync", (lambda dc: lambda g: g.dma_start(out=XT[:, dc * TC:(dc + 1) * TC], in_=xt[dc * 128:(dc + 1) * 128, :]))(dc),
           w=("XT%d" % dc,), chan="cin")
    ED("sync", lambda g: g.dma_start(out=IDF[:], in_=idn), w=("IDF",), chan="cin")
    ED("sync", lambda g: g.dma_start(out=SEL8[:], in_=selw8), w=("SEL8",), chan="cin")
    ED("sync", lambda g: g.dma_start(out=SELR8[:], in_=selr8), w=("SELR8",), chan="cin")
    ED("sync", lambda g: g.dma_start(out=SEL16[:], in_=selw16), w=("SEL16",), chan="cin")
    ED("sync", lambda g: g.dma_start(out=SELR16[:], in_=selr16), w=("SELR16",), chan="cin")
    for src, cb in ((tmk, 16), (tmv, 24), (tmr, 32), (tdec, 40), (tfirst, 48)):
        ED("sync", (lambda src, cb: lambda g: g.dma_start(out=DCV[:, cb:cb + DC], in_=src))(src, cb),
           r=("DCV",), w=("DCV",), chan="cin")
    E("vector", lambda v: v.memset(ONEC[:], 1.0), w=("ONEC",))
    E("vector", lambda v: v.memset(ONER[:], 1.0), w=("ONER",))
    E("vector", lambda v: v.memset(COLS[:, 15:16], -CM), r=("COLS",), w=("COLS",))
    E("vector", lambda v: v.tensor_copy(out=IDB[:], in_=IDF[:]), r=("IDF",), w=("IDB",))
    E("vector", lambda v: v.memset(CCT[:], 0.0), w=("SQT",))
    ED("gpsimd", lambda g: g.dma_start(out=cc_in, in_=CCT[:]), r=("SQT",), w=("CCIN",), chan="ccg")

    E("scalar", lambda s: s.activation(out=DCV[:, 0:8], in_=DCV[:, 48:56], func=AF.Exp),
      r=("DCV",), w=("DCV",))
    E("scalar", lambda s: s.activation(out=DCV[:, 56:64], in_=DCV[:, 40:48], func=AF.Exp),
      r=("DCV",), w=("DCV",))
    E("scalar", lambda s: s.activation(out=DCV[:, 8:16], in_=DCV[:, 56:64], func=AF.Exp, scale=-1.0),
      r=("DCV",), w=("DCV",))

    def ln_stats_and_apply(tag, maxtrack):
        for tg in range(NTG):
            for dc in range(DC):
                sl = XT[:, dc * TC + tg * TG: dc * TC + tg * TG + TG]
                E("scalar", (lambda sl: lambda s: s.activation(out=SQT[:], in_=sl, func=AF.Square))(sl),
                  r=("XT%d" % dc, "SQT"), w=("SQT",))
                E("tensor", (lambda sl, dc: lambda t: t.matmul(PBC[0][0:1, 0:TG], ONEC[:, 0:1], sl, start=(dc == 0), stop=(dc == DC - 1)))(sl, dc),
                  r=("XT%d" % dc, "ONEC", "PBC0"), w=("PBC0",))
                E("tensor", (lambda dc: lambda t: t.matmul(PBC[1][0:1, 0:TG], ONEC[:, 0:1], SQT[:], start=(dc == 0), stop=(dc == DC - 1)))(dc),
                  r=("SQT", "ONEC", "PBC1"), w=("PBC1",))
            mrow = ROWS[0:1, 0:TG]
            arow = ROWS[0:1, TG:2 * TG]
            vr1 = ROWS[0:1, 2 * TG:3 * TG]
            vr2 = ROWS[0:1, 3 * TG:4 * TG]
            E("vector", (lambda mrow: lambda v: v.tensor_scalar(out=mrow, in0=PBC[0][0:1, 0:TG], scalar1=INV_D, scalar2=None, op0=ALU.mult))(mrow),
              r=("PBC0",), w=("ROWS",))
            E("vector", (lambda vr1: lambda v: v.tensor_scalar(out=vr1, in0=PBC[1][0:1, 0:TG], scalar1=INV_D, scalar2=LNEPS, op0=ALU.mult, op1=ALU.add))(vr1),
              r=("PBC1",), w=("ROWS",))
            E("vector", (lambda mrow, vr2: lambda v: v.tensor_tensor(out=vr2, in0=mrow, in1=mrow, op=ALU.mult))(mrow, vr2),
              r=("ROWS",), w=("ROWS",))
            E("vector", (lambda vr1, vr2: lambda v: v.tensor_tensor(out=vr1, in0=vr1, in1=vr2, op=ALU.subtract))(vr1, vr2),
              r=("ROWS", "ROWS"), w=("ROWS",))
            E("scalar", (lambda vr1, vr2: lambda s: s.activation(out=vr2, in_=vr1, func=AF.Ln))(vr1, vr2),
              r=("ROWS",), w=("ROWS",))
            E("scalar", (lambda arow, vr2: lambda s: s.activation(out=arow, in_=vr2, func=AF.Exp, scale=-0.5))(arow, vr2),
              r=("ROWS",), w=("ROWS",))
            E("vector", (lambda mrow, arow: lambda v: v.scalar_tensor_tensor(out=mrow, in0=mrow, scalar=-1.0, in1=arow, op0=ALU.mult, op1=ALU.mult))(mrow, arow),
              r=("ROWS", "ROWS"), w=("ROWS",))
            E("tensor", (lambda arow: lambda t: t.matmul(PBC[0][:, 0:TG], ONER[0:1, :], arow, start=True, stop=True))(arow),
              r=("ROWS", "ONER", "PBC0"), w=("PBC0",))
            E("tensor", (lambda mrow: lambda t: t.matmul(PBC[1][:, 0:TG], ONER[0:1, :], mrow, start=True, stop=True))(mrow),
              r=("ROWS", "ONER", "PBC1"), w=("PBC1",))
            for dc in range(DC):
                srcsl = XT[:, dc * TC + tg * TG: dc * TC + tg * TG + TG]
                dstsl = A2[:, 1 + dc * T1 + tg * TG: 1 + dc * T1 + tg * TG + TG]
                E("vector", (lambda srcsl, dstsl: lambda v: v.tensor_tensor(out=dstsl, in0=srcsl, in1=PBC[0][:, 0:TG], op=ALU.mult))(srcsl, dstsl),
                  r=("XT%d" % dc, "PBC0", "A2_%d" % dc), w=("A2_%d" % dc,))
                E("vector", (lambda dstsl: lambda v: v.tensor_tensor(out=dstsl, in0=dstsl, in1=PBC[1][:, 0:TG], op=ALU.add))(dstsl),
                  r=("A2_%d" % dc, "PBC1"), w=("A2_%d" % dc,))
                if maxtrack:
                    cix = MX_A + tg * DC + dc
                    E("vector", (lambda dstsl, cix: lambda v: v.tensor_reduce(out=MAXC[:, cix:cix + 1], in_=dstsl, axis=AX.X, op=ALU.max, apply_absolute_value=True))(dstsl, cix),
                      r=("A2_%d" % dc,), w=("MXA",))

    def part_reduce_to_scalar(coltok, srccol, outslot, tagout):
        E("tensor", (lambda srccol: lambda t: t.transpose(PSG[0:1, 0:128], MAXC[:, srccol:srccol + 1], IDF[:]))(srccol),
          r=(coltok, "IDF", "PSG"), w=("PSG",))
        E("vector", (lambda outslot: lambda v: v.tensor_reduce(out=SROW[0:1, outslot:outslot + 1], in_=PSG[0:1, 0:128], axis=AX.X, op=ALU.max))(outslot),
          r=("PSG", "SROW"), w=("SROW", tagout))

    def collective_max(slot0, nsl, rtags, tagout):
        ED("gpsimd", (lambda slot0, nsl: lambda g: g.dma_start(out=cc_in[0:1, 0:nsl], in_=SROW[0:1, slot0:slot0 + nsl]))(slot0, nsl),
           r=rtags + ("SROW",), w=("CCIN",), chan="ccg")
        p.emit("gpsimd", lambda g: g.collective_compute("AllReduce", ALU.max, replica_groups=[list(range(NC))],
                                                        ins=[cc_in.opt()], outs=[cc_out.opt()]),
               [T("CCIN"), T("CCOUT")], [T("CCOUT")], False, sem_override="ccol")
        ED("sync", (lambda slot0, nsl: lambda g: g.dma_start(out=SROW[0:1, slot0:slot0 + nsl], in_=cc_out[0:1, 0:nsl]))(slot0, nsl),
           r=("CCOUT", "SROW"), w=("SROW", tagout), chan="cci")

    def collective_add_tile(rtag, tagout):
        ED("gpsimd", lambda g: g.dma_start(out=cc_in, in_=CCT[:]),
           r=(rtag, "SQT"), w=("CCIN",), chan="ccg")
        p.emit("gpsimd", lambda g: g.collective_compute("AllReduce", ALU.add, replica_groups=[list(range(NC))],
                                                        ins=[cc_in.opt()], outs=[cc_out.opt()]),
               [T("CCIN"), T("CCOUT")], [T("CCOUT")], False, sem_override="ccol")
        ED("sync", lambda g: g.dma_start(out=CCT[:], in_=cc_out),
           r=("CCOUT", "SQT"), w=("SQT", tagout), chan="cci")

    def bcast_slots(slot0, n, colbase, rtags):
        E("tensor", (lambda slot0, n: lambda t: t.matmul(PSG[:, 0:n], ONER[0:1, :], SROW[0:1, slot0:slot0 + n], start=True, stop=True))(slot0, n),
          r=rtags + ("SROW", "ONER", "PSG"), w=("PSG",))
        E("vector", (lambda colbase, n: lambda v: v.tensor_copy(out=COLS[:, colbase:colbase + n], in_=PSG[:, 0:n]))(colbase, n),
          r=("PSG", "COLS"), w=("COLS",))

    def sc_op(fn, r, w):
        E("vector", fn, r=r + ("SROW",), w=w + ("SROW",))

    def dq_core(sl):
        wsl = WB[:, sl * D:(sl + 1) * D]
        wtok = "WB%d" % sl
        bm = DQS[:, sl * 128:sl * 128 + 32]
        sbv = DQS[:, sl * 128 + 64:sl * 128 + 96]
        dtok = "DQS%d" % sl
        ptok = "LWP%d" % sl
        E("vector", (lambda wsl, bm: lambda v: v.tensor_reduce(out=bm, in_=wsl.rearrange("p (b c) -> p b c", c=32), axis=AX.X, op=ALU.max, apply_absolute_value=True))(wsl, bm),
          r=(wtok, dtok), w=(dtok,))
        E("vector", (lambda bm, sbv: lambda v: v.tensor_scalar(out=sbv, in0=bm, scalar1=INV7, scalar2=EPS12, op0=ALU.mult, op1=ALU.max))(bm, sbv),
          r=(dtok,), w=(dtok,))
        E("vector", (lambda bm, sbv: lambda v: v.reciprocal(out=bm, in_=sbv))(bm, sbv),
          r=(dtok,), w=(dtok,))
        E("vector", (lambda wsl, bm: lambda v: v.tensor_tensor(
            out=wsl.rearrange("p (b c) -> p b c", c=32), in0=wsl.rearrange("p (b c) -> p b c", c=32),
            in1=bm.unsqueeze(2).broadcast_to([128, 32, 32]), op=ALU.mult))(wsl, bm),
          r=(dtok, wtok), w=(wtok,))
        E("vector", (lambda wsl: lambda v: v.tensor_scalar(out=wsl, in0=wsl, scalar1=CM, scalar2=CM, op0=ALU.add, op1=ALU.subtract))(wsl),
          r=(wtok,), w=(wtok,))
        E("vector", (lambda wsl, sbv, sl: lambda v: v.tensor_tensor(
            out=LWP[:, sl * 1024:(sl + 1) * 1024].rearrange("p (b c) -> p b c", c=32),
            in0=wsl.rearrange("p (b c) -> p b c", c=32),
            in1=sbv.unsqueeze(2).broadcast_to([128, 32, 32]), op=ALU.mult))(wsl, sbv, sl),
          r=(wtok, dtok, ptok), w=(ptok,))
        for ic in range(DC):
            E("tensor", (lambda sl, ic: lambda t: t.transpose(PTR[:, ic * 128:(ic + 1) * 128], LWP[:, sl * 1024 + ic * 128:sl * 1024 + (ic + 1) * 128], IDB[:]))(sl, ic),
              r=(ptok, "IDB", "PTR"), w=("PTR",))

    def dq_chunk(oc, sl, wdst, stash_tag_pfx, lw_slot):
        dq_core(sl)
        ltok = "LW%d" % lw_slot
        E("scalar", (lambda lw_slot: lambda s: s.copy(out=LW[:, lw_slot * 1024:(lw_slot + 1) * 1024], in_=PTR[:, 0:1024]))(lw_slot),
          r=("PTR", ltok), w=(ltok,))
        if SKIP_STASH:
            return
        ED("gpsimd", (lambda wdst, oc, lw_slot: lambda g: g.dma_start(out=wdst[oc], in_=LW[:, lw_slot * 1024:(lw_slot + 1) * 1024]))(wdst, oc, lw_slot),
           r=(ltok,), w=("%sT%d" % (stash_tag_pfx, oc),), chan="cst")

    def dq_dn_seg(s):
        oc, seg = s // 3, s % 3
        sl = s % 2
        ED("sync", (lambda oc, seg, sl: lambda g: g.dma_start(out=WB[:, sl * D:(sl + 1) * D], in_=wdn[oc * 128:(oc + 1) * 128, seg * 1024:(seg + 1) * 1024]))(oc, seg, sl),
           r=("WB%d" % sl,), w=("WB%d" % sl,), chan="cwb%d" % sl)
        dq_core(sl)
        ptok = "LWP%d" % sl
        E("scalar", (lambda sl: lambda s_: s_.copy(out=LWP[:, sl * 1024:(sl + 1) * 1024], in_=PTR[:, 0:1024]))(sl),
          r=("PTR", ptok), w=(ptok,))
        ED("gpsimd", (lambda oc, seg, sl: lambda g: g.dma_start(out=wdnT[oc][:, seg * 1024:(seg + 1) * 1024], in_=LWP[:, sl * 1024:(sl + 1) * 1024]))(oc, seg, sl),
           r=(ptok,), w=("wdT%d" % oc,), chan="cst")

    def ffn_phase(phase):
        pf = "F%d" % phase
        ln_stats_and_apply(pf, True)
        if phase == 1 and SUB <= 0:
            return
        E("vector", lambda v: v.tensor_reduce(out=MAXC[:, MX_F:MX_F + 1], in_=MAXC[:, MX_A:MX_A + 32], axis=AX.X, op=ALU.max),
          r=("MXA",), w=("MXF",))
        part_reduce_to_scalar("MXF", MX_F, 0, pf + "gx")
        collective_max(0, 1, (pf + "gx",), pf + "gxg")
        sc_op(lambda v: v.tensor_scalar(out=SROW[0:1, 1:2], in0=SROW[0:1, 0:1], scalar1=INV_QMAX, scalar2=EPS12, op0=ALU.mult, op1=ALU.max),
              r=(pf + "gxg",), w=(pf + "sx",))
        sc_op(lambda v: v.reciprocal(out=SROW[0:1, 2:3], in_=SROW[0:1, 1:2]),
              r=(pf + "sx",), w=(pf + "rx",))
        bcast_slots(1, 2, 0, (pf + "sx", pf + "rx"))
        if phase == 1 and SUB <= 1:
            return
        for dc in range(DC):
            for tg in range(NTG):
                asl = A2[:, 1 + dc * T1 + tg * TG: 1 + dc * T1 + tg * TG + TG]
                csl = CB[:, dc * TC + tg * TG: dc * TC + tg * TG + TG]
                E("vector", (lambda asl: lambda v: v.tensor_scalar(out=asl, in0=asl, scalar1=COLS[:, 1:2], scalar2=CM, op0=ALU.mult, op1=ALU.add))(asl),
                  r=("A2_%d" % dc, "COLS"), w=("A2_%d" % dc,))
                E("vector", (lambda asl, csl: lambda v: v.tensor_scalar(out=csl, in0=asl, scalar1=CM, scalar2=COLS[:, 0:1], op0=ALU.subtract, op1=ALU.mult))(asl, csl),
                  r=("A2_%d" % dc, "COLS", "CB%d" % dc), w=("CB%d" % dc,))
        if phase == 1 and SUB <= 2:
            return
        for j in range(IC):
            if phase == 1 and j >= NJ:
                continue
            for widx, (stash_pfx, wsrc, wdst, dbuf) in enumerate((("wu", wup, wupT, upbuf), ("wgx", wg, wgT, gbuf))):
                mxb = MX_A if widx == 0 else MX_B
                mxtok = "MXA" if widx == 0 else "MXB"
                lw_slot = widx * 2 + (j % 2)
                ltok = "LW%d" % lw_slot
                if phase == 1:
                    sl = (2 * j + widx) % 2
                    ED("sync", (lambda wsrc, j, sl: lambda g: g.dma_start(out=WB[:, sl * D:(sl + 1) * D], in_=wsrc[j * 128:(j + 1) * 128, :]))(wsrc, j, sl),
                       r=("WB%d" % sl,), w=("WB%d" % sl,), chan="cwb%d" % sl)
                    dq_chunk(j, sl, wdst, stash_pfx, lw_slot)
                else:
                    ED("sync", (lambda wdst, j, lw_slot: lambda g: g.dma_start(out=LW[:, lw_slot * 1024:(lw_slot + 1) * 1024], in_=wdst[j]))(wdst, j, lw_slot),
                       r=(ltok, "%sT%d" % (stash_pfx, j)), w=(ltok,), chan="clw%d" % (lw_slot % 2))
                for tg in range(NTG):
                    bi = tg % 2 + (0 if widx == 0 else 2)
                    bank = PMM[bi]
                    def mmfn(lw_slot=lw_slot, tg=tg, bank=bank):
                        def f(t):
                            last = None
                            for ic in range(DC):
                                last = t.matmul(bank[:, 0:TG], LW[:, lw_slot * 1024 + ic * 128: lw_slot * 1024 + (ic + 1) * 128],
                                                CB[:, ic * TC + tg * TG: ic * TC + tg * TG + TG],
                                                start=(ic == 0), stop=(ic == DC - 1))
                            return last
                        return f
                    E("tensor", mmfn(), r=tuple([ltok] + ["CB%d" % ic for ic in range(DC)] + ["PMM%d" % bi]),
                      w=("PMM%d" % bi,))
                    ust = bi
                    E("scalar", (lambda bank, ust: lambda s: s.copy(out=US[:, ust * TG:(ust + 1) * TG], in_=bank[:, 0:TG]))(bank, ust),
                      r=("PMM%d" % bi, "US%d" % ust), w=("US%d" % ust,))
                    cix = mxb + j * NTG + tg
                    E("vector", (lambda ust, cix: lambda v: v.tensor_reduce(out=MAXC[:, cix:cix + 1], in_=US[:, ust * TG:(ust + 1) * TG], axis=AX.X, op=ALU.max, apply_absolute_value=True))(ust, cix),
                      r=("US%d" % ust,), w=(mxtok,))
                    ED("gpsimd", (lambda dbuf, j, tg, ust: lambda g: g.dma_start(
                        out=dbuf[j * 128:(j + 1) * 128, tg * TG:(tg + 1) * TG], in_=US[:, ust * TG:(ust + 1) * TG]))(dbuf, j, tg, ust),
                       r=("US%d" % ust,), w=(pf + "UB%d_%d_%d" % (widx, j, tg),), chan="cus%d" % ust)
        for widx in range(2):
            mxb = MX_A if widx == 0 else MX_B
            mxtok = "MXA" if widx == 0 else "MXB"
            E("vector", (lambda mxb, widx: lambda v: v.tensor_reduce(out=MAXC[:, MX_F + widx:MX_F + widx + 1], in_=MAXC[:, mxb:mxb + 96], axis=AX.X, op=ALU.max))(mxb, widx),
              r=(mxtok,), w=("MXF",))
            part_reduce_to_scalar("MXF", MX_F + widx, 8 + widx, pf + "gm%d" % widx)
        if phase == 1 and SUB <= 4:
            return
        collective_max(8, 2, (pf + "gm0", pf + "gm1"), pf + "gmg")
        sc_op(lambda v: v.tensor_scalar(out=SROW[0:1, 10:12], in0=SROW[0:1, 8:10], scalar1=INV_QMAX, scalar2=EPS12, op0=ALU.mult, op1=ALU.max),
              r=(pf + "gmg",), w=(pf + "sug",))
        sc_op(lambda v: v.reciprocal(out=SROW[0:1, 12:14], in_=SROW[0:1, 10:12]),
              r=(pf + "sug",), w=(pf + "rug",))
        sc_op(lambda v: v.tensor_scalar(out=SROW[0:1, 14:15], in0=SROW[0:1, 11:12], scalar1=INV65536, scalar2=None, op0=ALU.mult),
              r=(pf + "sug",), w=(pf + "kap",))
        bcast_slots(12, 2, 2, (pf + "rug",))
        for oc in range(IC):
            for tg in range(NTG):
                t = oc * NTG + tg
                if phase == 1 and t % 4 == 0 and t // 4 < 3 * DC:
                    dq_dn_seg(t // 4)
                i = t % 2
                gt = MT[:, i * TG:(i + 1) * TG]
                sg = MT[:, (2 + i) * TG:(3 + i) * TG]
                ED("sync", (lambda oc, tg, gt: lambda g: g.dma_start(out=gt, in_=gbuf[oc * 128:(oc + 1) * 128, tg * TG:(tg + 1) * TG]))(oc, tg, gt),
                   r=(pf + "UB1_%d_%d" % (oc, tg), "MT%d" % i), w=("MT%d" % i,), chan="cmg%d" % i)
                E("scalar", (lambda gt, sg: lambda s: s.activation(out=sg, in_=gt, func=AF.Sigmoid))(gt, sg),
                  r=("MT%d" % i, "MT%d" % (2 + i)), w=("MT%d" % (2 + i),))
                E("vector", (lambda sg: lambda v: v.tensor_scalar(out=sg, in0=sg, scalar1=K65536, scalar2=CM, op0=ALU.mult, op1=ALU.add))(sg),
                  r=("MT%d" % (2 + i),), w=("MT%d" % (2 + i),))
                E("scalar", (lambda sg: lambda s: s.activation(out=sg, in_=sg, func=AF.Identity, bias=COLS[:, 15:16]))(sg),
                  r=("MT%d" % (2 + i), "COLS"), w=("MT%d" % (2 + i),))
                E("vector", (lambda gt: lambda v: v.tensor_scalar(out=gt, in0=gt, scalar1=COLS[:, 3:4], scalar2=CM, op0=ALU.mult, op1=ALU.add))(gt),
                  r=("MT%d" % i, "COLS"), w=("MT%d" % i,))
                E("vector", (lambda gt, sg: lambda v: v.scalar_tensor_tensor(out=gt, in0=gt, scalar=-CM, in1=sg, op0=ALU.add, op1=ALU.mult))(gt, sg),
                  r=("MT%d" % i, "MT%d" % (2 + i)), w=("MT%d" % i,))
                cix = MX_A + oc * NTG + tg
                E("vector", (lambda gt, cix: lambda g: g.tensor_reduce(out=MAXC[:, cix:cix + 1], in_=gt, axis=AX.X, op=ALU.max, apply_absolute_value=True))(gt, cix),
                  r=("MT%d" % i,), w=("MXA",))
                ED("gpsimd", (lambda gt, oc, tg: lambda g: g.dma_start(out=t2buf[oc * 128:(oc + 1) * 128, tg * TG:(tg + 1) * TG], in_=gt))(gt, oc, tg),
                   r=("MT%d" % i,), w=(pf + "T2%d_%d" % (oc, tg),), chan="cms%d" % i)
        E("vector", lambda v: v.tensor_reduce(out=MAXC[:, MX_F:MX_F + 1], in_=MAXC[:, MX_A:MX_A + 96], axis=AX.X, op=ALU.max),
          r=("MXA",), w=("MXF",))
        part_reduce_to_scalar("MXF", MX_F, 16, pf + "g2")
        collective_max(16, 1, (pf + "g2",), pf + "g2g")
        sc_op(lambda v: v.tensor_tensor(out=SROW[0:1, 17:18], in0=SROW[0:1, 16:17], in1=SROW[0:1, 14:15], op=ALU.mult),
              r=(pf + "g2g", pf + "kap"), w=())
        sc_op(lambda v: v.tensor_scalar(out=SROW[0:1, 18:19], in0=SROW[0:1, 17:18], scalar1=INV_QMAX, scalar2=EPS12, op0=ALU.mult, op1=ALU.max),
              r=(), w=(pf + "s2",))
        sc_op(lambda v: v.reciprocal(out=SROW[0:1, 19:20], in_=SROW[0:1, 18:19]),
              r=(pf + "s2",), w=())
        sc_op(lambda v: v.tensor_tensor(out=SROW[0:1, 20:21], in0=SROW[0:1, 19:20], in1=SROW[0:1, 14:15], op=ALU.mult),
              r=(pf + "kap",), w=(pf + "r2",))
        sc_op(lambda v: v.tensor_tensor(out=SROW[0:1, 21:22], in0=SROW[0:1, 18:19], in1=SROW[0:1, 10:11], op=ALU.mult),
              r=(pf + "s2", pf + "sug"), w=(pf + "lam",))
        bcast_slots(20, 1, 4, (pf + "r2",))
        if phase == 1 and SUB <= 5:
            return
        for oc in range(IC):
            for tg in range(NTG):
                i = (oc * NTG + tg) % 2
                t2t = MT[:, i * TG:(i + 1) * TG]
                ut = MT[:, (2 + i) * TG:(3 + i) * TG]
                ED("sync", (lambda oc, tg, t2t: lambda g: g.dma_start(out=t2t, in_=t2buf[oc * 128:(oc + 1) * 128, tg * TG:(tg + 1) * TG]))(oc, tg, t2t),
                   r=(pf + "T2%d_%d" % (oc, tg), "MT%d" % i), w=("MT%d" % i,), chan="cmg%d" % i)
                ED("sync", (lambda oc, tg, ut: lambda g: g.dma_start(out=ut, in_=upbuf[oc * 128:(oc + 1) * 128, tg * TG:(tg + 1) * TG]))(oc, tg, ut),
                   r=(pf + "UB0_%d_%d" % (oc, tg), "MT%d" % (2 + i)), w=("MT%d" % (2 + i),), chan="cmu%d" % i)
                E("vector", (lambda t2t: lambda v: v.tensor_scalar(out=t2t, in0=t2t, scalar1=COLS[:, 4:5], scalar2=CM, op0=ALU.mult, op1=ALU.add))(t2t),
                  r=("MT%d" % i, "COLS"), w=("MT%d" % i,))
                E("vector", (lambda ut: lambda v: v.tensor_scalar(out=ut, in0=ut, scalar1=COLS[:, 2:3], scalar2=CM, op0=ALU.mult, op1=ALU.add))(ut),
                  r=("MT%d" % (2 + i), "COLS"), w=("MT%d" % (2 + i),))
                E("scalar", (lambda ut: lambda s: s.activation(out=ut, in_=ut, func=AF.Identity, bias=COLS[:, 15:16]))(ut),
                  r=("MT%d" % (2 + i), "COLS"), w=("MT%d" % (2 + i),))
                E("vector", (lambda t2t, ut: lambda v: v.scalar_tensor_tensor(out=t2t, in0=t2t, scalar=-CM, in1=ut, op0=ALU.add, op1=ALU.mult))(t2t, ut),
                  r=("MT%d" % i, "MT%d" % (2 + i)), w=("MT%d" % i,))
                cix = MX_A + oc * NTG + tg
                E("vector", (lambda t2t, cix: lambda g: g.tensor_reduce(out=MAXC[:, cix:cix + 1], in_=t2t, axis=AX.X, op=ALU.max, apply_absolute_value=True))(t2t, cix),
                  r=("MT%d" % i,), w=("MXA",))
                ED("gpsimd", (lambda t2t, oc, tg: lambda g: g.dma_start(out=gbuf[oc * 128:(oc + 1) * 128, tg * TG:(tg + 1) * TG], in_=t2t))(t2t, oc, tg),
                   r=("MT%d" % i,), w=(pf + "T4%d_%d" % (oc, tg),), chan="cms%d" % i)
        E("vector", lambda v: v.tensor_reduce(out=MAXC[:, MX_F:MX_F + 1], in_=MAXC[:, MX_A:MX_A + 96], axis=AX.X, op=ALU.max),
          r=("MXA",), w=("MXF",))
        part_reduce_to_scalar("MXF", MX_F, 24, pf + "g4")
        collective_max(24, 1, (pf + "g4",), pf + "g4g")
        sc_op(lambda v: v.tensor_tensor(out=SROW[0:1, 25:26], in0=SROW[0:1, 24:25], in1=SROW[0:1, 21:22], op=ALU.mult),
              r=(pf + "g4g", pf + "lam"), w=())
        sc_op(lambda v: v.tensor_scalar(out=SROW[0:1, 26:27], in0=SROW[0:1, 25:26], scalar1=INV_QMAX, scalar2=EPS12, op0=ALU.mult, op1=ALU.max),
              r=(), w=(pf + "s4",))
        sc_op(lambda v: v.reciprocal(out=SROW[0:1, 27:28], in_=SROW[0:1, 26:27]),
              r=(pf + "s4",), w=())
        sc_op(lambda v: v.tensor_tensor(out=SROW[0:1, 28:29], in0=SROW[0:1, 27:28], in1=SROW[0:1, 21:22], op=ALU.mult),
              r=(pf + "lam",), w=(pf + "r4",))
        sc_op(lambda v: v.tensor_scalar(out=SROW[0:1, 29:30], in0=SROW[0:1, 26:27], scalar1=0.5, scalar2=None, op0=ALU.mult),
              r=(pf + "s4",), w=(pf + "hs4",))
        bcast_slots(28, 2, 5, (pf + "r4", pf + "hs4"))
        if phase == 1 and SUB <= 6:
            return
        for tg in range(NTG):
            for ic in range(IC):
                i = ic % 2
                t4t = MT[:, i * TG:(i + 1) * TG]
                ED("sync", (lambda ic, tg, t4t: lambda g: g.dma_start(out=t4t, in_=gbuf[ic * 128:(ic + 1) * 128, tg * TG:(tg + 1) * TG]))(ic, tg, t4t),
                   r=(pf + "T4%d_%d" % (ic, tg), "MT%d" % i), w=("MT%d" % i,), chan="cmg%d" % i)
                E("vector", (lambda t4t: lambda v: v.tensor_scalar(out=t4t, in0=t4t, scalar1=COLS[:, 5:6], scalar2=CM, op0=ALU.mult, op1=ALU.add))(t4t),
                  r=("MT%d" % i, "COLS"), w=("MT%d" % i,))
                csl = CB[:, ic * TG:(ic + 1) * TG]
                E("vector", (lambda t4t, csl: lambda v: v.tensor_scalar(out=csl, in0=t4t, scalar1=CM, scalar2=None, op0=ALU.subtract))(t4t, csl),
                  r=("MT%d" % i, "CB%d" % (ic // 4)), w=("CB%d" % (ic // 4),))
            for oc in range(DC):
                if P4MODE == 1:
                    continue
                bank = PMM[oc % 2]
                for grp in range(3):
                    sl = (oc * 3 + grp) % 4
                    ED("sync", (lambda oc, grp, sl: lambda g: g.dma_start(out=LW[:, sl * 1024:(sl + 1) * 1024], in_=wdnT[oc][:, grp * 1024:(grp + 1) * 1024]))(oc, grp, sl),
                       r=("LW%d" % sl, "wdT%d" % oc), w=("LW%d" % sl,), chan="clw%d" % (sl % 2))
                    def mmdn(oc=oc, grp=grp, sl=sl, bank=bank, tg=tg):
                        def f(t):
                            last = None
                            for j in range(8):
                                ic = grp * 8 + j
                                last = t.matmul(bank[:, 0:TG], LW[:, sl * 1024 + j * 128: sl * 1024 + (j + 1) * 128],
                                                CB[:, ic * TG:(ic + 1) * TG],
                                                start=(ic == 0), stop=(ic == IC - 1))
                            return last
                        return f
                    E("tensor", mmdn(), r=tuple(["LW%d" % sl] + ["CB%d" % icg for icg in range(grp * 2, grp * 2 + 2)] + ["PMM%d" % (oc % 2)]),
                      w=("PMM%d" % (oc % 2),))
                xsl = XT[:, oc * TC + tg * TG: oc * TC + tg * TG + TG]
                if P4MODE == 2:
                    E("vector", (lambda bank: lambda v: v.tensor_copy(out=US[:, 0:TG], in_=bank[:, 0:TG]))(bank),
                      r=("PMM%d" % (oc % 2), "US0"), w=("US0",))
                else:
                    E("vector", (lambda bank, xsl: lambda v: v.scalar_tensor_tensor(out=xsl, in0=bank[:, 0:TG], scalar=COLS[:, 6:7], in1=xsl, op0=ALU.mult, op1=ALU.add))(bank, xsl),
                      r=("PMM%d" % (oc % 2), "COLS", "XT%d" % oc), w=("XT%d" % oc,))
                if phase == 2:
                    ED("gpsimd", (lambda oc, tg, xsl: lambda g: g.dma_start(out=out[oc * 128:(oc + 1) * 128, tg * TG:(tg + 1) * TG], in_=xsl))(oc, tg, xsl),
                       r=("XT%d" % oc,), w=("OUT%d_%d" % (oc, tg),), chan="cout")

    def attn_proj(mi, wsrc, dst_tensor, act, outpfx):
        for dc in range(DC):
            ED("sync", (lambda mi, dc: lambda g: g.dma_start(out=CB[:, dc * TC:(dc + 1) * TC], in_=mixbuf[mi, dc * 128:(dc + 1) * 128, :]))(mi, dc),
               r=("MIXB%d_%d" % (mi, dc), "CB%d" % dc), w=("CB%d" % dc,), chan="cmxr")
        for oc in range(DC):
            sl = oc % 4
            wsl = oc % 2
            ED("sync", (lambda wsrc, oc, wsl: lambda g: g.dma_start(out=WB[:, wsl * D:(wsl + 1) * D], in_=wsrc[oc]))(wsrc, oc, wsl),
               r=("WB%d" % wsl,), w=("WB%d" % wsl,), chan="cwb%d" % wsl)
            E("vector", (lambda sl, wsl: lambda v: v.tensor_copy(out=LW[:, sl * 1024:(sl + 1) * 1024], in_=WB[:, wsl * D:(wsl + 1) * D]))(sl, wsl),
              r=("WB%d" % wsl, "LW%d" % sl), w=("LW%d" % sl,))
            for tg in range(NTG):
                bi = (oc * NTG + tg) % 2
                bank = PMM[bi]
                def mmp(oc=oc, tg=tg, sl=sl, bank=bank):
                    def f(t):
                        last = None
                        for ic in range(DC):
                            last = t.matmul(bank[:, 0:TG], LW[:, sl * 1024 + ic * 128:sl * 1024 + (ic + 1) * 128],
                                            CB[:, ic * TC + tg * TG:ic * TC + tg * TG + TG],
                                            start=(ic == 0), stop=(ic == DC - 1))
                        return last
                    return f
                E("tensor", mmp(), r=tuple(["LW%d" % sl] + ["CB%d" % ic for ic in range(DC)] + ["PMM%d" % bi]),
                  w=("PMM%d" % bi,))
                dsl = dst_tensor[:, 1 + oc * T1 + tg * TG: 1 + oc * T1 + tg * TG + TG]
                if dst_tensor is A2:
                    extr = ("A2_%d" % oc,)
                else:
                    extr = ("B2_%d" % oc,) + tuple("XSP%d" % q for q in range(DC))
                dtok = extr[0]
                E("scalar", (lambda bank, dsl, act: lambda s: s.activation(out=dsl, in_=bank[:, 0:TG], func=act))(bank, dsl, act),
                  r=("PMM%d" % bi,) + extr, w=(dtok, "%s%d_%d" % (outpfx, oc, tg)))

    def attention():
        for dc in range(DC):
            E("vector", (lambda dc: lambda v: v.tensor_copy(out=MAXC[:, MX_BX + dc:MX_BX + dc + 1], in_=XT[:, dc * TC + TC - 1:dc * TC + TC]))(dc),
              r=("XT%d" % dc,), w=("MXBX",))
        E("tensor", lambda t: t.transpose(PSG[0:8, 0:128], MAXC[:, MX_BX:MX_BX + 8], IDF[:]),
          r=("MXBX", "IDF", "PSG"), w=("PSG",))
        E("scalar", lambda s: s.copy(out=XR2[:, :], in_=PSG[0:8, 0:128]),
          r=("PSG", "SQT"), w=("SQT",))
        E("tensor", lambda t: t.matmul(PSG[:, 0:128], SEL8[:, :], XR2[:, :], start=True, stop=True),
          r=("SQT", "SEL8", "PSG"), w=("PSG",))
        E("vector", lambda v: v.tensor_copy(out=CCT[:], in_=PSG[:, 0:128]),
          r=("PSG", "SQT"), w=("SQT",))
        collective_add_tile("SQT", "bndx")
        E("tensor", lambda t: t.matmul(PSG[0:8, 0:128], SELR8[:, :], CCT[:], start=True, stop=True),
          r=("bndx", "SQT", "SELR8", "PSG"), w=("PSG",))
        E("scalar", lambda s: s.copy(out=XR2[:, :], in_=PSG[0:8, 0:128]),
          r=("PSG", "SQT"), w=("SQT",))
        E("scalar", lambda s: s.activation(out=XRS[:, :], in_=XR2[:, :], func=AF.Square),
          r=("SQT", "SQT"), w=("SQT",))
        E("tensor", lambda t: t.matmul(PBC[0][0:1, 0:128], ONEC[0:8, 0:1], XR2[:, :], start=True, stop=True),
          r=("SQT", "ONEC", "PBC0"), w=("PBC0",))
        E("tensor", lambda t: t.matmul(PBC[1][0:1, 0:128], ONEC[0:8, 0:1], XRS[:, :], start=True, stop=True),
          r=("SQT", "ONEC", "PBC1"), w=("PBC1",))
        sc_op(lambda v: v.tensor_reduce(out=SROW[0:1, 40:41], in_=PBC[0][0:1, 0:128], axis=AX.X, op=ALU.add),
              r=("PBC0",), w=())
        sc_op(lambda v: v.tensor_reduce(out=SROW[0:1, 41:42], in_=PBC[1][0:1, 0:128], axis=AX.X, op=ALU.add),
              r=("PBC1",), w=())
        sc_op(lambda v: v.tensor_scalar(out=SROW[0:1, 40:42], in0=SROW[0:1, 40:42], scalar1=INV_D, scalar2=None, op0=ALU.mult),
              r=(), w=())
        sc_op(lambda v: v.tensor_tensor(out=SROW[0:1, 42:43], in0=SROW[0:1, 40:41], in1=SROW[0:1, 40:41], op=ALU.mult),
              r=(), w=())
        sc_op(lambda v: v.tensor_scalar(out=SROW[0:1, 43:44], in0=SROW[0:1, 42:43], scalar1=-1.0, scalar2=LNEPS, op0=ALU.mult, op1=ALU.add),
              r=(), w=())
        sc_op(lambda v: v.tensor_tensor(out=SROW[0:1, 43:44], in0=SROW[0:1, 41:42], in1=SROW[0:1, 43:44], op=ALU.add),
              r=(), w=())
        E("scalar", lambda s: s.activation(out=SROW[0:1, 44:45], in_=SROW[0:1, 43:44], func=AF.Ln),
          r=("SROW",), w=("SROW",))
        E("scalar", lambda s: s.activation(out=SROW[0:1, 45:46], in_=SROW[0:1, 44:45], func=AF.Exp, scale=-0.5),
          r=("SROW",), w=("SROW",))
        sc_op(lambda v: v.scalar_tensor_tensor(out=SROW[0:1, 46:47], in0=SROW[0:1, 40:41], scalar=-1.0, in1=SROW[0:1, 45:46], op0=ALU.mult, op1=ALU.mult),
              r=(), w=("bln",))
        bcast_slots(45, 2, 7, ("bln",))
        E("tensor", lambda t: t.transpose(PSG[:, 0:8], XR2[:, :], IDF[0:8, 0:8]),
          r=("SQT", "IDF", "PSG"), w=("PSG",))
        E("vector", lambda v: v.tensor_copy(out=MAXC[:, MX_BX:MX_BX + 8], in_=PSG[:, 0:8]),
          r=("PSG", "MXBX"), w=("MXBX",))
        for dc in range(DC):
            E("vector", (lambda dc: lambda v: v.tensor_scalar(out=A2[:, dc * T1:dc * T1 + 1], in0=MAXC[:, MX_BX + dc:MX_BX + dc + 1], scalar1=COLS[:, 7:8], scalar2=COLS[:, 8:9], op0=ALU.mult, op1=ALU.add))(dc),
              r=("MXBX", "COLS", "A2_%d" % dc), w=("A2_%d" % dc,))
        ln_stats_and_apply("A", False)
        if DBG:
            for dc in range(DC):
                ED("gpsimd", (lambda dc: lambda g: g.dma_start(out=dbg2[dc * 128:(dc + 1) * 128, :], in_=A2[:, 1 + dc * T1:1 + dc * T1 + TC]))(dc),
                   r=("A2_%d" % dc,), w=("DBG2_%d" % dc,), chan="cdbg")
            ED("gpsimd", lambda g: g.dma_start(out=dbg4[:, 0:128], in_=SQT[:, 128:256]), r=("SQT",), w=("DBG4",), chan="cdbg")
            ED("gpsimd", lambda g: g.dma_start(out=dbg4[:, 128:136], in_=MAXC[:, MX_BX:MX_BX + 8]), r=("MXBX",), w=("DBG4",), chan="cdbg")
            ED("gpsimd", lambda g: g.dma_start(out=dbg4[:, 136:145], in_=COLS[:, 0:9]), r=("COLS",), w=("DBG4",), chan="cdbg")
            ED("gpsimd", lambda g: g.dma_start(out=dbg4[0:1, 152:159], in_=SROW[0:1, 40:47]), r=("SROW",), w=("DBG4",), chan="cdbg")
            for dc in range(DC):
                ED("gpsimd", (lambda dc: lambda g: g.dma_start(out=dbg4[:, 216 + dc:217 + dc], in_=A2[:, dc * T1:dc * T1 + 1], allow_slow_non_contiguous=True))(dc),
                   r=("A2_%d" % dc,), w=("DBG4",), chan="cdbg")
        for dc in range(DC):
            ED("gpsimd", (lambda dc: lambda g: g.dma_start(out=xspill[dc * 128:(dc + 1) * 128, :], in_=XT[:, dc * TC:(dc + 1) * TC]))(dc),
               r=("XT%d" % dc,), w=("XSP%d" % dc,), chan="csp")
        for mi, colb in ((0, 16), (1, 24), (2, 32)):
            for dc in range(DC):
                cur = A2[:, 1 + dc * T1:1 + dc * T1 + TC]
                prv = A2[:, dc * T1:dc * T1 + TC]
                scr = MT[:, 0:TC]
                E("vector", (lambda cur, prv, scr: lambda v: v.tensor_tensor(out=scr, in0=cur, in1=prv, op=ALU.subtract))(cur, prv, scr),
                  r=("A2_%d" % dc, "MT0", "MT1", "MT2", "MT3"), w=("MT0", "MT1", "MT2", "MT3"))
                E("vector", (lambda prv, scr, colb, dc: lambda v: v.scalar_tensor_tensor(out=CB[:, dc * TC:(dc + 1) * TC], in0=scr, scalar=DCV[:, colb + dc:colb + dc + 1], in1=prv, op0=ALU.mult, op1=ALU.add))(prv, scr, colb, dc),
                  r=("MT0", "MT1", "MT2", "MT3", "A2_%d" % dc, "DCV", "CB%d" % dc), w=("CB%d" % dc,))
                ED("gpsimd", (lambda mi, dc: lambda g: g.dma_start(out=mixbuf[mi, dc * 128:(dc + 1) * 128, :], in_=CB[:, dc * TC:(dc + 1) * TC]))(mi, dc),
                   r=("CB%d" % dc,), w=("MIXB%d_%d" % (mi, dc),), chan="cmx")
        attn_proj(0, wk, A2, AF.Exp, "EK")
        attn_proj(1, wv, B2, AF.Identity, "VV")
        for dc in range(DC):
            for tg in range(NTG):
                a = A2[:, 1 + dc * T1 + tg * TG:1 + dc * T1 + tg * TG + TG]
                b = B2[:, 1 + dc * T1 + tg * TG:1 + dc * T1 + tg * TG + TG]
                E("gpsimd", (lambda a, b: lambda g: g.tensor_tensor(out=b, in0=b, in1=a, op=ALU.mult))(a, b),
                  r=("A2_%d" % dc, "B2_%d" % dc), w=("B2_%d" % dc,))
        for dc in range(DC):
            E("vector", (lambda dc: lambda v: v.memset(A2[:, dc * T1:dc * T1 + 1], 0.0))(dc),
              r=("A2_%d" % dc,), w=("A2_%d" % dc,))
            E("vector", (lambda dc: lambda v: v.memset(B2[:, dc * T1:dc * T1 + 1], 0.0))(dc),
              r=("B2_%d" % dc,), w=("B2_%d" % dc,))
        for dc in range(DC):
            ek = A2[:, dc * T1:(dc + 1) * T1]
            av = B2[:, dc * T1:(dc + 1) * T1]
            bo = SC[:, 0:T1]
            ao = SC[:, T1:2 * T1]
            emwb = DCV[:, 8 + dc:9 + dc].broadcast_to([128, T1])
            E("vector", (lambda ek, bo, emwb: lambda v: v.tensor_tensor_scan(out=bo, data0=emwb, data1=ek, initial=0.0, op0=ALU.mult, op1=ALU.add))(ek, bo, emwb),
              r=("A2_%d" % dc, "DCV", "SC0", "SC1", "SC2", "CB0", "CB1", "CB2", "CB3", "CB4", "CB5", "CB6"), w=("SC0", "SC1", "SC2", "CB0", "CB1", "CB2", "CB3", "CB4", "CB5", "CB6"))
            E("vector", (lambda av, ao, emwb: lambda v: v.tensor_tensor_scan(out=ao, data0=emwb, data1=av, initial=0.0, op0=ALU.mult, op1=ALU.add))(av, ao, emwb),
              r=("B2_%d" % dc, "DCV", "SC0", "SC1", "SC2", "CB0", "CB1", "CB2", "CB3", "CB4", "CB5", "CB6"), w=("SC0", "SC1", "SC2", "CB0", "CB1", "CB2", "CB3", "CB4", "CB5", "CB6"))
            E("vector", (lambda bo, dc: lambda v: v.tensor_copy(out=MAXC[:, MX_SF + dc:MX_SF + dc + 1], in_=bo[:, TC:TC + 1]))(bo, dc),
              r=("SC0", "SC1", "SC2", "CB0", "CB1", "CB2", "CB3", "CB4", "CB5", "CB6", "MXSF"), w=("MXSF",))
            E("vector", (lambda ao, dc: lambda v: v.tensor_copy(out=MAXC[:, MX_SF + 8 + dc:MX_SF + 8 + dc + 1], in_=ao[:, TC:TC + 1]))(ao, dc),
              r=("SC0", "SC1", "SC2", "CB0", "CB1", "CB2", "CB3", "CB4", "CB5", "CB6", "MXSF"), w=("MXSF",))
            ED("gpsimd", (lambda bo, dc: lambda g: g.dma_start(out=scanbuf[dc * 128:(dc + 1) * 128, :], in_=bo))(bo, dc),
               r=("SC0", "SC1", "SC2", "CB0", "CB1", "CB2", "CB3", "CB4", "CB5", "CB6",), w=("SCB%d" % dc,), chan="csb0")
            ED("gpsimd", (lambda ao, dc: lambda g: g.dma_start(out=scanbuf[D + dc * 128:D + (dc + 1) * 128, :], in_=ao))(ao, dc),
               r=("SC0", "SC1", "SC2", "CB0", "CB1", "CB2", "CB3", "CB4", "CB5", "CB6",), w=("SCA%d" % dc,), chan="csb1")
        E("tensor", lambda t: t.transpose(PSG[0:16, 0:128], MAXC[:, MX_SF:MX_SF + 16], IDF[:]),
          r=("MXSF", "IDF", "PSG"), w=("PSG",))
        E("scalar", lambda s: s.copy(out=H16[:, :], in_=PSG[0:16, 0:128]),
          r=("PSG", "SQT"), w=("SQT",))
        E("tensor", lambda t: t.matmul(PSG[:, 0:128], SEL16[:, :], H16[:, :], start=True, stop=True),
          r=("SQT", "SEL16", "PSG"), w=("PSG",))
        E("vector", lambda v: v.tensor_copy(out=CCT[:], in_=PSG[:, 0:128]),
          r=("PSG", "SQT"), w=("SQT",))
        collective_add_tile("SQT", "hoff")
        E("tensor", lambda t: t.matmul(PSG[0:16, 0:128], SELR16[:, :], CCT[:], start=True, stop=True),
          r=("hoff", "SQT", "SELR16", "PSG"), w=("PSG",))
        E("scalar", lambda s: s.copy(out=H16[:, :], in_=PSG[0:16, 0:128]),
          r=("PSG", "SQT"), w=("SQT",))
        E("tensor", lambda t: t.transpose(PSG[:, 0:16], H16[:, :], IDF[0:16, 0:16]),
          r=("SQT", "IDF", "PSG"), w=("PSG",))
        E("vector", lambda v: v.tensor_copy(out=MAXC[:, MX_B0:MX_B0 + 16], in_=PSG[:, 0:16]),
          r=("PSG", "MXB0"), w=("MXB0",))
        for dc in range(DC):
            ek = A2[:, dc * T1:(dc + 1) * T1]
            av = B2[:, dc * T1:(dc + 1) * T1]
            bo = SC[:, 0:T1]
            ao = SC[:, T1:2 * T1]
            dec = SC[:, 2 * T1:3 * T1]
            emwb = DCV[:, 8 + dc:9 + dc].broadcast_to([128, T1])
            ED("sync", (lambda bo, dc: lambda g: g.dma_start(out=bo, in_=scanbuf[dc * 128:(dc + 1) * 128, :]))(bo, dc),
               r=("SCB%d" % dc, "SC0", "SC1", "SC2", "CB0", "CB1", "CB2", "CB3", "CB4", "CB5", "CB6"), w=("SC0", "SC1", "SC2", "CB0", "CB1", "CB2", "CB3", "CB4", "CB5", "CB6"), chan="csr0")
            ED("sync", (lambda ao, dc: lambda g: g.dma_start(out=ao, in_=scanbuf[D + dc * 128:D + (dc + 1) * 128, :]))(ao, dc),
               r=("SCA%d" % dc, "SC0", "SC1", "SC2", "CB0", "CB1", "CB2", "CB3", "CB4", "CB5", "CB6"), w=("SC0", "SC1", "SC2", "CB0", "CB1", "CB2", "CB3", "CB4", "CB5", "CB6"), chan="csr1")
            E("vector", (lambda dec: lambda v: v.memset(dec[:, 0:T1], 0.0))(dec), r=("SC0", "SC1", "SC2", "CB0", "CB1", "CB2", "CB3", "CB4", "CB5", "CB6",), w=("SC0", "SC1", "SC2", "CB0", "CB1", "CB2", "CB3", "CB4", "CB5", "CB6"))
            E("vector", (lambda dec: lambda v: v.memset(dec[:, 0:1], 1.0))(dec), r=("SC0", "SC1", "SC2", "CB0", "CB1", "CB2", "CB3", "CB4", "CB5", "CB6",), w=("SC0", "SC1", "SC2", "CB0", "CB1", "CB2", "CB3", "CB4", "CB5", "CB6"))
            E("vector", (lambda dec, emwb: lambda v: v.tensor_tensor_scan(out=dec, data0=emwb, data1=dec, initial=0.0, op0=ALU.mult, op1=ALU.add))(dec, emwb),
              r=("SC0", "SC1", "SC2", "CB0", "CB1", "CB2", "CB3", "CB4", "CB5", "CB6", "DCV"), w=("SC0", "SC1", "SC2", "CB0", "CB1", "CB2", "CB3", "CB4", "CB5", "CB6"))
            E("vector", (lambda bo, dec, dc: lambda v: v.scalar_tensor_tensor(out=bo, in0=dec, scalar=MAXC[:, MX_B0 + dc:MX_B0 + dc + 1], in1=bo, op0=ALU.mult, op1=ALU.add))(bo, dec, dc),
              r=("SC0", "SC1", "SC2", "CB0", "CB1", "CB2", "CB3", "CB4", "CB5", "CB6", "MXB0"), w=("SC0", "SC1", "SC2", "CB0", "CB1", "CB2", "CB3", "CB4", "CB5", "CB6"))
            E("vector", (lambda ao, dec, dc: lambda v: v.scalar_tensor_tensor(out=ao, in0=dec, scalar=MAXC[:, MX_B0 + 8 + dc:MX_B0 + 8 + dc + 1], in1=ao, op0=ALU.mult, op1=ALU.add))(ao, dec, dc),
              r=("SC0", "SC1", "SC2", "CB0", "CB1", "CB2", "CB3", "CB4", "CB5", "CB6", "MXB0"), w=("SC0", "SC1", "SC2", "CB0", "CB1", "CB2", "CB3", "CB4", "CB5", "CB6"))
            E("vector", (lambda ek, bo, dc: lambda v: v.scalar_tensor_tensor(out=ek[:, 1:T1], in0=ek[:, 1:T1], scalar=DCV[:, 0 + dc:1 + dc], in1=bo[:, 0:TC], op0=ALU.mult, op1=ALU.add))(ek, bo, dc),
              r=("A2_%d" % dc, "SC0", "SC1", "SC2", "CB0", "CB1", "CB2", "CB3", "CB4", "CB5", "CB6", "DCV"), w=("A2_%d" % dc,))
            E("vector", (lambda av, ao, dc: lambda v: v.scalar_tensor_tensor(out=av[:, 1:T1], in0=av[:, 1:T1], scalar=DCV[:, 0 + dc:1 + dc], in1=ao[:, 0:TC], op0=ALU.mult, op1=ALU.add))(av, ao, dc),
              r=("B2_%d" % dc, "SC0", "SC1", "SC2", "CB0", "CB1", "CB2", "CB3", "CB4", "CB5", "CB6", "DCV"), w=("B2_%d" % dc,))
            E("scalar", (lambda ek: lambda s: s.activation(out=ek[:, 1:T1], in_=ek[:, 1:T1], func=AF.Ln))(ek),
              r=("A2_%d" % dc,), w=("A2_%d" % dc,))
            E("scalar", (lambda ek: lambda s: s.activation(out=ek[:, 1:T1], in_=ek[:, 1:T1], func=AF.Exp, scale=-1.0))(ek),
              r=("A2_%d" % dc,), w=("A2_%d" % dc,))
            E("vector", (lambda ek, av: lambda v: v.tensor_tensor(out=av[:, 1:T1], in0=av[:, 1:T1], in1=ek[:, 1:T1], op=ALU.mult))(ek, av),
              r=("A2_%d" % dc, "B2_%d" % dc), w=("B2_%d" % dc,))
        attn_proj(2, wr, A2, AF.Sigmoid, "SR")
        for dc in range(DC):
            for tg in range(NTG):
                a = A2[:, 1 + dc * T1 + tg * TG:1 + dc * T1 + tg * TG + TG]
                b = B2[:, 1 + dc * T1 + tg * TG:1 + dc * T1 + tg * TG + TG]
                c = CB[:, dc * TC + tg * TG:dc * TC + tg * TG + TG]
                E("vector", (lambda a, b, c: lambda v: v.tensor_tensor(out=c, in0=b, in1=a, op=ALU.mult))(a, b, c),
                  r=("A2_%d" % dc, "B2_%d" % dc, "CB%d" % dc), w=("CB%d" % dc,))
        for dc in range(DC):
            ED("sync", (lambda dc: lambda g: g.dma_start(out=XT[:, dc * TC:(dc + 1) * TC], in_=xspill[dc * 128:(dc + 1) * 128, :]))(dc),
               r=("XSP%d" % dc, "XT%d" % dc), w=("XT%d" % dc,) + tuple("B2_%d" % q for q in range(DC)), chan="cxr")
        for oc in range(DC):
            sl = oc % 4
            wsl = oc % 2
            ED("sync", (lambda oc, wsl: lambda g: g.dma_start(out=WB[:, wsl * D:(wsl + 1) * D], in_=wo[oc]))(oc, wsl),
               r=("WB%d" % wsl,), w=("WB%d" % wsl,), chan="cwb%d" % wsl)
            E("vector", (lambda sl, wsl: lambda v: v.tensor_copy(out=LW[:, sl * 1024:(sl + 1) * 1024], in_=WB[:, wsl * D:(wsl + 1) * D]))(sl, wsl),
              r=("WB%d" % wsl, "LW%d" % sl), w=("LW%d" % sl,))
            for tg in range(NTG):
                bi = (oc * NTG + tg) % 2
                bank = PMM[bi]
                def mmo(oc=oc, tg=tg, sl=sl, bank=bank):
                    def f(t):
                        last = None
                        for ic in range(DC):
                            last = t.matmul(bank[:, 0:TG], LW[:, sl * 1024 + ic * 128:sl * 1024 + (ic + 1) * 128],
                                            CB[:, ic * TC + tg * TG:ic * TC + tg * TG + TG],
                                            start=(ic == 0), stop=(ic == DC - 1))
                        return last
                    return f
                E("tensor", mmo(), r=tuple(["LW%d" % sl] + ["CB%d" % ic for ic in range(DC)] + ["PMM%d" % bi]),
                  w=("PMM%d" % bi,))
                xsl = XT[:, oc * TC + tg * TG:oc * TC + tg * TG + TG]
                E("vector", (lambda bank, xsl: lambda v: v.tensor_tensor(out=xsl, in0=xsl, in1=bank[:, 0:TG], op=ALU.add))(bank, xsl),
                  r=("PMM%d" % bi, "XT%d" % oc), w=("XT%d" % oc,))

    def finish_early():
        for oc in range(DC):
            for tg in range(NTG):
                xsl = XT[:, oc * TC + tg * TG: oc * TC + tg * TG + TG]
                ED("gpsimd", (lambda oc, tg, xsl: lambda g: g.dma_start(out=out[oc * 128:(oc + 1) * 128, tg * TG:(tg + 1) * TG], in_=xsl))(oc, tg, xsl),
                   r=("XT%d" % oc,), w=("OUT%d_%d" % (oc, tg),), chan="cout")
        p.barrier_wait("gpsimd", [T("OUT%d_%d" % (oc, tg)) for oc in range(DC) for tg in range(NTG)])
        p.drain_all("gpsimd")
        for ch in sorted(p.chans):
            sems[ch] = stack.enter_context(nc.semaphore("sd_" + ch))
        if "ccol" in p.cnt:
            sems["ccol"] = stack.enter_context(nc.semaphore("s_ccol"))
        with nc.Block() as block:
            p.realize(block, sems)
        stack.close()
        return nc

    if STAGE <= 0:
        return finish_early()
    ffn_phase(1)
    if DBG:
        pass
    if DBG:
        for dc in range(DC):
            ED("gpsimd", (lambda dc: lambda g: g.dma_start(out=dbg1[dc * 128:(dc + 1) * 128, :], in_=XT[:, dc * TC:(dc + 1) * TC]))(dc),
               r=("XT%d" % dc,), w=("DBG1_%d" % dc,), chan="cdbg")
    if STAGE <= 1:
        return finish_early()
    attention()
    if DBG:
        for dc in range(DC):
            ED("gpsimd", (lambda dc: lambda g: g.dma_start(out=dbg5[dc * 128:(dc + 1) * 128, :], in_=XT[:, dc * TC:(dc + 1) * TC]))(dc),
               r=("XT%d" % dc,), w=("DBG5_%d" % dc,), chan="cdbg")
    if STAGE <= 2:
        return finish_early()
    ffn_phase(2)
    p.barrier_wait("gpsimd", [T("OUT%d_%d" % (oc, tg)) for oc in range(DC) for tg in range(NTG)])
    p.drain_all("gpsimd")

    for ch in sorted(p.chans):
        sems[ch] = stack.enter_context(nc.semaphore("sd_" + ch))
    if "ccol" in p.cnt:
        sems["ccol"] = stack.enter_context(nc.semaphore("s_ccol"))
    with nc.Block() as block:
        p.realize(block, sems)
    stack.close()
    return nc


def kernel(L0_attn_x, L0_attn_time_decay, L0_attn_time_first, L0_attn_time_mix_k,
           L0_attn_time_mix_v, L0_attn_time_mix_r, L0_attn_W_key, L0_attn_W_value,
           L0_attn_W_receptance, L0_attn_W_output, L0_ffn_w_gate, L0_ffn_w_up, L0_ffn_w_down):
    x = np.asarray(L0_attn_x, np.float32)
    B, Ts, _ = x.shape
    xt_full = np.ascontiguousarray(x.reshape(B * Ts, D).T)

    def permW(w):
        wT = np.asarray(w, np.float32).T
        return np.ascontiguousarray(wT.reshape(8, 128, 8, 128).transpose(2, 1, 0, 3).reshape(8, 128, 1024))

    def permV(v):
        return np.ascontiguousarray(np.asarray(v, np.float32).reshape(8, 128).T)

    wkT = permW(L0_attn_W_key)
    wvT = permW(L0_attn_W_value)
    wrT = permW(L0_attn_W_receptance)
    woT = permW(L0_attn_W_output)
    wupN = np.ascontiguousarray(np.asarray(L0_ffn_w_up, np.float32))
    wgN = np.ascontiguousarray(np.asarray(L0_ffn_w_gate, np.float32))
    wdnN = np.ascontiguousarray(np.asarray(L0_ffn_w_down, np.float32))
    tdec = permV(L0_attn_time_decay)
    tfirst = permV(L0_attn_time_first)
    tmk = permV(L0_attn_time_mix_k)
    tmv = permV(L0_attn_time_mix_v)
    tmr = permV(L0_attn_time_mix_r)
    idn = np.eye(128, dtype=np.float32)

    in_maps = []
    for c in range(NC):
        selw8 = np.zeros((8, 128), np.float32)
        for q in range(8):
            selw8[q, 8 * c + q] = 1.0
        selr8 = np.zeros((128, 8), np.float32)
        selw16 = np.zeros((16, 128), np.float32)
        for q in range(16):
            selw16[q, 16 * c + q] = 1.0
        selr16 = np.zeros((128, 16), np.float32)
        if c % 2 == 1:
            for q in range(8):
                selr8[8 * (c - 1) + q, q] = 1.0
            for q in range(16):
                selr16[16 * (c - 1) + q, q] = 1.0
        in_maps.append({
            "xt": np.ascontiguousarray(xt_full[:, c * TC:(c + 1) * TC]),
            "wk": wkT, "wv": wvT, "wr": wrT, "wo": woT,
            "wup": wupN, "wg": wgN, "wdn": wdnN,
            "tdec": tdec, "tfirst": tfirst, "tmk": tmk, "tmv": tmv, "tmr": tmr,
            "idn": idn, "selw8": selw8, "selr8": selr8,
            "selw16": selw16, "selr16": selr16,
        })
    nc = build_program()
    res = run_bass_kernel_spmd(nc, in_maps, core_ids=list(range(NC)))
    outs = [res.results[c]["out"] for c in range(NC)]
    full = np.concatenate(outs, axis=1)
    return np.ascontiguousarray(full.T).reshape(B, Ts, D).astype(np.float32)


# version 06
# speedup vs baseline: 1.5253x; 1.0002x over previous
import numpy as np
import contextlib
import concourse.bass as bass
import concourse.mybir as mybir
from concourse.bass_utils import run_bass_kernel_spmd

F32 = mybir.dt.float32
BF16 = mybir.dt.bfloat16
ALU = mybir.AluOpType
AF = mybir.ActivationFunctionType
AX = mybir.AxisListType

NC = 8
D = 1024
DC = 8
I3 = 3072
IC = 24
TC = 2048
TG = 512
NTG = TC // TG
T1 = TC + 1
CM = 12582912.0
QMAX = 32767.0
INV_QMAX = float(np.float32(1.0) / np.float32(32767.0))
INV7 = float(np.float32(1.0) / np.float32(7.0))
INV_D = float(np.float32(1.0) / np.float32(1024.0))
LNEPS = 1e-5
EPS12 = 1e-12
K65536 = 65536.0
INV65536 = float(np.float32(1.0) / np.float32(65536.0))

ENGS = ("sync", "vector", "scalar", "tensor", "gpsimd")
DBG = False
STAGE = 99
SUB = 99
NJ = 99
SKIP_EVAC = False
SKIP_STASH = False
P4MODE = 0
DQN = 99
DQDN = True
DQSLOT2 = True
DQTRUNC = 99


class Tok:
    __slots__ = ("w", "r")
    def __init__(self):
        self.w = None
        self.r = []


class Plan:
    def __init__(self):
        self.cnt = {}
        self.ops = {e: [] for e in ENGS}
        self.marks = {e: {} for e in ENGS}
        self.chans = set()
        for e in ENGS:
            self.cnt[e] = 0

    def emit(self, eng, fn, reads, writes, dma, chan=None, sem_override=None):
        if dma:
            sem = chan
            self.chans.add(chan)
            if sem not in self.cnt:
                self.cnt[sem] = 0
        elif sem_override is not None:
            sem = sem_override
            if sem not in self.cnt:
                self.cnt[sem] = 0
        else:
            sem = eng
        step = 16 if dma else 1
        waits = []
        marks = self.marks[eng]
        def need(sig):
            if sig is None:
                return
            s, v = sig
            if s in self.chans:
                v = self.cnt[s]
            if marks.get(s, 0) < v:
                waits.append((s, v))
                marks[s] = v
        for t in reads:
            need(t.w)
        for t in writes:
            need(t.w)
            for sig in t.r:
                need(sig)
        self.cnt[sem] += step
        sig_new = (sem, self.cnt[sem])
        for t in reads:
            t.r.append(sig_new)
            if len(t.r) > 12:
                t.r = t.r[-12:]
        for t in writes:
            t.w = sig_new
            t.r = []
        self.ops[eng].append((waits, fn, sem, step))

    def barrier_wait(self, eng, reads):
        waits = []
        marks = self.marks[eng]
        for t in reads:
            if t.w is None:
                continue
            s, v = t.w
            if s in self.chans:
                v = self.cnt[s]
            if marks.get(s, 0) < v:
                waits.append((s, v))
                marks[s] = v
        if waits:
            self.ops[eng].append((waits, None, eng, 1))

    def drain_all(self, eng):
        waits = []
        marks = self.marks[eng]
        for s in sorted(self.chans):
            v = self.cnt[s]
            if v > 0 and marks.get(s, 0) < v:
                waits.append((s, v))
                marks[s] = v
        if "ccol" in self.cnt and marks.get("ccol", 0) < self.cnt["ccol"]:
            waits.append(("ccol", self.cnt["ccol"]))
            marks["ccol"] = self.cnt["ccol"]
        if waits:
            self.ops[eng].append((waits, None, eng, 1))

    def realize(self, block, sems):
        def mk(e):
            def run(engh):
                for waits, fn, sem, step in self.ops[e]:
                    for s, v in waits:
                        engh.wait_ge(sems[s], v)
                    if fn is not None:
                        inst = fn(engh)
                        inst.then_inc(sems[sem], step)
            return run
        for e in ENGS:
            getattr(block, e)(mk(e))


def build_program():
    nc = bass.Bass("TRN2", target_bir_lowering=False)
    dtn = nc.dram_tensor
    xt = dtn("xt", [D, TC], F32, kind="ExternalInput").ap()
    wk = dtn("wk", [DC, 128, D], F32, kind="ExternalInput").ap()
    wv = dtn("wv", [DC, 128, D], F32, kind="ExternalInput").ap()
    wr = dtn("wr", [DC, 128, D], F32, kind="ExternalInput").ap()
    wo = dtn("wo", [DC, 128, D], F32, kind="ExternalInput").ap()
    wup = dtn("wup", [I3, D], F32, kind="ExternalInput").ap()
    wg = dtn("wg", [I3, D], F32, kind="ExternalInput").ap()
    wdn = dtn("wdn", [D, I3], F32, kind="ExternalInput").ap()
    tdec = dtn("tdec", [128, DC], F32, kind="ExternalInput").ap()
    tfirst = dtn("tfirst", [128, DC], F32, kind="ExternalInput").ap()
    tmk = dtn("tmk", [128, DC], F32, kind="ExternalInput").ap()
    tmv = dtn("tmv", [128, DC], F32, kind="ExternalInput").ap()
    tmr = dtn("tmr", [128, DC], F32, kind="ExternalInput").ap()
    idn = dtn("idn", [128, 128], F32, kind="ExternalInput").ap()
    selw8 = dtn("selw8", [8, 128], F32, kind="ExternalInput").ap()
    selr8 = dtn("selr8", [128, 8], F32, kind="ExternalInput").ap()
    selw16 = dtn("selw16", [16, 128], F32, kind="ExternalInput").ap()
    selr16 = dtn("selr16", [128, 16], F32, kind="ExternalInput").ap()
    out = dtn("out", [D, TC], F32, kind="ExternalOutput").ap()
    if DBG:
        dbg1 = dtn("dbg1", [D, TC], F32, kind="ExternalOutput").ap()
        dbg2 = dtn("dbg2", [D, TC], F32, kind="ExternalOutput").ap()
        dbg3 = dtn("dbg3", [D, TC], F32, kind="ExternalOutput").ap()
        dbg4 = dtn("dbg4", [128, 512], F32, kind="ExternalOutput").ap()
        dbg5 = dtn("dbg5", [D, TC], F32, kind="ExternalOutput").ap()
    upbuf = dtn("upbuf", [I3, TC], F32).ap()
    gbuf = dtn("gbuf", [I3, TC], F32).ap()
    t2buf = dtn("t2buf", [I3, TC], F32).ap()
    wupT = dtn("wupT", [IC, 128, D], BF16).ap()
    wgT = dtn("wgT", [IC, 128, D], BF16).ap()
    wdnT = dtn("wdnT", [DC, 128, I3], BF16).ap()
    xspill = dtn("xspill", [D, TC], F32).ap()
    mixbuf = dtn("mixbuf", [3, D, TC], BF16).ap()
    scanbuf = dtn("scanbuf", [2 * D, T1], F32).ap()
    cc_in = dtn("cc_in", [128, 128], F32).ap()
    cc_out = dtn("cc_out", [128, 128], F32).ap()

    p = Plan()
    stack = contextlib.ExitStack()
    sems = {}
    for e in ENGS:
        sems[e] = stack.enter_context(nc.semaphore("s_" + e))
    sb = lambda name, shp, dtp: stack.enter_context(nc.sbuf_tensor(name, shp, dtp))
    psal = lambda name, shp, dtp: stack.enter_context(nc.psum_tensor(name, shp, dtp))

    XTB = sb("XTB", [128, DC * T1], F32)
    XT = XTB[:, 0:DC * TC]
    B2 = XTB[:, :]
    A2 = sb("A2", [128, DC * T1], F32)
    CB = sb("CB", [128, DC * TC], BF16)
    SC = CB.ap().bitcast(F32)[:, 0:3 * T1]
    MT = sb("MT", [128, 4 * TG], F32)
    US = sb("US", [128, 4 * TG], F32)
    SQT = sb("SQT", [128, TG], F32)
    WB = sb("WB", [128, 2 * D], F32)
    LW = sb("LW", [128, 4 * 1024], BF16)
    LWP = sb("LWP", [128, 2 * 1024], BF16)
    DQS = sb("DQS", [128, 256], F32)
    MAXC = sb("MAXC", [128, 240], F32)
    COLS = sb("COLS", [128, 16], F32)
    DCV = sb("DCV", [128, 64], F32)
    ONEC = sb("ONEC", [128, 2], F32)
    ONER = sb("ONER", [1, 128], F32)
    IDF = sb("IDF", [128, 128], F32)
    IDB = sb("IDB", [128, 128], BF16)
    SEL8 = sb("SEL8", [8, 128], F32)
    SELR8 = sb("SELR8", [128, 8], F32)
    SEL16 = sb("SEL16", [16, 128], F32)
    SELR16 = sb("SELR16", [128, 16], F32)
    ROWS = US[0:1, 0:4 * TG]
    SROW = sb("SROW", [1, 64], F32)
    CCT = SQT[:, 0:128]
    XR2 = SQT[0:8, 128:256]
    XRS = SQT[0:8, 256:384]
    H16 = SQT[0:16, 384:512]

    PMM = [psal("PMM%d" % i, [128, TG], F32) for i in range(4)]
    PBC = [psal("PBC%d" % i, [128, TG], F32) for i in range(2)]
    PTR = psal("PTR", [128, 1024], BF16)
    PSG = psal("PSG", [128, 128], F32)

    MX_A = 0
    MX_B = 96
    MX_F = 192
    MX_NT = 193
    MX_NR = 195
    MX_P = 197
    MX_BX = 200
    MX_SF = 208
    MX_B0 = 224

    tks = {}
    def T(name):
        if name not in tks:
            tks[name] = Tok()
        return tks[name]

    def E(eng, fn, r=(), w=()):
        p.emit(eng, fn, [T(x) for x in r], [T(x) for x in w], False)

    def ED(eng, fn, r=(), w=(), chan="c0"):
        p.emit(eng, fn, [T(x) for x in r], [T(x) for x in w], True, chan=chan)

    for dc in range(DC):
        ED("sync", (lambda dc: lambda g: g.dma_start(out=XT[:, dc * TC:(dc + 1) * TC], in_=xt[dc * 128:(dc + 1) * 128, :]))(dc),
           w=("XT%d" % dc,), chan="cin")
    ED("sync", lambda g: g.dma_start(out=IDF[:], in_=idn), w=("IDF",), chan="cin")
    ED("sync", lambda g: g.dma_start(out=SEL8[:], in_=selw8), w=("SEL8",), chan="cin")
    ED("sync", lambda g: g.dma_start(out=SELR8[:], in_=selr8), w=("SELR8",), chan="cin")
    ED("sync", lambda g: g.dma_start(out=SEL16[:], in_=selw16), w=("SEL16",), chan="cin")
    ED("sync", lambda g: g.dma_start(out=SELR16[:], in_=selr16), w=("SELR16",), chan="cin")
    for src, cb in ((tmk, 16), (tmv, 24), (tmr, 32), (tdec, 40), (tfirst, 48)):
        ED("sync", (lambda src, cb: lambda g: g.dma_start(out=DCV[:, cb:cb + DC], in_=src))(src, cb),
           r=("DCV",), w=("DCV",), chan="cin")
    E("vector", lambda v: v.memset(ONEC[:], 1.0), w=("ONEC",))
    E("vector", lambda v: v.memset(ONER[:], 1.0), w=("ONER",))
    E("vector", lambda v: v.memset(COLS[:, 15:16], -CM), r=("COLS",), w=("COLS",))
    E("vector", lambda v: v.tensor_copy(out=IDB[:], in_=IDF[:]), r=("IDF",), w=("IDB",))
    E("vector", lambda v: v.memset(CCT[:], 0.0), w=("SQT",))
    ED("gpsimd", lambda g: g.dma_start(out=cc_in, in_=CCT[:]), r=("SQT",), w=("CCIN",), chan="ccg")

    E("scalar", lambda s: s.activation(out=DCV[:, 0:8], in_=DCV[:, 48:56], func=AF.Exp),
      r=("DCV",), w=("DCV",))
    E("scalar", lambda s: s.activation(out=DCV[:, 56:64], in_=DCV[:, 40:48], func=AF.Exp),
      r=("DCV",), w=("DCV",))
    E("scalar", lambda s: s.activation(out=DCV[:, 8:16], in_=DCV[:, 56:64], func=AF.Exp, scale=-1.0),
      r=("DCV",), w=("DCV",))

    def ln_stats_and_apply(tag, maxtrack):
        for tg in range(NTG):
            for dc in range(DC):
                sl = XT[:, dc * TC + tg * TG: dc * TC + tg * TG + TG]
                E("scalar", (lambda sl: lambda s: s.activation(out=SQT[:], in_=sl, func=AF.Square))(sl),
                  r=("XT%d" % dc, "SQT"), w=("SQT",))
                E("tensor", (lambda sl, dc: lambda t: t.matmul(PBC[0][0:1, 0:TG], ONEC[:, 0:1], sl, start=(dc == 0), stop=(dc == DC - 1)))(sl, dc),
                  r=("XT%d" % dc, "ONEC", "PBC0"), w=("PBC0",))
                E("tensor", (lambda dc: lambda t: t.matmul(PBC[1][0:1, 0:TG], ONEC[:, 0:1], SQT[:], start=(dc == 0), stop=(dc == DC - 1)))(dc),
                  r=("SQT", "ONEC", "PBC1"), w=("PBC1",))
            mrow = ROWS[0:1, 0:TG]
            arow = ROWS[0:1, TG:2 * TG]
            vr1 = ROWS[0:1, 2 * TG:3 * TG]
            vr2 = ROWS[0:1, 3 * TG:4 * TG]
            E("vector", (lambda mrow: lambda v: v.tensor_scalar(out=mrow, in0=PBC[0][0:1, 0:TG], scalar1=INV_D, scalar2=None, op0=ALU.mult))(mrow),
              r=("PBC0",), w=("ROWS",))
            E("vector", (lambda vr1: lambda v: v.tensor_scalar(out=vr1, in0=PBC[1][0:1, 0:TG], scalar1=INV_D, scalar2=LNEPS, op0=ALU.mult, op1=ALU.add))(vr1),
              r=("PBC1",), w=("ROWS",))
            E("vector", (lambda mrow, vr2: lambda v: v.tensor_tensor(out=vr2, in0=mrow, in1=mrow, op=ALU.mult))(mrow, vr2),
              r=("ROWS",), w=("ROWS",))
            E("vector", (lambda vr1, vr2: lambda v: v.tensor_tensor(out=vr1, in0=vr1, in1=vr2, op=ALU.subtract))(vr1, vr2),
              r=("ROWS", "ROWS"), w=("ROWS",))
            E("scalar", (lambda vr1, vr2: lambda s: s.activation(out=vr2, in_=vr1, func=AF.Ln))(vr1, vr2),
              r=("ROWS",), w=("ROWS",))
            E("scalar", (lambda arow, vr2: lambda s: s.activation(out=arow, in_=vr2, func=AF.Exp, scale=-0.5))(arow, vr2),
              r=("ROWS",), w=("ROWS",))
            E("vector", (lambda mrow, arow: lambda v: v.scalar_tensor_tensor(out=mrow, in0=mrow, scalar=-1.0, in1=arow, op0=ALU.mult, op1=ALU.mult))(mrow, arow),
              r=("ROWS", "ROWS"), w=("ROWS",))
            E("tensor", (lambda arow: lambda t: t.matmul(PBC[0][:, 0:TG], ONER[0:1, :], arow, start=True, stop=True))(arow),
              r=("ROWS", "ONER", "PBC0"), w=("PBC0",))
            E("tensor", (lambda mrow: lambda t: t.matmul(PBC[1][:, 0:TG], ONER[0:1, :], mrow, start=True, stop=True))(mrow),
              r=("ROWS", "ONER", "PBC1"), w=("PBC1",))
            for dc in range(DC):
                srcsl = XT[:, dc * TC + tg * TG: dc * TC + tg * TG + TG]
                dstsl = A2[:, 1 + dc * T1 + tg * TG: 1 + dc * T1 + tg * TG + TG]
                E("vector", (lambda srcsl, dstsl: lambda v: v.tensor_tensor(out=dstsl, in0=srcsl, in1=PBC[0][:, 0:TG], op=ALU.mult))(srcsl, dstsl),
                  r=("XT%d" % dc, "PBC0", "A2_%d" % dc), w=("A2_%d" % dc,))
                E("vector", (lambda dstsl: lambda v: v.tensor_tensor(out=dstsl, in0=dstsl, in1=PBC[1][:, 0:TG], op=ALU.add))(dstsl),
                  r=("A2_%d" % dc, "PBC1"), w=("A2_%d" % dc,))
                if maxtrack:
                    cix = MX_A + tg * DC + dc
                    E("vector", (lambda dstsl, cix: lambda v: v.tensor_reduce(out=MAXC[:, cix:cix + 1], in_=dstsl, axis=AX.X, op=ALU.max, apply_absolute_value=True))(dstsl, cix),
                      r=("A2_%d" % dc,), w=("MXA",))

    def part_reduce_to_scalar(coltok, srccol, outslot, tagout):
        E("tensor", (lambda srccol: lambda t: t.transpose(PSG[0:1, 0:128], MAXC[:, srccol:srccol + 1], IDF[:]))(srccol),
          r=(coltok, "IDF", "PSG"), w=("PSG",))
        E("vector", (lambda outslot: lambda v: v.tensor_reduce(out=SROW[0:1, outslot:outslot + 1], in_=PSG[0:1, 0:128], axis=AX.X, op=ALU.max))(outslot),
          r=("PSG", "SROW"), w=("SROW", tagout))

    def collective_max(slot0, nsl, rtags, tagout):
        ED("gpsimd", (lambda slot0, nsl: lambda g: g.dma_start(out=cc_in[0:1, 0:nsl], in_=SROW[0:1, slot0:slot0 + nsl]))(slot0, nsl),
           r=rtags + ("SROW",), w=("CCIN",), chan="ccg")
        p.emit("gpsimd", lambda g: g.collective_compute("AllReduce", ALU.max, replica_groups=[list(range(NC))],
                                                        ins=[cc_in.opt()], outs=[cc_out.opt()]),
               [T("CCIN"), T("CCOUT")], [T("CCOUT")], False, sem_override="ccol")
        ED("sync", (lambda slot0, nsl: lambda g: g.dma_start(out=SROW[0:1, slot0:slot0 + nsl], in_=cc_out[0:1, 0:nsl]))(slot0, nsl),
           r=("CCOUT", "SROW"), w=("SROW", tagout), chan="cci")

    def collective_add_tile(rtag, tagout):
        ED("gpsimd", lambda g: g.dma_start(out=cc_in, in_=CCT[:]),
           r=(rtag, "SQT"), w=("CCIN",), chan="ccg")
        p.emit("gpsimd", lambda g: g.collective_compute("AllReduce", ALU.add, replica_groups=[list(range(NC))],
                                                        ins=[cc_in.opt()], outs=[cc_out.opt()]),
               [T("CCIN"), T("CCOUT")], [T("CCOUT")], False, sem_override="ccol")
        ED("sync", lambda g: g.dma_start(out=CCT[:], in_=cc_out),
           r=("CCOUT", "SQT"), w=("SQT", tagout), chan="cci")

    def bcast_slots(slot0, n, colbase, rtags):
        E("tensor", (lambda slot0, n: lambda t: t.matmul(PSG[:, 0:n], ONER[0:1, :], SROW[0:1, slot0:slot0 + n], start=True, stop=True))(slot0, n),
          r=rtags + ("SROW", "ONER", "PSG"), w=("PSG",))
        E("vector", (lambda colbase, n: lambda v: v.tensor_copy(out=COLS[:, colbase:colbase + n], in_=PSG[:, 0:n]))(colbase, n),
          r=("PSG", "COLS"), w=("COLS",))

    def sc_op(fn, r, w):
        E("vector", fn, r=r + ("SROW",), w=w + ("SROW",))

    def dq_core(sl):
        wsl = WB[:, sl * D:(sl + 1) * D]
        wtok = "WB%d" % sl
        bm = DQS[:, sl * 128:sl * 128 + 32]
        sbv = DQS[:, sl * 128 + 64:sl * 128 + 96]
        dtok = "DQS%d" % sl
        ptok = "LWP%d" % sl
        E("vector", (lambda wsl, bm: lambda v: v.tensor_reduce(out=bm, in_=wsl.rearrange("p (b c) -> p b c", c=32), axis=AX.X, op=ALU.max, apply_absolute_value=True))(wsl, bm),
          r=(wtok, dtok), w=(dtok,))
        E("vector", (lambda bm, sbv: lambda v: v.tensor_scalar(out=sbv, in0=bm, scalar1=INV7, scalar2=EPS12, op0=ALU.mult, op1=ALU.max))(bm, sbv),
          r=(dtok,), w=(dtok,))
        E("vector", (lambda bm, sbv: lambda v: v.reciprocal(out=bm, in_=sbv))(bm, sbv),
          r=(dtok,), w=(dtok,))
        E("vector", (lambda wsl, bm: lambda v: v.tensor_tensor(
            out=wsl.rearrange("p (b c) -> p b c", c=32), in0=wsl.rearrange("p (b c) -> p b c", c=32),
            in1=bm.unsqueeze(2).broadcast_to([128, 32, 32]), op=ALU.mult))(wsl, bm),
          r=(dtok, wtok), w=(wtok,))
        E("vector", (lambda wsl: lambda v: v.tensor_scalar(out=wsl, in0=wsl, scalar1=CM, scalar2=CM, op0=ALU.add, op1=ALU.subtract))(wsl),
          r=(wtok,), w=(wtok,))
        E("vector", (lambda wsl, sbv, sl: lambda v: v.tensor_tensor(
            out=LWP[:, sl * 1024:(sl + 1) * 1024].rearrange("p (b c) -> p b c", c=32),
            in0=wsl.rearrange("p (b c) -> p b c", c=32),
            in1=sbv.unsqueeze(2).broadcast_to([128, 32, 32]), op=ALU.mult))(wsl, sbv, sl),
          r=(wtok, dtok, ptok), w=(ptok,))
        for ic in range(DC):
            E("tensor", (lambda sl, ic: lambda t: t.transpose(PTR[:, ic * 128:(ic + 1) * 128], LWP[:, sl * 1024 + ic * 128:sl * 1024 + (ic + 1) * 128], IDB[:]))(sl, ic),
              r=(ptok, "IDB", "PTR"), w=("PTR",))

    def dq_chunk(oc, sl, wdst, stash_tag_pfx, lw_slot):
        dq_core(sl)
        ltok = "LW%d" % lw_slot
        E("scalar", (lambda lw_slot: lambda s: s.copy(out=LW[:, lw_slot * 1024:(lw_slot + 1) * 1024], in_=PTR[:, 0:1024]))(lw_slot),
          r=("PTR", ltok), w=(ltok,))
        if SKIP_STASH:
            return
        ED("gpsimd", (lambda wdst, oc, lw_slot: lambda g: g.dma_start(out=wdst[oc], in_=LW[:, lw_slot * 1024:(lw_slot + 1) * 1024]))(wdst, oc, lw_slot),
           r=(ltok,), w=("%sT%d" % (stash_tag_pfx, oc),), chan="cst")

    def dq_dn_seg(s):
        oc, seg = s // 3, s % 3
        sl = s % 2
        ED("sync", (lambda oc, seg, sl: lambda g: g.dma_start(out=WB[:, sl * D:(sl + 1) * D], in_=wdn[oc * 128:(oc + 1) * 128, seg * 1024:(seg + 1) * 1024]))(oc, seg, sl),
           r=("WB%d" % sl,), w=("WB%d" % sl,), chan="cwb%d" % sl)
        dq_core(sl)
        ptok = "LWP%d" % sl
        E("scalar", (lambda sl: lambda s_: s_.copy(out=LWP[:, sl * 1024:(sl + 1) * 1024], in_=PTR[:, 0:1024]))(sl),
          r=("PTR", ptok), w=(ptok,))
        ED("gpsimd", (lambda oc, seg, sl: lambda g: g.dma_start(out=wdnT[oc][:, seg * 1024:(seg + 1) * 1024], in_=LWP[:, sl * 1024:(sl + 1) * 1024]))(oc, seg, sl),
           r=(ptok,), w=("wdT%d" % oc,), chan="cst")

    def ffn_phase(phase):
        pf = "F%d" % phase
        ln_stats_and_apply(pf, True)
        if phase == 1 and SUB <= 0:
            return
        E("vector", lambda v: v.tensor_reduce(out=MAXC[:, MX_F:MX_F + 1], in_=MAXC[:, MX_A:MX_A + 32], axis=AX.X, op=ALU.max),
          r=("MXA",), w=("MXF",))
        part_reduce_to_scalar("MXF", MX_F, 0, pf + "gx")
        collective_max(0, 1, (pf + "gx",), pf + "gxg")
        sc_op(lambda v: v.tensor_scalar(out=SROW[0:1, 1:2], in0=SROW[0:1, 0:1], scalar1=INV_QMAX, scalar2=EPS12, op0=ALU.mult, op1=ALU.max),
              r=(pf + "gxg",), w=(pf + "sx",))
        sc_op(lambda v: v.reciprocal(out=SROW[0:1, 2:3], in_=SROW[0:1, 1:2]),
              r=(pf + "sx",), w=(pf + "rx",))
        bcast_slots(1, 2, 0, (pf + "sx", pf + "rx"))
        if phase == 1 and SUB <= 1:
            return
        for dc in range(DC):
            for tg in range(NTG):
                asl = A2[:, 1 + dc * T1 + tg * TG: 1 + dc * T1 + tg * TG + TG]
                csl = CB[:, dc * TC + tg * TG: dc * TC + tg * TG + TG]
                E("vector", (lambda asl: lambda v: v.tensor_scalar(out=asl, in0=asl, scalar1=COLS[:, 1:2], scalar2=CM, op0=ALU.mult, op1=ALU.add))(asl),
                  r=("A2_%d" % dc, "COLS"), w=("A2_%d" % dc,))
                E("vector", (lambda asl, csl: lambda v: v.tensor_scalar(out=csl, in0=asl, scalar1=CM, scalar2=COLS[:, 0:1], op0=ALU.subtract, op1=ALU.mult))(asl, csl),
                  r=("A2_%d" % dc, "COLS", "CB%d" % dc), w=("CB%d" % dc,))
        if phase == 1 and SUB <= 2:
            return
        for j in range(IC):
            if phase == 1 and j >= NJ:
                continue
            for widx, (stash_pfx, wsrc, wdst, dbuf) in enumerate((("wu", wup, wupT, upbuf), ("wgx", wg, wgT, gbuf))):
                mxb = MX_A if widx == 0 else MX_B
                mxtok = "MXA" if widx == 0 else "MXB"
                lw_slot = widx * 2 + (j % 2)
                ltok = "LW%d" % lw_slot
                if phase == 1:
                    sl = (2 * j + widx) % 2
                    ED("sync", (lambda wsrc, j, sl: lambda g: g.dma_start(out=WB[:, sl * D:(sl + 1) * D], in_=wsrc[j * 128:(j + 1) * 128, :]))(wsrc, j, sl),
                       r=("WB%d" % sl,), w=("WB%d" % sl,), chan="cwb%d" % sl)
                    dq_chunk(j, sl, wdst, stash_pfx, lw_slot)
                else:
                    ED("sync", (lambda wdst, j, lw_slot: lambda g: g.dma_start(out=LW[:, lw_slot * 1024:(lw_slot + 1) * 1024], in_=wdst[j]))(wdst, j, lw_slot),
                       r=(ltok, "%sT%d" % (stash_pfx, j)), w=(ltok,), chan="clw%d" % (lw_slot % 2))
                for tg in range(NTG):
                    bi = tg % 2 + (0 if widx == 0 else 2)
                    bank = PMM[bi]
                    def mmfn(lw_slot=lw_slot, tg=tg, bank=bank):
                        def f(t):
                            last = None
                            for ic in range(DC):
                                last = t.matmul(bank[:, 0:TG], LW[:, lw_slot * 1024 + ic * 128: lw_slot * 1024 + (ic + 1) * 128],
                                                CB[:, ic * TC + tg * TG: ic * TC + tg * TG + TG],
                                                start=(ic == 0), stop=(ic == DC - 1))
                            return last
                        return f
                    E("tensor", mmfn(), r=tuple([ltok] + ["CB%d" % ic for ic in range(DC)] + ["PMM%d" % bi]),
                      w=("PMM%d" % bi,))
                    ust = bi
                    E("scalar", (lambda bank, ust: lambda s: s.copy(out=US[:, ust * TG:(ust + 1) * TG], in_=bank[:, 0:TG]))(bank, ust),
                      r=("PMM%d" % bi, "US%d" % ust), w=("US%d" % ust,))
                    cix = mxb + j * NTG + tg
                    E("vector", (lambda ust, cix: lambda v: v.tensor_reduce(out=MAXC[:, cix:cix + 1], in_=US[:, ust * TG:(ust + 1) * TG], axis=AX.X, op=ALU.max, apply_absolute_value=True))(ust, cix),
                      r=("US%d" % ust,), w=(mxtok,))
                    ED("gpsimd", (lambda dbuf, j, tg, ust: lambda g: g.dma_start(
                        out=dbuf[j * 128:(j + 1) * 128, tg * TG:(tg + 1) * TG], in_=US[:, ust * TG:(ust + 1) * TG]))(dbuf, j, tg, ust),
                       r=("US%d" % ust,), w=(pf + "UB%d_%d_%d" % (widx, j, tg),), chan="cus%d" % ust)
        for widx in range(2):
            mxb = MX_A if widx == 0 else MX_B
            mxtok = "MXA" if widx == 0 else "MXB"
            E("vector", (lambda mxb, widx: lambda v: v.tensor_reduce(out=MAXC[:, MX_F + widx:MX_F + widx + 1], in_=MAXC[:, mxb:mxb + 96], axis=AX.X, op=ALU.max))(mxb, widx),
              r=(mxtok,), w=("MXF",))
            part_reduce_to_scalar("MXF", MX_F + widx, 8 + widx, pf + "gm%d" % widx)
        if phase == 1 and SUB <= 4:
            return
        collective_max(8, 2, (pf + "gm0", pf + "gm1"), pf + "gmg")
        sc_op(lambda v: v.tensor_scalar(out=SROW[0:1, 10:12], in0=SROW[0:1, 8:10], scalar1=INV_QMAX, scalar2=EPS12, op0=ALU.mult, op1=ALU.max),
              r=(pf + "gmg",), w=(pf + "sug",))
        sc_op(lambda v: v.reciprocal(out=SROW[0:1, 12:14], in_=SROW[0:1, 10:12]),
              r=(pf + "sug",), w=(pf + "rug",))
        sc_op(lambda v: v.tensor_scalar(out=SROW[0:1, 14:15], in0=SROW[0:1, 11:12], scalar1=INV65536, scalar2=None, op0=ALU.mult),
              r=(pf + "sug",), w=(pf + "kap",))
        bcast_slots(12, 2, 2, (pf + "rug",))
        for oc in range(IC):
            for tg in range(NTG):
                t = oc * NTG + tg
                if phase == 1 and t % 4 == 0 and t // 4 < 3 * DC:
                    dq_dn_seg(t // 4)
                i = t % 2
                gt = MT[:, i * TG:(i + 1) * TG]
                sg = MT[:, (2 + i) * TG:(3 + i) * TG]
                ED("sync", (lambda oc, tg, gt: lambda g: g.dma_start(out=gt, in_=gbuf[oc * 128:(oc + 1) * 128, tg * TG:(tg + 1) * TG]))(oc, tg, gt),
                   r=(pf + "UB1_%d_%d" % (oc, tg), "MT%d" % i), w=("MT%d" % i,), chan="cmg%d" % i)
                E("scalar", (lambda gt, sg: lambda s: s.activation(out=sg, in_=gt, func=AF.Sigmoid))(gt, sg),
                  r=("MT%d" % i, "MT%d" % (2 + i)), w=("MT%d" % (2 + i),))
                E("vector", (lambda sg: lambda v: v.tensor_scalar(out=sg, in0=sg, scalar1=K65536, scalar2=CM, op0=ALU.mult, op1=ALU.add))(sg),
                  r=("MT%d" % (2 + i),), w=("MT%d" % (2 + i),))
                E("scalar", (lambda sg: lambda s: s.activation(out=sg, in_=sg, func=AF.Identity, bias=COLS[:, 15:16]))(sg),
                  r=("MT%d" % (2 + i), "COLS"), w=("MT%d" % (2 + i),))
                E("vector", (lambda gt: lambda v: v.tensor_scalar(out=gt, in0=gt, scalar1=COLS[:, 3:4], scalar2=CM, op0=ALU.mult, op1=ALU.add))(gt),
                  r=("MT%d" % i, "COLS"), w=("MT%d" % i,))
                E("vector", (lambda gt, sg: lambda v: v.scalar_tensor_tensor(out=gt, in0=gt, scalar=-CM, in1=sg, op0=ALU.add, op1=ALU.mult))(gt, sg),
                  r=("MT%d" % i, "MT%d" % (2 + i)), w=("MT%d" % i,))
                cix = MX_A + oc * NTG + tg
                E("vector", (lambda gt, cix: lambda g: g.tensor_reduce(out=MAXC[:, cix:cix + 1], in_=gt, axis=AX.X, op=ALU.max, apply_absolute_value=True))(gt, cix),
                  r=("MT%d" % i,), w=("MXA",))
                ED("gpsimd", (lambda gt, oc, tg: lambda g: g.dma_start(out=t2buf[oc * 128:(oc + 1) * 128, tg * TG:(tg + 1) * TG], in_=gt))(gt, oc, tg),
                   r=("MT%d" % i,), w=(pf + "T2%d_%d" % (oc, tg),), chan="cms%d" % i)
        E("vector", lambda v: v.tensor_reduce(out=MAXC[:, MX_F:MX_F + 1], in_=MAXC[:, MX_A:MX_A + 96], axis=AX.X, op=ALU.max),
          r=("MXA",), w=("MXF",))
        part_reduce_to_scalar("MXF", MX_F, 16, pf + "g2")
        collective_max(16, 1, (pf + "g2",), pf + "g2g")
        sc_op(lambda v: v.tensor_tensor(out=SROW[0:1, 17:18], in0=SROW[0:1, 16:17], in1=SROW[0:1, 14:15], op=ALU.mult),
              r=(pf + "g2g", pf + "kap"), w=())
        sc_op(lambda v: v.tensor_scalar(out=SROW[0:1, 18:19], in0=SROW[0:1, 17:18], scalar1=INV_QMAX, scalar2=EPS12, op0=ALU.mult, op1=ALU.max),
              r=(), w=(pf + "s2",))
        sc_op(lambda v: v.reciprocal(out=SROW[0:1, 19:20], in_=SROW[0:1, 18:19]),
              r=(pf + "s2",), w=())
        sc_op(lambda v: v.tensor_tensor(out=SROW[0:1, 20:21], in0=SROW[0:1, 19:20], in1=SROW[0:1, 14:15], op=ALU.mult),
              r=(pf + "kap",), w=(pf + "r2",))
        sc_op(lambda v: v.tensor_tensor(out=SROW[0:1, 21:22], in0=SROW[0:1, 18:19], in1=SROW[0:1, 10:11], op=ALU.mult),
              r=(pf + "s2", pf + "sug"), w=(pf + "lam",))
        bcast_slots(20, 1, 4, (pf + "r2",))
        if phase == 1 and SUB <= 5:
            return
        for oc in range(IC):
            for tg in range(NTG):
                i = (oc * NTG + tg) % 2
                t2t = MT[:, i * TG:(i + 1) * TG]
                ut = MT[:, (2 + i) * TG:(3 + i) * TG]
                ED("sync", (lambda oc, tg, t2t: lambda g: g.dma_start(out=t2t, in_=t2buf[oc * 128:(oc + 1) * 128, tg * TG:(tg + 1) * TG]))(oc, tg, t2t),
                   r=(pf + "T2%d_%d" % (oc, tg), "MT%d" % i), w=("MT%d" % i,), chan="cmg%d" % i)
                ED("sync", (lambda oc, tg, ut: lambda g: g.dma_start(out=ut, in_=upbuf[oc * 128:(oc + 1) * 128, tg * TG:(tg + 1) * TG]))(oc, tg, ut),
                   r=(pf + "UB0_%d_%d" % (oc, tg), "MT%d" % (2 + i)), w=("MT%d" % (2 + i),), chan="cmu%d" % i)
                E("vector", (lambda t2t: lambda v: v.tensor_scalar(out=t2t, in0=t2t, scalar1=COLS[:, 4:5], scalar2=CM, op0=ALU.mult, op1=ALU.add))(t2t),
                  r=("MT%d" % i, "COLS"), w=("MT%d" % i,))
                E("vector", (lambda ut: lambda v: v.tensor_scalar(out=ut, in0=ut, scalar1=COLS[:, 2:3], scalar2=CM, op0=ALU.mult, op1=ALU.add))(ut),
                  r=("MT%d" % (2 + i), "COLS"), w=("MT%d" % (2 + i),))
                E("scalar", (lambda ut: lambda s: s.activation(out=ut, in_=ut, func=AF.Identity, bias=COLS[:, 15:16]))(ut),
                  r=("MT%d" % (2 + i), "COLS"), w=("MT%d" % (2 + i),))
                E("vector", (lambda t2t, ut: lambda v: v.scalar_tensor_tensor(out=t2t, in0=t2t, scalar=-CM, in1=ut, op0=ALU.add, op1=ALU.mult))(t2t, ut),
                  r=("MT%d" % i, "MT%d" % (2 + i)), w=("MT%d" % i,))
                cix = MX_A + oc * NTG + tg
                E("vector", (lambda t2t, cix: lambda g: g.tensor_reduce(out=MAXC[:, cix:cix + 1], in_=t2t, axis=AX.X, op=ALU.max, apply_absolute_value=True))(t2t, cix),
                  r=("MT%d" % i,), w=("MXA",))
                ED("gpsimd", (lambda t2t, oc, tg: lambda g: g.dma_start(out=gbuf[oc * 128:(oc + 1) * 128, tg * TG:(tg + 1) * TG], in_=t2t))(t2t, oc, tg),
                   r=("MT%d" % i,), w=(pf + "T4%d_%d" % (oc, tg),), chan="cms%d" % i)
        E("vector", lambda v: v.tensor_reduce(out=MAXC[:, MX_F:MX_F + 1], in_=MAXC[:, MX_A:MX_A + 96], axis=AX.X, op=ALU.max),
          r=("MXA",), w=("MXF",))
        part_reduce_to_scalar("MXF", MX_F, 24, pf + "g4")
        collective_max(24, 1, (pf + "g4",), pf + "g4g")
        sc_op(lambda v: v.tensor_tensor(out=SROW[0:1, 25:26], in0=SROW[0:1, 24:25], in1=SROW[0:1, 21:22], op=ALU.mult),
              r=(pf + "g4g", pf + "lam"), w=())
        sc_op(lambda v: v.tensor_scalar(out=SROW[0:1, 26:27], in0=SROW[0:1, 25:26], scalar1=INV_QMAX, scalar2=EPS12, op0=ALU.mult, op1=ALU.max),
              r=(), w=(pf + "s4",))
        sc_op(lambda v: v.reciprocal(out=SROW[0:1, 27:28], in_=SROW[0:1, 26:27]),
              r=(pf + "s4",), w=())
        sc_op(lambda v: v.tensor_tensor(out=SROW[0:1, 28:29], in0=SROW[0:1, 27:28], in1=SROW[0:1, 21:22], op=ALU.mult),
              r=(pf + "lam",), w=(pf + "r4",))
        sc_op(lambda v: v.tensor_scalar(out=SROW[0:1, 29:30], in0=SROW[0:1, 26:27], scalar1=0.5, scalar2=None, op0=ALU.mult),
              r=(pf + "s4",), w=(pf + "hs4",))
        bcast_slots(28, 2, 5, (pf + "r4", pf + "hs4"))
        if phase == 1 and SUB <= 6:
            return
        for tg in range(NTG):
            for ic in range(IC):
                i = ic % 2
                t4t = MT[:, i * TG:(i + 1) * TG]
                ED("sync", (lambda ic, tg, t4t: lambda g: g.dma_start(out=t4t, in_=gbuf[ic * 128:(ic + 1) * 128, tg * TG:(tg + 1) * TG]))(ic, tg, t4t),
                   r=(pf + "T4%d_%d" % (ic, tg), "MT%d" % i), w=("MT%d" % i,), chan="cmg%d" % i)
                E("vector", (lambda t4t: lambda v: v.tensor_scalar(out=t4t, in0=t4t, scalar1=COLS[:, 5:6], scalar2=CM, op0=ALU.mult, op1=ALU.add))(t4t),
                  r=("MT%d" % i, "COLS"), w=("MT%d" % i,))
                csl = CB[:, ic * TG:(ic + 1) * TG]
                E("vector", (lambda t4t, csl: lambda v: v.tensor_scalar(out=csl, in0=t4t, scalar1=CM, scalar2=None, op0=ALU.subtract))(t4t, csl),
                  r=("MT%d" % i, "CB%d" % (ic // 4)), w=("CB%d" % (ic // 4),))
            for oc in range(DC):
                if P4MODE == 1:
                    continue
                bank = PMM[oc % 2]
                for grp in range(3):
                    sl = (oc * 3 + grp) % 4
                    ED("sync", (lambda oc, grp, sl: lambda g: g.dma_start(out=LW[:, sl * 1024:(sl + 1) * 1024], in_=wdnT[oc][:, grp * 1024:(grp + 1) * 1024]))(oc, grp, sl),
                       r=("LW%d" % sl, "wdT%d" % oc), w=("LW%d" % sl,), chan="clw%d" % (sl % 2))
                    def mmdn(oc=oc, grp=grp, sl=sl, bank=bank, tg=tg):
                        def f(t):
                            last = None
                            for j in range(8):
                                ic = grp * 8 + j
                                last = t.matmul(bank[:, 0:TG], LW[:, sl * 1024 + j * 128: sl * 1024 + (j + 1) * 128],
                                                CB[:, ic * TG:(ic + 1) * TG],
                                                start=(ic == 0), stop=(ic == IC - 1))
                            return last
                        return f
                    E("tensor", mmdn(), r=tuple(["LW%d" % sl] + ["CB%d" % icg for icg in range(grp * 2, grp * 2 + 2)] + ["PMM%d" % (oc % 2)]),
                      w=("PMM%d" % (oc % 2),))
                xsl = XT[:, oc * TC + tg * TG: oc * TC + tg * TG + TG]
                if P4MODE == 2:
                    E("vector", (lambda bank: lambda v: v.tensor_copy(out=US[:, 0:TG], in_=bank[:, 0:TG]))(bank),
                      r=("PMM%d" % (oc % 2), "US0"), w=("US0",))
                else:
                    E("vector", (lambda bank, xsl: lambda v: v.scalar_tensor_tensor(out=xsl, in0=bank[:, 0:TG], scalar=COLS[:, 6:7], in1=xsl, op0=ALU.mult, op1=ALU.add))(bank, xsl),
                      r=("PMM%d" % (oc % 2), "COLS", "XT%d" % oc), w=("XT%d" % oc,))
                if phase == 2:
                    ED("gpsimd", (lambda oc, tg, xsl: lambda g: g.dma_start(out=out[oc * 128:(oc + 1) * 128, tg * TG:(tg + 1) * TG], in_=xsl))(oc, tg, xsl),
                       r=("XT%d" % oc,), w=("OUT%d_%d" % (oc, tg),), chan="cout")

    def attn_proj(mi, wsrc, dst_tensor, act, outpfx):
        for dc in range(DC):
            ED("sync", (lambda mi, dc: lambda g: g.dma_start(out=CB[:, dc * TC:(dc + 1) * TC], in_=mixbuf[mi, dc * 128:(dc + 1) * 128, :]))(mi, dc),
               r=("MIXB%d_%d" % (mi, dc), "CB%d" % dc), w=("CB%d" % dc,), chan="cmxr")
        for oc in range(DC):
            sl = oc % 4
            wsl = oc % 2
            ED("sync", (lambda wsrc, oc, wsl: lambda g: g.dma_start(out=WB[:, wsl * D:(wsl + 1) * D], in_=wsrc[oc]))(wsrc, oc, wsl),
               r=("WB%d" % wsl,), w=("WB%d" % wsl,), chan="cwb%d" % wsl)
            E("vector", (lambda sl, wsl: lambda v: v.tensor_copy(out=LW[:, sl * 1024:(sl + 1) * 1024], in_=WB[:, wsl * D:(wsl + 1) * D]))(sl, wsl),
              r=("WB%d" % wsl, "LW%d" % sl), w=("LW%d" % sl,))
            for tg in range(NTG):
                bi = (oc * NTG + tg) % 2
                bank = PMM[bi]
                def mmp(oc=oc, tg=tg, sl=sl, bank=bank):
                    def f(t):
                        last = None
                        for ic in range(DC):
                            last = t.matmul(bank[:, 0:TG], LW[:, sl * 1024 + ic * 128:sl * 1024 + (ic + 1) * 128],
                                            CB[:, ic * TC + tg * TG:ic * TC + tg * TG + TG],
                                            start=(ic == 0), stop=(ic == DC - 1))
                        return last
                    return f
                E("tensor", mmp(), r=tuple(["LW%d" % sl] + ["CB%d" % ic for ic in range(DC)] + ["PMM%d" % bi]),
                  w=("PMM%d" % bi,))
                dsl = dst_tensor[:, 1 + oc * T1 + tg * TG: 1 + oc * T1 + tg * TG + TG]
                if dst_tensor is A2:
                    extr = ("A2_%d" % oc,)
                else:
                    extr = ("B2_%d" % oc,) + tuple("XSP%d" % q for q in range(DC))
                dtok = extr[0]
                E("scalar", (lambda bank, dsl, act: lambda s: s.activation(out=dsl, in_=bank[:, 0:TG], func=act))(bank, dsl, act),
                  r=("PMM%d" % bi,) + extr, w=(dtok, "%s%d_%d" % (outpfx, oc, tg)))

    def attention():
        for dc in range(DC):
            E("vector", (lambda dc: lambda v: v.tensor_copy(out=MAXC[:, MX_BX + dc:MX_BX + dc + 1], in_=XT[:, dc * TC + TC - 1:dc * TC + TC]))(dc),
              r=("XT%d" % dc,), w=("MXBX",))
        E("tensor", lambda t: t.transpose(PSG[0:8, 0:128], MAXC[:, MX_BX:MX_BX + 8], IDF[:]),
          r=("MXBX", "IDF", "PSG"), w=("PSG",))
        E("scalar", lambda s: s.copy(out=XR2[:, :], in_=PSG[0:8, 0:128]),
          r=("PSG", "SQT"), w=("SQT",))
        E("tensor", lambda t: t.matmul(PSG[:, 0:128], SEL8[:, :], XR2[:, :], start=True, stop=True),
          r=("SQT", "SEL8", "PSG"), w=("PSG",))
        E("vector", lambda v: v.tensor_copy(out=CCT[:], in_=PSG[:, 0:128]),
          r=("PSG", "SQT"), w=("SQT",))
        collective_add_tile("SQT", "bndx")
        E("tensor", lambda t: t.matmul(PSG[0:8, 0:128], SELR8[:, :], CCT[:], start=True, stop=True),
          r=("bndx", "SQT", "SELR8", "PSG"), w=("PSG",))
        E("scalar", lambda s: s.copy(out=XR2[:, :], in_=PSG[0:8, 0:128]),
          r=("PSG", "SQT"), w=("SQT",))
        E("scalar", lambda s: s.activation(out=XRS[:, :], in_=XR2[:, :], func=AF.Square),
          r=("SQT", "SQT"), w=("SQT",))
        E("tensor", lambda t: t.matmul(PBC[0][0:1, 0:128], ONEC[0:8, 0:1], XR2[:, :], start=True, stop=True),
          r=("SQT", "ONEC", "PBC0"), w=("PBC0",))
        E("tensor", lambda t: t.matmul(PBC[1][0:1, 0:128], ONEC[0:8, 0:1], XRS[:, :], start=True, stop=True),
          r=("SQT", "ONEC", "PBC1"), w=("PBC1",))
        sc_op(lambda v: v.tensor_reduce(out=SROW[0:1, 40:41], in_=PBC[0][0:1, 0:128], axis=AX.X, op=ALU.add),
              r=("PBC0",), w=())
        sc_op(lambda v: v.tensor_reduce(out=SROW[0:1, 41:42], in_=PBC[1][0:1, 0:128], axis=AX.X, op=ALU.add),
              r=("PBC1",), w=())
        sc_op(lambda v: v.tensor_scalar(out=SROW[0:1, 40:42], in0=SROW[0:1, 40:42], scalar1=INV_D, scalar2=None, op0=ALU.mult),
              r=(), w=())
        sc_op(lambda v: v.tensor_tensor(out=SROW[0:1, 42:43], in0=SROW[0:1, 40:41], in1=SROW[0:1, 40:41], op=ALU.mult),
              r=(), w=())
        sc_op(lambda v: v.tensor_scalar(out=SROW[0:1, 43:44], in0=SROW[0:1, 42:43], scalar1=-1.0, scalar2=LNEPS, op0=ALU.mult, op1=ALU.add),
              r=(), w=())
        sc_op(lambda v: v.tensor_tensor(out=SROW[0:1, 43:44], in0=SROW[0:1, 41:42], in1=SROW[0:1, 43:44], op=ALU.add),
              r=(), w=())
        E("scalar", lambda s: s.activation(out=SROW[0:1, 44:45], in_=SROW[0:1, 43:44], func=AF.Ln),
          r=("SROW",), w=("SROW",))
        E("scalar", lambda s: s.activation(out=SROW[0:1, 45:46], in_=SROW[0:1, 44:45], func=AF.Exp, scale=-0.5),
          r=("SROW",), w=("SROW",))
        sc_op(lambda v: v.scalar_tensor_tensor(out=SROW[0:1, 46:47], in0=SROW[0:1, 40:41], scalar=-1.0, in1=SROW[0:1, 45:46], op0=ALU.mult, op1=ALU.mult),
              r=(), w=("bln",))
        bcast_slots(45, 2, 7, ("bln",))
        E("tensor", lambda t: t.transpose(PSG[:, 0:8], XR2[:, :], IDF[0:8, 0:8]),
          r=("SQT", "IDF", "PSG"), w=("PSG",))
        E("vector", lambda v: v.tensor_copy(out=MAXC[:, MX_BX:MX_BX + 8], in_=PSG[:, 0:8]),
          r=("PSG", "MXBX"), w=("MXBX",))
        for dc in range(DC):
            E("vector", (lambda dc: lambda v: v.tensor_scalar(out=A2[:, dc * T1:dc * T1 + 1], in0=MAXC[:, MX_BX + dc:MX_BX + dc + 1], scalar1=COLS[:, 7:8], scalar2=COLS[:, 8:9], op0=ALU.mult, op1=ALU.add))(dc),
              r=("MXBX", "COLS", "A2_%d" % dc), w=("A2_%d" % dc,))
        ln_stats_and_apply("A", False)
        if DBG:
            for dc in range(DC):
                ED("gpsimd", (lambda dc: lambda g: g.dma_start(out=dbg2[dc * 128:(dc + 1) * 128, :], in_=A2[:, 1 + dc * T1:1 + dc * T1 + TC]))(dc),
                   r=("A2_%d" % dc,), w=("DBG2_%d" % dc,), chan="cdbg")
            ED("gpsimd", lambda g: g.dma_start(out=dbg4[:, 0:128], in_=SQT[:, 128:256]), r=("SQT",), w=("DBG4",), chan="cdbg")
            ED("gpsimd", lambda g: g.dma_start(out=dbg4[:, 128:136], in_=MAXC[:, MX_BX:MX_BX + 8]), r=("MXBX",), w=("DBG4",), chan="cdbg")
            ED("gpsimd", lambda g: g.dma_start(out=dbg4[:, 136:145], in_=COLS[:, 0:9]), r=("COLS",), w=("DBG4",), chan="cdbg")
            ED("gpsimd", lambda g: g.dma_start(out=dbg4[0:1, 152:159], in_=SROW[0:1, 40:47]), r=("SROW",), w=("DBG4",), chan="cdbg")
            for dc in range(DC):
                ED("gpsimd", (lambda dc: lambda g: g.dma_start(out=dbg4[:, 216 + dc:217 + dc], in_=A2[:, dc * T1:dc * T1 + 1], allow_slow_non_contiguous=True))(dc),
                   r=("A2_%d" % dc,), w=("DBG4",), chan="cdbg")
        for dc in range(DC):
            ED("gpsimd", (lambda dc: lambda g: g.dma_start(out=xspill[dc * 128:(dc + 1) * 128, :], in_=XT[:, dc * TC:(dc + 1) * TC]))(dc),
               r=("XT%d" % dc,), w=("XSP%d" % dc,), chan="csp")
        for mi, colb in ((0, 16), (1, 24), (2, 32)):
            for dc in range(DC):
                cur = A2[:, 1 + dc * T1:1 + dc * T1 + TC]
                prv = A2[:, dc * T1:dc * T1 + TC]
                scr = MT[:, 0:TC]
                E("vector", (lambda cur, prv, scr: lambda v: v.tensor_tensor(out=scr, in0=cur, in1=prv, op=ALU.subtract))(cur, prv, scr),
                  r=("A2_%d" % dc, "MT0", "MT1", "MT2", "MT3"), w=("MT0", "MT1", "MT2", "MT3"))
                E("vector", (lambda prv, scr, colb, dc: lambda v: v.scalar_tensor_tensor(out=CB[:, dc * TC:(dc + 1) * TC], in0=scr, scalar=DCV[:, colb + dc:colb + dc + 1], in1=prv, op0=ALU.mult, op1=ALU.add))(prv, scr, colb, dc),
                  r=("MT0", "MT1", "MT2", "MT3", "A2_%d" % dc, "DCV", "CB%d" % dc), w=("CB%d" % dc,))
                ED("gpsimd", (lambda mi, dc: lambda g: g.dma_start(out=mixbuf[mi, dc * 128:(dc + 1) * 128, :], in_=CB[:, dc * TC:(dc + 1) * TC]))(mi, dc),
                   r=("CB%d" % dc,), w=("MIXB%d_%d" % (mi, dc),), chan="cmx")
        attn_proj(0, wk, A2, AF.Exp, "EK")
        attn_proj(1, wv, B2, AF.Identity, "VV")
        for dc in range(DC):
            for tg in range(NTG):
                a = A2[:, 1 + dc * T1 + tg * TG:1 + dc * T1 + tg * TG + TG]
                b = B2[:, 1 + dc * T1 + tg * TG:1 + dc * T1 + tg * TG + TG]
                E("vector", (lambda a, b: lambda v: v.tensor_tensor(out=b, in0=b, in1=a, op=ALU.mult))(a, b),
                  r=("A2_%d" % dc, "B2_%d" % dc), w=("B2_%d" % dc,))
        for dc in range(DC):
            E("vector", (lambda dc: lambda v: v.memset(A2[:, dc * T1:dc * T1 + 1], 0.0))(dc),
              r=("A2_%d" % dc,), w=("A2_%d" % dc,))
            E("vector", (lambda dc: lambda v: v.memset(B2[:, dc * T1:dc * T1 + 1], 0.0))(dc),
              r=("B2_%d" % dc,), w=("B2_%d" % dc,))
        for dc in range(DC):
            ek = A2[:, dc * T1:(dc + 1) * T1]
            av = B2[:, dc * T1:(dc + 1) * T1]
            bo = SC[:, 0:T1]
            ao = SC[:, T1:2 * T1]
            emwb = DCV[:, 8 + dc:9 + dc].broadcast_to([128, T1])
            E("vector", (lambda ek, bo, emwb: lambda v: v.tensor_tensor_scan(out=bo, data0=emwb, data1=ek, initial=0.0, op0=ALU.mult, op1=ALU.add))(ek, bo, emwb),
              r=("A2_%d" % dc, "DCV", "SC0", "SC1", "SC2", "CB0", "CB1", "CB2", "CB3", "CB4", "CB5", "CB6"), w=("SC0", "SC1", "SC2", "CB0", "CB1", "CB2", "CB3", "CB4", "CB5", "CB6"))
            E("vector", (lambda av, ao, emwb: lambda v: v.tensor_tensor_scan(out=ao, data0=emwb, data1=av, initial=0.0, op0=ALU.mult, op1=ALU.add))(av, ao, emwb),
              r=("B2_%d" % dc, "DCV", "SC0", "SC1", "SC2", "CB0", "CB1", "CB2", "CB3", "CB4", "CB5", "CB6"), w=("SC0", "SC1", "SC2", "CB0", "CB1", "CB2", "CB3", "CB4", "CB5", "CB6"))
            E("vector", (lambda bo, dc: lambda v: v.tensor_copy(out=MAXC[:, MX_SF + dc:MX_SF + dc + 1], in_=bo[:, TC:TC + 1]))(bo, dc),
              r=("SC0", "SC1", "SC2", "CB0", "CB1", "CB2", "CB3", "CB4", "CB5", "CB6", "MXSF"), w=("MXSF",))
            E("vector", (lambda ao, dc: lambda v: v.tensor_copy(out=MAXC[:, MX_SF + 8 + dc:MX_SF + 8 + dc + 1], in_=ao[:, TC:TC + 1]))(ao, dc),
              r=("SC0", "SC1", "SC2", "CB0", "CB1", "CB2", "CB3", "CB4", "CB5", "CB6", "MXSF"), w=("MXSF",))
            ED("gpsimd", (lambda bo, dc: lambda g: g.dma_start(out=scanbuf[dc * 128:(dc + 1) * 128, :], in_=bo))(bo, dc),
               r=("SC0", "SC1", "SC2", "CB0", "CB1", "CB2", "CB3", "CB4", "CB5", "CB6",), w=("SCB%d" % dc,), chan="csb0")
            ED("gpsimd", (lambda ao, dc: lambda g: g.dma_start(out=scanbuf[D + dc * 128:D + (dc + 1) * 128, :], in_=ao))(ao, dc),
               r=("SC0", "SC1", "SC2", "CB0", "CB1", "CB2", "CB3", "CB4", "CB5", "CB6",), w=("SCA%d" % dc,), chan="csb1")
        E("tensor", lambda t: t.transpose(PSG[0:16, 0:128], MAXC[:, MX_SF:MX_SF + 16], IDF[:]),
          r=("MXSF", "IDF", "PSG"), w=("PSG",))
        E("scalar", lambda s: s.copy(out=H16[:, :], in_=PSG[0:16, 0:128]),
          r=("PSG", "SQT"), w=("SQT",))
        E("tensor", lambda t: t.matmul(PSG[:, 0:128], SEL16[:, :], H16[:, :], start=True, stop=True),
          r=("SQT", "SEL16", "PSG"), w=("PSG",))
        E("vector", lambda v: v.tensor_copy(out=CCT[:], in_=PSG[:, 0:128]),
          r=("PSG", "SQT"), w=("SQT",))
        collective_add_tile("SQT", "hoff")
        E("tensor", lambda t: t.matmul(PSG[0:16, 0:128], SELR16[:, :], CCT[:], start=True, stop=True),
          r=("hoff", "SQT", "SELR16", "PSG"), w=("PSG",))
        E("scalar", lambda s: s.copy(out=H16[:, :], in_=PSG[0:16, 0:128]),
          r=("PSG", "SQT"), w=("SQT",))
        E("tensor", lambda t: t.transpose(PSG[:, 0:16], H16[:, :], IDF[0:16, 0:16]),
          r=("SQT", "IDF", "PSG"), w=("PSG",))
        E("vector", lambda v: v.tensor_copy(out=MAXC[:, MX_B0:MX_B0 + 16], in_=PSG[:, 0:16]),
          r=("PSG", "MXB0"), w=("MXB0",))
        for dc in range(DC):
            ek = A2[:, dc * T1:(dc + 1) * T1]
            av = B2[:, dc * T1:(dc + 1) * T1]
            bo = SC[:, 0:T1]
            ao = SC[:, T1:2 * T1]
            dec = SC[:, 2 * T1:3 * T1]
            emwb = DCV[:, 8 + dc:9 + dc].broadcast_to([128, T1])
            ED("sync", (lambda bo, dc: lambda g: g.dma_start(out=bo, in_=scanbuf[dc * 128:(dc + 1) * 128, :]))(bo, dc),
               r=("SCB%d" % dc, "SC0", "SC1", "SC2", "CB0", "CB1", "CB2", "CB3", "CB4", "CB5", "CB6"), w=("SC0", "SC1", "SC2", "CB0", "CB1", "CB2", "CB3", "CB4", "CB5", "CB6"), chan="csr0")
            ED("sync", (lambda ao, dc: lambda g: g.dma_start(out=ao, in_=scanbuf[D + dc * 128:D + (dc + 1) * 128, :]))(ao, dc),
               r=("SCA%d" % dc, "SC0", "SC1", "SC2", "CB0", "CB1", "CB2", "CB3", "CB4", "CB5", "CB6"), w=("SC0", "SC1", "SC2", "CB0", "CB1", "CB2", "CB3", "CB4", "CB5", "CB6"), chan="csr1")
            E("vector", (lambda dec: lambda v: v.memset(dec[:, 0:T1], 0.0))(dec), r=("SC0", "SC1", "SC2", "CB0", "CB1", "CB2", "CB3", "CB4", "CB5", "CB6",), w=("SC0", "SC1", "SC2", "CB0", "CB1", "CB2", "CB3", "CB4", "CB5", "CB6"))
            E("vector", (lambda dec: lambda v: v.memset(dec[:, 0:1], 1.0))(dec), r=("SC0", "SC1", "SC2", "CB0", "CB1", "CB2", "CB3", "CB4", "CB5", "CB6",), w=("SC0", "SC1", "SC2", "CB0", "CB1", "CB2", "CB3", "CB4", "CB5", "CB6"))
            E("vector", (lambda dec, emwb: lambda v: v.tensor_tensor_scan(out=dec, data0=emwb, data1=dec, initial=0.0, op0=ALU.mult, op1=ALU.add))(dec, emwb),
              r=("SC0", "SC1", "SC2", "CB0", "CB1", "CB2", "CB3", "CB4", "CB5", "CB6", "DCV"), w=("SC0", "SC1", "SC2", "CB0", "CB1", "CB2", "CB3", "CB4", "CB5", "CB6"))
            E("vector", (lambda bo, dec, dc: lambda v: v.scalar_tensor_tensor(out=bo, in0=dec, scalar=MAXC[:, MX_B0 + dc:MX_B0 + dc + 1], in1=bo, op0=ALU.mult, op1=ALU.add))(bo, dec, dc),
              r=("SC0", "SC1", "SC2", "CB0", "CB1", "CB2", "CB3", "CB4", "CB5", "CB6", "MXB0"), w=("SC0", "SC1", "SC2", "CB0", "CB1", "CB2", "CB3", "CB4", "CB5", "CB6"))
            E("vector", (lambda ao, dec, dc: lambda v: v.scalar_tensor_tensor(out=ao, in0=dec, scalar=MAXC[:, MX_B0 + 8 + dc:MX_B0 + 8 + dc + 1], in1=ao, op0=ALU.mult, op1=ALU.add))(ao, dec, dc),
              r=("SC0", "SC1", "SC2", "CB0", "CB1", "CB2", "CB3", "CB4", "CB5", "CB6", "MXB0"), w=("SC0", "SC1", "SC2", "CB0", "CB1", "CB2", "CB3", "CB4", "CB5", "CB6"))
            E("vector", (lambda ek, bo, dc: lambda v: v.scalar_tensor_tensor(out=ek[:, 1:T1], in0=ek[:, 1:T1], scalar=DCV[:, 0 + dc:1 + dc], in1=bo[:, 0:TC], op0=ALU.mult, op1=ALU.add))(ek, bo, dc),
              r=("A2_%d" % dc, "SC0", "SC1", "SC2", "CB0", "CB1", "CB2", "CB3", "CB4", "CB5", "CB6", "DCV"), w=("A2_%d" % dc,))
            E("vector", (lambda av, ao, dc: lambda v: v.scalar_tensor_tensor(out=av[:, 1:T1], in0=av[:, 1:T1], scalar=DCV[:, 0 + dc:1 + dc], in1=ao[:, 0:TC], op0=ALU.mult, op1=ALU.add))(av, ao, dc),
              r=("B2_%d" % dc, "SC0", "SC1", "SC2", "CB0", "CB1", "CB2", "CB3", "CB4", "CB5", "CB6", "DCV"), w=("B2_%d" % dc,))
            E("scalar", (lambda ek: lambda s: s.activation(out=ek[:, 1:T1], in_=ek[:, 1:T1], func=AF.Ln))(ek),
              r=("A2_%d" % dc,), w=("A2_%d" % dc,))
            E("scalar", (lambda ek: lambda s: s.activation(out=ek[:, 1:T1], in_=ek[:, 1:T1], func=AF.Exp, scale=-1.0))(ek),
              r=("A2_%d" % dc,), w=("A2_%d" % dc,))
            E("vector", (lambda ek, av: lambda v: v.tensor_tensor(out=av[:, 1:T1], in0=av[:, 1:T1], in1=ek[:, 1:T1], op=ALU.mult))(ek, av),
              r=("A2_%d" % dc, "B2_%d" % dc), w=("B2_%d" % dc,))
        attn_proj(2, wr, A2, AF.Sigmoid, "SR")
        for dc in range(DC):
            for tg in range(NTG):
                a = A2[:, 1 + dc * T1 + tg * TG:1 + dc * T1 + tg * TG + TG]
                b = B2[:, 1 + dc * T1 + tg * TG:1 + dc * T1 + tg * TG + TG]
                c = CB[:, dc * TC + tg * TG:dc * TC + tg * TG + TG]
                E("vector", (lambda a, b, c: lambda v: v.tensor_tensor(out=c, in0=b, in1=a, op=ALU.mult))(a, b, c),
                  r=("A2_%d" % dc, "B2_%d" % dc, "CB%d" % dc), w=("CB%d" % dc,))
        for dc in range(DC):
            ED("sync", (lambda dc: lambda g: g.dma_start(out=XT[:, dc * TC:(dc + 1) * TC], in_=xspill[dc * 128:(dc + 1) * 128, :]))(dc),
               r=("XSP%d" % dc, "XT%d" % dc), w=("XT%d" % dc,) + tuple("B2_%d" % q for q in range(DC)), chan="cxr")
        for oc in range(DC):
            sl = oc % 4
            wsl = oc % 2
            ED("sync", (lambda oc, wsl: lambda g: g.dma_start(out=WB[:, wsl * D:(wsl + 1) * D], in_=wo[oc]))(oc, wsl),
               r=("WB%d" % wsl,), w=("WB%d" % wsl,), chan="cwb%d" % wsl)
            E("vector", (lambda sl, wsl: lambda v: v.tensor_copy(out=LW[:, sl * 1024:(sl + 1) * 1024], in_=WB[:, wsl * D:(wsl + 1) * D]))(sl, wsl),
              r=("WB%d" % wsl, "LW%d" % sl), w=("LW%d" % sl,))
            for tg in range(NTG):
                bi = (oc * NTG + tg) % 2
                bank = PMM[bi]
                def mmo(oc=oc, tg=tg, sl=sl, bank=bank):
                    def f(t):
                        last = None
                        for ic in range(DC):
                            last = t.matmul(bank[:, 0:TG], LW[:, sl * 1024 + ic * 128:sl * 1024 + (ic + 1) * 128],
                                            CB[:, ic * TC + tg * TG:ic * TC + tg * TG + TG],
                                            start=(ic == 0), stop=(ic == DC - 1))
                        return last
                    return f
                E("tensor", mmo(), r=tuple(["LW%d" % sl] + ["CB%d" % ic for ic in range(DC)] + ["PMM%d" % bi]),
                  w=("PMM%d" % bi,))
                xsl = XT[:, oc * TC + tg * TG:oc * TC + tg * TG + TG]
                E("vector", (lambda bank, xsl: lambda v: v.tensor_tensor(out=xsl, in0=xsl, in1=bank[:, 0:TG], op=ALU.add))(bank, xsl),
                  r=("PMM%d" % bi, "XT%d" % oc), w=("XT%d" % oc,))

    def finish_early():
        for oc in range(DC):
            for tg in range(NTG):
                xsl = XT[:, oc * TC + tg * TG: oc * TC + tg * TG + TG]
                ED("gpsimd", (lambda oc, tg, xsl: lambda g: g.dma_start(out=out[oc * 128:(oc + 1) * 128, tg * TG:(tg + 1) * TG], in_=xsl))(oc, tg, xsl),
                   r=("XT%d" % oc,), w=("OUT%d_%d" % (oc, tg),), chan="cout")
        p.barrier_wait("gpsimd", [T("OUT%d_%d" % (oc, tg)) for oc in range(DC) for tg in range(NTG)])
        p.drain_all("gpsimd")
        for ch in sorted(p.chans):
            sems[ch] = stack.enter_context(nc.semaphore("sd_" + ch))
        if "ccol" in p.cnt:
            sems["ccol"] = stack.enter_context(nc.semaphore("s_ccol"))
        with nc.Block() as block:
            p.realize(block, sems)
        stack.close()
        return nc

    if STAGE <= 0:
        return finish_early()
    ffn_phase(1)
    if DBG:
        pass
    if DBG:
        for dc in range(DC):
            ED("gpsimd", (lambda dc: lambda g: g.dma_start(out=dbg1[dc * 128:(dc + 1) * 128, :], in_=XT[:, dc * TC:(dc + 1) * TC]))(dc),
               r=("XT%d" % dc,), w=("DBG1_%d" % dc,), chan="cdbg")
    if STAGE <= 1:
        return finish_early()
    attention()
    if DBG:
        for dc in range(DC):
            ED("gpsimd", (lambda dc: lambda g: g.dma_start(out=dbg5[dc * 128:(dc + 1) * 128, :], in_=XT[:, dc * TC:(dc + 1) * TC]))(dc),
               r=("XT%d" % dc,), w=("DBG5_%d" % dc,), chan="cdbg")
    if STAGE <= 2:
        return finish_early()
    ffn_phase(2)
    p.barrier_wait("gpsimd", [T("OUT%d_%d" % (oc, tg)) for oc in range(DC) for tg in range(NTG)])
    p.drain_all("gpsimd")

    for ch in sorted(p.chans):
        sems[ch] = stack.enter_context(nc.semaphore("sd_" + ch))
    if "ccol" in p.cnt:
        sems["ccol"] = stack.enter_context(nc.semaphore("s_ccol"))
    with nc.Block() as block:
        p.realize(block, sems)
    stack.close()
    return nc


def kernel(L0_attn_x, L0_attn_time_decay, L0_attn_time_first, L0_attn_time_mix_k,
           L0_attn_time_mix_v, L0_attn_time_mix_r, L0_attn_W_key, L0_attn_W_value,
           L0_attn_W_receptance, L0_attn_W_output, L0_ffn_w_gate, L0_ffn_w_up, L0_ffn_w_down):
    x = np.asarray(L0_attn_x, np.float32)
    B, Ts, _ = x.shape
    xt_full = np.ascontiguousarray(x.reshape(B * Ts, D).T)

    def permW(w):
        wT = np.asarray(w, np.float32).T
        return np.ascontiguousarray(wT.reshape(8, 128, 8, 128).transpose(2, 1, 0, 3).reshape(8, 128, 1024))

    def permV(v):
        return np.ascontiguousarray(np.asarray(v, np.float32).reshape(8, 128).T)

    wkT = permW(L0_attn_W_key)
    wvT = permW(L0_attn_W_value)
    wrT = permW(L0_attn_W_receptance)
    woT = permW(L0_attn_W_output)
    wupN = np.ascontiguousarray(np.asarray(L0_ffn_w_up, np.float32))
    wgN = np.ascontiguousarray(np.asarray(L0_ffn_w_gate, np.float32))
    wdnN = np.ascontiguousarray(np.asarray(L0_ffn_w_down, np.float32))
    tdec = permV(L0_attn_time_decay)
    tfirst = permV(L0_attn_time_first)
    tmk = permV(L0_attn_time_mix_k)
    tmv = permV(L0_attn_time_mix_v)
    tmr = permV(L0_attn_time_mix_r)
    idn = np.eye(128, dtype=np.float32)

    in_maps = []
    for c in range(NC):
        selw8 = np.zeros((8, 128), np.float32)
        for q in range(8):
            selw8[q, 8 * c + q] = 1.0
        selr8 = np.zeros((128, 8), np.float32)
        selw16 = np.zeros((16, 128), np.float32)
        for q in range(16):
            selw16[q, 16 * c + q] = 1.0
        selr16 = np.zeros((128, 16), np.float32)
        if c % 2 == 1:
            for q in range(8):
                selr8[8 * (c - 1) + q, q] = 1.0
            for q in range(16):
                selr16[16 * (c - 1) + q, q] = 1.0
        in_maps.append({
            "xt": np.ascontiguousarray(xt_full[:, c * TC:(c + 1) * TC]),
            "wk": wkT, "wv": wvT, "wr": wrT, "wo": woT,
            "wup": wupN, "wg": wgN, "wdn": wdnN,
            "tdec": tdec, "tfirst": tfirst, "tmk": tmk, "tmv": tmv, "tmr": tmr,
            "idn": idn, "selw8": selw8, "selr8": selr8,
            "selw16": selw16, "selr16": selr16,
        })
    nc = build_program()
    res = run_bass_kernel_spmd(nc, in_maps, core_ids=list(range(NC)))
    outs = [res.results[c]["out"] for c in range(NC)]
    full = np.concatenate(outs, axis=1)
    return np.ascontiguousarray(full.T).reshape(B, Ts, D).astype(np.float32)
